# version 03
# baseline (speedup 1.0000x reference)
import numpy as np
import ml_dtypes
from contextlib import ExitStack
import concourse.bass as bass
import concourse.mybir as mybir
from concourse.bass_utils import run_bass_kernel_spmd

F32 = mybir.dt.float32
BF16 = mybir.dt.bfloat16
AF = mybir.ActivationFunctionType
ALU = mybir.AluOpType
AX = mybir.AxisListType

NCORES = 8
D = 1024
B = 2
S = 8192
T_ALL = B * S
T_CORE = T_ALL // NCORES

BF = ml_dtypes.bfloat16


class Plan:
    ENGINES = ["sync", "tensor", "scalar", "vector", "gpsimd"]

    def __init__(self):
        self.ops = {e: [] for e in self.ENGINES}
        self.cnt = {}

    def add(self, eng, fn, waits=(), inc=None):
        self.ops[eng].append([fn, list(waits), inc])

    def bump(self, sem, amt=1):
        v = self.cnt.get(sem, 0) + amt
        self.cnt[sem] = v
        return v

    def val(self, sem):
        return self.cnt.get(sem, 0)

    CHAIN = {"vector": "vch", "scalar": "sch", "gpsimd": "gch", "sync": "dq"}

    def run_all(self, block, sems):
        def runner(eng_name):
            chain = self.CHAIN.get(eng_name)
            def go(e):
                cnt2 = {}
                last = None
                for fn, waits, inc in self.ops[eng_name]:
                    for s, v in waits:
                        e.wait_ge(sems[s], v)
                    if fn is None:
                        if inc is not None and inc[0] is not None:
                            cnt2[inc[0]] = cnt2.get(inc[0], 0) + inc[1]
                            last = (inc[0], cnt2[inc[0]])
                        continue
                    if chain and last is not None:
                        e.wait_ge(sems[last[0]], last[1])
                    ins = fn(e)
                    up = inc if inc is not None else (chain, 1)
                    if up[0] is not None:
                        ins.then_inc(sems[up[0]], up[1])
                        cnt2[up[0]] = cnt2.get(up[0], 0) + up[1]
                        last = (up[0], cnt2[up[0]])
                    n = 0
            return go
        block.sync(runner("sync"))
        block.tensor(runner("tensor"))
        block.scalar(runner("scalar"))
        block.vector(runner("vector"))
        block.gpsimd(runner("gpsimd"))


def _chunk_w(w, pin=128):
    In, Out = w.shape
    nic, noc = In // pin, Out // pin
    return np.ascontiguousarray(w.reshape(nic, pin, noc, pin).transpose(2, 1, 0, 3))


def _feat_major(x2d):
    T, Din = x2d.shape
    nic = Din // 128
    return np.ascontiguousarray(x2d.T.reshape(nic, 128, T).transpose(1, 0, 2))


def _unfeat_major(xfm):
    p, nic, T = xfm.shape
    return np.ascontiguousarray(xfm.transpose(1, 0, 2).reshape(nic * 128, T).T)


def emit_ln_stats(P, src, nic, TP, ones_col, xsq, mrow, rrow, srow,
                  ps_sum, ps_sq, eps_col, extra_wait, CS=None, off=0):
    NT5 = TP // 512
    if CS is None:
        CS = TP
    for s5 in range(NT5):
        for ic in range(nic):
            P.add("tensor",
                  (lambda s5=s5, ic=ic: (lambda e: e.matmul(
                      out=ps_sum[:, s5 * 512:(s5 + 1) * 512], lhsT=ones_col[:, 0:1],
                      rhs=src[:, ic * CS + off + s5 * 512: ic * CS + off + (s5 + 1) * 512],
                      start=(ic == 0), stop=(ic == nic - 1))))(),
                  waits=(list(extra_wait) if (s5 == 0 and ic == 0) else []),
                  inc=(("pe", 1) if (s5 == NT5 - 1 and ic == nic - 1) else None))
    P.bump("pe")
    pb0 = P.val("pe")
    sd0 = P.val("sd")
    for ic in range(nic):
        w = list(extra_wait) if ic == 0 else []
        if ic >= 2:
            w.append(("pe", pb0 + ic - 1))
        P.add("scalar",
              (lambda ic=ic: (lambda e: e.activation(
                  out=xsq[ic % 2][:, 0:TP], in_=src[:, ic * CS + off:ic * CS + off + TP],
                  func=AF.Square)))(),
              waits=w, inc=("sd", 1))
        P.bump("sd")
        for s5 in range(NT5):
            P.add("tensor",
                  (lambda s5=s5, ic=ic: (lambda e: e.matmul(
                      out=ps_sq[:, s5 * 512:(s5 + 1) * 512], lhsT=ones_col[:, 0:1],
                      rhs=xsq[ic % 2][:, s5 * 512:(s5 + 1) * 512],
                      start=(ic == 0), stop=(ic == nic - 1))))(),
                  waits=([("sd", sd0 + ic + 1)] if s5 == 0 else []),
                  inc=(("pe", 1) if s5 == NT5 - 1 else None))
        P.bump("pe")
    pe_done = P.val("pe")
    P.add("vector", lambda e: e.tensor_scalar(
        out=mrow[:, 0:TP], in0=ps_sum[:, 0:TP], scalar1=1.0 / D, scalar2=None,
        op0=ALU.mult), waits=[("pe", pe_done)])
    P.add("vector", lambda e: e.tensor_scalar(
        out=srow[:, 0:TP], in0=ps_sq[:, 0:TP], scalar1=1.0 / D, scalar2=None,
        op0=ALU.mult))
    P.add("vector", lambda e: e.tensor_tensor(
        out=rrow[:, 0:TP], in0=mrow[:, 0:TP], in1=mrow[:, 0:TP], op=ALU.mult))
    P.add("vector", lambda e: e.tensor_tensor(
        out=srow[:, 0:TP], in0=srow[:, 0:TP], in1=rrow[:, 0:TP], op=ALU.subtract),
        inc=("vd", 1))
    P.bump("vd")
    P.add("scalar", lambda e: e.activation(
        out=srow[:, 0:TP], in_=srow[:, 0:TP], func=AF.Sqrt, bias=eps_col[0:1, 0:1]),
        waits=[("vd", P.val("vd"))], inc=("sd", 1))
    P.bump("sd")
    P.add("vector", lambda e: e.reciprocal(out=rrow[:, 0:TP], in_=srow[:, 0:TP]),
        waits=[("sd", P.val("sd"))], inc=("vd", 1))
    P.bump("vd")


def emit_bcast_row(P, row, ps_dst, TP, ones_row, wait):
    NT5 = TP // 512
    for s5 in range(NT5):
        P.add("tensor",
              (lambda s5=s5: (lambda e: e.matmul(
                  out=ps_dst[:, s5 * 512:(s5 + 1) * 512], lhsT=ones_row[0:1, :],
                  rhs=row[:, s5 * 512:(s5 + 1) * 512], start=True, stop=True)))(),
              waits=(list(wait) if s5 == 0 else []),
              inc=(("pe", 1) if s5 == NT5 - 1 else None))
    P.bump("pe")



def build_l0ffn(TP=1024):
    nc = bass.Bass("TRN2", target_bir_lowering=False, debug=False)
    NPASS = T_CORE // TP
    xin = nc.dram_tensor("xin", [128, 8, T_CORE], F32, kind="ExternalInput").ap()
    xout = nc.dram_tensor("xout", [128, 8, T_CORE], F32, kind="ExternalOutput").ap()
    wdr = {}
    for nm, shp in [("w1", [32, 128, 8, 128]), ("w2", [32, 128, 8, 128]),
                    ("h1", [32, 128, 32, 128]), ("h2", [32, 128, 32, 128]),
                    ("w3", [8, 128, 32, 128])]:
        wdr[nm] = nc.dram_tensor(nm, shp, BF16, kind="ExternalInput").ap()
    scl_d = nc.dram_tensor("scl", [128, 136], F32, kind="ExternalInput").ap()

    with ExitStack() as ctx:
        sb = lambda n, shp, dt=F32: ctx.enter_context(nc.sbuf_tensor(n, shp, dt))
        xf = sb("xf", [128, 8 * TP])
        xnb = sb("xnb", [128, 8 * TP], BF16)
        hB = sb("hB", [128, 32 * TP], BF16)
        hC = sb("hC", [128, 32 * TP], BF16)
        yv = hC[:, 0:16 * TP].bitcast(F32)
        wbuf = [sb(f"wbuf{i}", [128, 40 * 128], BF16) for i in range(2)]
        xsq = [hB[:, 0:2 * TP].bitcast(F32), hB[:, 2 * TP:4 * TP].bitcast(F32)]
        scl = sb("scl_sb", [128, 136])
        ones_col = sb("ones_col", [128, 1])
        ones_row = sb("ones_row", [1, 128])
        eps_col = sb("eps_col", [128, 1])
        mrow = hC[0:1, 16 * TP:18 * TP].bitcast(F32)
        rrow = hC[0:1, 18 * TP:20 * TP].bitcast(F32)
        srow = hC[0:1, 20 * TP:22 * TP].bitcast(F32)
        gtile = hC[:, 30 * TP:32 * TP]

        ps = [ctx.enter_context(nc.psum_tensor(f"ps{i}", [128, TP], F32))
              for i in range(4)]
        sems = {}
        for s in ["dq", "pe", "sd", "vd", "vch", "sch", "gch"]:
            sems[s] = ctx.enter_context(nc.semaphore("sem_" + s))
        block = ctx.enter_context(nc.Block())

        P = Plan()
        SCL_OFF = {"w1": 0, "w2": 32, "h1": 64, "h2": 96, "w3": 128}

        P.add("vector", lambda e: e.memset(ones_col[:], 1.0))
        P.add("vector", lambda e: e.memset(eps_col[:], 1e-5))
        P.add("vector", lambda e: e.memset(ones_row[:], 1.0), inc=("vd", 1))
        P.bump("vd")
        P.add("sync", lambda e: e.dma_start(out=scl[:], in_=scl_d), inc=("dq", 16))
        P.bump("dq")

        STAGES = [("B", "w1", "w2", 8, 32, xnb, hB),
                  ("C", "h1", None, 32, 32, hB, hC),
                  ("D", "h2", None, 32, 32, hC, hB),
                  ("E", "w3", None, 32, 8, hB, yv)]

        for p in range(NPASS):
            P.add("sync",
                  (lambda p=p: (lambda e: e.dma_start(
                      out=xf.rearrange("q (c t) -> q c t", c=8),
                      in_=xin[:, :, p * TP:(p + 1) * TP])))(),
                  waits=([("vd", P.cnt.get("vd_pass_end", 0))] if p > 0 else []),
                  inc=("dq", 16))
            xldv = P.bump("dq") * 16
            emit_ln_stats(P, xf, 8, TP, ones_col, xsq, mrow, rrow, srow,
                          ps[0][0:1, :], ps[1][0:1, :], eps_col,
                          extra_wait=[("dq", xldv), ("sd", P.val("sd"))])
            emit_bcast_row(P, mrow, ps[2], TP, ones_row, wait=[("vd", P.val("vd"))])
            emit_bcast_row(P, rrow, ps[3], TP, ones_row, wait=[])
            bcast_pe = P.val("pe")
            for ic in range(8):
                P.add("vector",
                      (lambda ic=ic: (lambda e: e.tensor_tensor(
                          out=xsq[0][:, 0:TP], in0=xf[:, ic * TP:(ic + 1) * TP],
                          in1=ps[2][:, 0:TP], op=ALU.subtract)))(),
                      waits=([("pe", bcast_pe)] if ic == 0 else []))
                P.add("vector",
                      (lambda ic=ic: (lambda e: e.tensor_tensor(
                          out=xnb[:, ic * TP:(ic + 1) * TP], in0=xsq[0][:, 0:TP],
                          in1=ps[3][:, 0:TP], op=ALU.mult)))(),
                      inc=(("vd", 1) if ic == 7 else None))
            xnb_vd = P.bump("vd")

            for nm, wA, wB, nic, noc, src, dst in STAGES:
                for oc in range(noc):
                    ibuf = oc % 2
                    wg = []
                    widx = P.cnt.get("wgroup_idx", 0)
                    if widx >= 2:
                        wg.append(("pe", P.cnt[f"wbuf_pe_{widx - 2}"]))
                    P.add("sync",
                          (lambda wA=wA, oc=oc, ibuf=ibuf, nic=nic: (lambda e: e.dma_start(
                              out=wbuf[ibuf][:, 0:nic * 128].rearrange(
                                  "q (c o) -> q c o", c=nic),
                              in_=wdr[wA][oc])))(),
                          waits=wg, inc=("dq", 16))
                    P.bump("dq")
                    if wB is not None:
                        P.add("sync",
                              (lambda wB=wB, oc=oc, ibuf=ibuf, nic=nic: (lambda e: e.dma_start(
                                  out=wbuf[ibuf][:, 32 * 128:(32 + nic) * 128].rearrange(
                                      "q (c o) -> q c o", c=nic),
                                  in_=wdr[wB][oc])))(),
                              inc=("dq", 16))
                        P.bump("dq")
                    wd_here = P.val("dq") * 16

                    psA = ps[oc % 2]
                    psB = ps[2 + oc % 2]
                    mmw = [("dq", wd_here)]
                    if f"sd_at_{nm}_{oc - 2}" in P.cnt:
                        mmw.append(("sd", P.cnt[f"sd_at_{nm}_{oc - 2}"]))
                    if nm == "B":
                        if f"vd_at_{nm}_{oc - 2}" in P.cnt:
                            mmw.append(("vd", P.cnt[f"vd_at_{nm}_{oc - 2}"]))
                        if oc in (0, 1):
                            mmw.append(("vd", xnb_vd))
                    if nm == "C" and oc in (0, 1):
                        mmw.append(("vd", P.cnt[f"vd_at_B_31"]))
                    if nm == "D" and oc in (0, 1):
                        mmw.append(("sd", P.cnt[f"sd_at_C_31"]))
                    if nm == "E" and oc in (0, 1):
                        mmw.append(("sd", P.cnt[f"sd_at_D_31"]))
                    NT5 = TP // 512
                    for s5 in range(NT5):
                        for ic in range(nic):
                            last = (s5 == NT5 - 1) and (ic == nic - 1) and (wB is None)
                            P.add("tensor",
                                  (lambda ic=ic, s5=s5, nic=nic, psA=psA, src=src, ibuf=ibuf:
                                   (lambda e: e.matmul(
                                       out=psA[:, s5 * 512:(s5 + 1) * 512],
                                       lhsT=wbuf[ibuf][:, ic * 128:(ic + 1) * 128],
                                       rhs=src[:, ic * TP + s5 * 512:ic * TP + (s5 + 1) * 512],
                                       start=(ic == 0), stop=(ic == nic - 1))))(),
                                  waits=(mmw if (ic == 0 and s5 == 0) else []),
                                  inc=(("pe", 1) if last else None))
                            mmw = []
                    if wB is not None:
                        for s5 in range(NT5):
                            for ic in range(nic):
                                last = (s5 == NT5 - 1) and (ic == nic - 1)
                                P.add("tensor",
                                      (lambda ic=ic, s5=s5, nic=nic, psB=psB, src=src, ibuf=ibuf:
                                       (lambda e: e.matmul(
                                           out=psB[:, s5 * 512:(s5 + 1) * 512],
                                           lhsT=wbuf[ibuf][:, (32 + ic) * 128:(33 + ic) * 128],
                                           rhs=src[:, ic * TP + s5 * 512:ic * TP + (s5 + 1) * 512],
                                           start=(ic == 0), stop=(ic == nic - 1))))(),
                                      inc=(("pe", 1) if last else None))
                    pe_now = P.bump("pe")
                    P.cnt[f"wbuf_pe_{widx}"] = pe_now
                    P.cnt["wgroup_idx"] = widx + 1

                    if nm == "B":
                        P.add("scalar",
                              (lambda oc=oc, psA=psA: (lambda e: e.activation(
                                  out=gtile[:, (oc % 2) * TP:(oc % 2 + 1) * TP],
                                  in_=psA[:, 0:TP], func=AF.Silu,
                                  scale=scl[:, SCL_OFF["w1"] + oc: SCL_OFF["w1"] + oc + 1])))(),
                              waits=[("pe", pe_now)], inc=("sd", 1))
                        sdv = P.bump("sd")
                        P.cnt[f"sd_at_{nm}_{oc}"] = sdv
                        P.add("vector",
                              (lambda oc=oc, psB=psB, dst=dst: (lambda e: e.scalar_tensor_tensor(
                                  out=dst[:, oc * TP:(oc + 1) * TP], in0=psB[:, 0:TP],
                                  scalar=scl[:, SCL_OFF["w2"] + oc: SCL_OFF["w2"] + oc + 1],
                                  in1=gtile[:, (oc % 2) * TP:(oc % 2 + 1) * TP],
                                  op0=ALU.mult, op1=ALU.mult)))(),
                              waits=[("sd", sdv)], inc=("vd", 1))
                        P.cnt[f"vd_at_{nm}_{oc}"] = P.bump("vd")
                    elif nm in ("C", "D"):
                        P.add("scalar",
                              (lambda oc=oc, psA=psA, dst=dst, wA=wA: (lambda e: e.activation(
                                  out=dst[:, oc * TP:(oc + 1) * TP], in_=psA[:, 0:TP],
                                  func=AF.Gelu,
                                  scale=scl[:, SCL_OFF[wA] + oc: SCL_OFF[wA] + oc + 1])))(),
                              waits=[("pe", pe_now)], inc=("sd", 1))
                        P.cnt[f"sd_at_{nm}_{oc}"] = P.bump("sd")
                    else:
                        P.add("scalar",
                              (lambda oc=oc, psA=psA, dst=dst: (lambda e: e.activation(
                                  out=dst[:, oc * TP:(oc + 1) * TP], in_=psA[:, 0:TP],
                                  func=AF.Copy,
                                  scale=scl[:, SCL_OFF["w3"] + oc: SCL_OFF["w3"] + oc + 1])))(),
                              waits=[("pe", pe_now)], inc=("sd", 1))
                        P.cnt[f"sd_at_{nm}_{oc}"] = P.bump("sd")

            emit_ln_stats(P, yv, 8, TP, ones_col, xsq, mrow, rrow, srow,
                          ps[0][0:1, :], ps[1][0:1, :], eps_col,
                          extra_wait=[("sd", P.val("sd"))])
            emit_bcast_row(P, mrow, ps[2], TP, ones_row, wait=[("vd", P.val("vd"))])
            emit_bcast_row(P, rrow, ps[3], TP, ones_row, wait=[])
            bpe = P.val("pe")
            for ic in range(8):
                P.add("vector",
                      (lambda ic=ic: (lambda e: e.tensor_tensor(
                          out=xsq[0][:, 0:TP], in0=yv[:, ic * TP:(ic + 1) * TP],
                          in1=ps[2][:, 0:TP], op=ALU.subtract)))(),
                      waits=([("pe", bpe)] if ic == 0 else []))
                P.add("vector",
                      (lambda ic=ic: (lambda e: e.tensor_tensor(
                          out=xsq[1][:, 0:TP], in0=xsq[0][:, 0:TP],
                          in1=ps[3][:, 0:TP], op=ALU.mult)))())
                P.add("vector",
                      (lambda ic=ic: (lambda e: e.tensor_tensor(
                          out=yv[:, ic * TP:(ic + 1) * TP], in0=xsq[1][:, 0:TP],
                          in1=xf[:, ic * TP:(ic + 1) * TP], op=ALU.add)))(),
                      inc=(("vd", 1) if ic == 7 else None))
            vout = P.bump("vd")
            P.cnt["vd_pass_end"] = vout
            P.add("sync",
                  (lambda p=p: (lambda e: e.dma_start(
                      out=xout[:, :, p * TP:(p + 1) * TP],
                      in_=yv.rearrange("q (c t) -> q c t", c=8))))(),
                  waits=[("vd", vout)], inc=("dq", 16))
            P.bump("dq")

        P.add("sync", None, waits=[("dq", P.val("dq") * 16)])
        P.run_all(block, sems)

    return nc


def _prep_l0ffn_weights(inputs):
    w = {}
    w["w1"] = _chunk_w(inputs["L0_ffn_w1_w"].T.astype(BF))
    w["w2"] = _chunk_w(inputs["L0_ffn_w2_w"].T.astype(BF))
    w["h1"] = _chunk_w(inputs["L0_ffn_h1_w"].T.astype(BF))
    w["h2"] = _chunk_w(inputs["L0_ffn_h2_w"].T.astype(BF))
    w["w3"] = _chunk_w(inputs["L0_ffn_w3_w"].T.astype(BF))
    scl = np.zeros((128, 136), np.float32)
    scl[:, 0:32] = inputs["L0_ffn_w1_s"].reshape(32, 128).T
    scl[:, 32:64] = inputs["L0_ffn_w2_s"].reshape(32, 128).T
    scl[:, 64:96] = inputs["L0_ffn_h1_s"].reshape(32, 128).T
    scl[:, 96:128] = inputs["L0_ffn_h2_s"].reshape(32, 128).T
    scl[:, 128:136] = inputs["L0_ffn_w3_s"].reshape(8, 128).T
    w["scl"] = scl
    return w


def launch_l0ffn(x1, inputs):
    nc = build_l0ffn()
    w = _prep_l0ffn_weights(inputs)
    xfm = _feat_major(x1.reshape(T_ALL, D))
    in_maps = []
    for c in range(NCORES):
        m = dict(w)
        m["xin"] = np.ascontiguousarray(xfm[:, :, c * T_CORE:(c + 1) * T_CORE])
        in_maps.append(m)
    res = run_bass_kernel_spmd(nc, in_maps, core_ids=list(range(NCORES)))
    out = np.concatenate([r["xout"] for r in res.results], axis=2)
    return _unfeat_major(out).reshape(B, S, D)


def kernel(**inputs):
    x0 = np.asarray(inputs["L0_attn_q"], np.float32)
    x1 = launch_l0attn(x0, inputs)
    x2 = launch_l0ffn(x1, inputs)
    x3 = launch_l1attn(x2, inputs)
    x4 = launch_l1ffn(x3, inputs)
    return np.asarray(x4, np.float32).reshape(2, S, D)



def _alibi_slopes16():
    start = 2.0 ** (-2.0 ** (-(np.log2(16) - 3)))
    return np.array([start * start ** (i) for i in range(16)], dtype=np.float64)


def build_l0attn_core(qsl, nhp=8, debug=False):
    slopes = _alibi_slopes16()
    nc = bass.Bass("TRN2", target_bir_lowering=False, debug=False)
    xb = nc.dram_tensor("xb", [128, 8, S], F32, kind="ExternalInput").ap()
    xout = nc.dram_tensor("xout", [128, 8, T_CORE], F32, kind="ExternalOutput").ap()
    wq_d = nc.dram_tensor("wq", [8, 128, 8, 128], F32, kind="ExternalInput").ap()
    wk_d = nc.dram_tensor("wk", [8, 128, 8, 128], F32, kind="ExternalInput").ap()
    wv_d = nc.dram_tensor("wv", [8, 128, 8, 128], F32, kind="ExternalInput").ap()
    wo_d = nc.dram_tensor("wo", [8, 128, 8, 128], F32, kind="ExternalInput").ap()
    bo_d = nc.dram_tensor("bo", [128, 8], F32, kind="ExternalInput").ap()
    bcol_d = nc.dram_tensor("bcol", [128, 16, 256], F32, kind="ExternalInput").ap()
    dtile_d = nc.dram_tensor("dtile", [4, 128, 512], F32, kind="ExternalInput").ap()
    qramp_d = nc.dram_tensor("qramp", [8, 2048], BF16, kind="ExternalInput").ap()
    kaug_d = nc.dram_tensor("kaug", [16, 8, S], BF16, kind="ExternalInput").ap()
    xnd = nc.dram_tensor("xnd", [128, 8, S], BF16, kind="Internal").ap()
    if debug:
        dbg_xn = nc.dram_tensor("dbg_xn", [128, 8, 2048], BF16, kind="ExternalOutput").ap()
        dbg_on = nc.dram_tensor("dbg_on", [128, 8, 2048], BF16, kind="ExternalOutput").ap()
        dbg_y = nc.dram_tensor("dbg_y", [128, 8, 1024], F32, kind="ExternalOutput").ap()
        dbg_st = nc.dram_tensor("dbg_st", [1, 2048], F32, kind="ExternalOutput").ap()
        dbg_y2 = nc.dram_tensor("dbg_y2", [128, 8, 1024], F32, kind="ExternalOutput").ap()
        dbg_x0 = nc.dram_tensor("dbg_x0", [128, 8, 1024], F32, kind="ExternalOutput").ap()

    with ExitStack() as ctx:
        sb = lambda n, shp, dt=F32: ctx.enter_context(nc.sbuf_tensor(n, shp, dt))
        bigA = sb("bigA", [128, 20544])
        xblk = bigA
        kt = [bigA[0:72, 0:4096].bitcast(BF16), bigA[0:72, 4096:8192].bitcast(BF16)]
        vt = [bigA[:, 8192:10272].bitcast(BF16), bigA[:, 10272:12352].bitcast(BF16)]
        xnkb2 = [bigA[:, 12352:16448].bitcast(BF16), bigA[:, 16448:20544].bitcast(BF16)]
        xnown = sb("xnown", [128, 8 * 2048], BF16)
        yvh = xnown.bitcast(F32)
        qthp = sb("qthp", [128, 4096])
        qt = [qthp[0:72, 0:1024].bitcast(BF16), qthp[0:72, 1024:2048].bitcast(BF16)]
        xsq = [qthp[:, 0:1024], qthp[:, 1024:2048]]
        xsq4 = xsq
        onorm = sb("onorm", [128, 8 * 2048], BF16)
        pbuf = [sb(f"pbuf{i}", [128, 512], BF16) for i in range(2)]
        wc32 = sb("wc32", [128, 8 * 128])
        wcb = sb("wcb", [128, 8 * 128], BF16)
        wcb2 = sb("wcb2", [128, 8 * 128], BF16)
        bcol = sb("bcol_sb", [128, 2 * 4 * 64])
        dts = sb("dts", [128, 512])
        osb = sb("osb", [65, 512])
        dnr = sb("dnr", [1, 512])
        ones_col = sb("ones_col", [128, 1])
        ones_row = sb("ones_row", [1, 128])
        eps_col = sb("eps_col", [128, 1])
        bo_sb = sb("bo_sb", [128, 8])
        mrow = sb("mrow", [1, 1024])
        rrow = sb("rrow", [1, 1024])
        srow = sb("srow", [1, 1024])

        pp = [ctx.enter_context(nc.psum_tensor(f"pp{i}", [128, 512], F32)) for i in range(2)]
        po = [ctx.enter_context(nc.psum_tensor(f"po{i}", [65, 512], F32)) for i in range(4)]
        pb = ctx.enter_context(nc.psum_tensor("pb", [128, 1024], F32))
        sems = {}
        for s_ in ["dq", "pe", "sd", "vd", "vch", "sch", "gch"]:
            sems[s_] = ctx.enter_context(nc.semaphore("sem_" + s_))
        block = ctx.enter_context(nc.Block())
        P = Plan()

        P.add("vector", lambda e: e.memset(ones_col[:], 1.0))
        P.add("vector", lambda e: e.memset(ones_row[:], 1.0))
        P.add("vector", lambda e: e.memset(eps_col[:], 1e-5))
        P.add("vector", lambda e: e.memset(onorm[:], 0.0), inc=("vd", 1))
        P.bump("vd")
        P.add("sync", lambda e: e.dma_start(out=bo_sb[:], in_=bo_d), inc=("dq", 16))
        P.bump("dq")

        for tb in range(4):
            P.add("sync", (lambda tb=tb: (lambda e: e.dma_start(
                out=xblk[:, 0:16384].rearrange("q (c t) -> q c t", c=8),
                in_=xb[:, :, tb * 2048:(tb + 1) * 2048])))(),
                waits=([("dq", P.val("dq") * 16)] if tb else []), inc=("dq", 16))
            xldv = P.bump("dq") * 16
            for half in range(2):
                off = half * 1024
                emit_ln_stats(P, xblk, 8, 1024, ones_col, xsq, mrow, rrow, srow,
                              pb[0:1, :], pb[32:33, :], eps_col,
                              [("dq", xldv), ("sd", P.val("sd")), ("vd", P.val("vd"))],
                              CS=2048, off=off)
                statvd = P.val("vd")
                for q5 in range(2):
                    emit_bcast_row(P, mrow[:, q5 * 512:(q5 + 1) * 512], pb[:, 0:512],
                                   512, ones_row, [("vd", P.val("vd"))])
                    emit_bcast_row(P, rrow[:, q5 * 512:(q5 + 1) * 512], pb[:, 512:1024],
                                   512, ones_row, [])
                    bpe = P.val("pe")
                    for ic in range(8):
                        o2 = ic * 2048 + off + q5 * 512
                        P.add("vector", (lambda o2=o2: (lambda e: e.tensor_tensor(
                            out=xsq[0][:, 0:512], in0=xblk[:, o2:o2 + 512],
                            in1=pb[:, 0:512], op=ALU.subtract)))(),
                            waits=([("pe", bpe), ("dq", P.val("dq") * 16)] if ic == 0 else []))
                        P.add("vector", (lambda o2=o2: (lambda e: e.tensor_tensor(
                            out=xnown[:, o2:o2 + 512], in0=xsq[0][:, 0:512],
                            in1=pb[:, 512:1024], op=ALU.mult)))(),
                            inc=(("vd", 1) if ic == 7 else None))
                    P.bump("vd")
            xnvd = P.val("vd")
            P.add("sync", (lambda tb=tb: (lambda e: e.dma_start(
                out=xnd[:, :, tb * 2048:(tb + 1) * 2048],
                in_=xnown.rearrange("q (c t) -> q c t", c=8))))(),
                waits=[("vd", xnvd)], inc=("dq", 16))
            P.bump("dq")
        P.add("sync", (lambda: (lambda e: e.dma_start(
            out=xnown.rearrange("q (c t) -> q c t", c=8),
            in_=xnd[:, :, qsl:qsl + 2048])))(),
            waits=[("dq", P.val("dq") * 16)], inc=("dq", 16))
        xnown_ld = P.bump("dq") * 16

        QBS = [qsl + i * 512 for i in range(4)]

        for hp in range(nhp):
            flash_prev_pe = P.cnt.get(f"flashend_{hp - 1}", 0)
            for wnm, wdram, dstc in [("wq", wq_d, wcb), ("wk", wk_d, wcb), ("wv", wv_d, wcb2)]:
                P.add("sync", (lambda wdram=wdram, hp=hp: (lambda e: e.dma_start(
                    out=wc32.rearrange("q (c o) -> q c o", c=8), in_=wdram[hp])))(),
                    waits=[("vd", P.cnt.get("cast_vd", 0))], inc=("dq", 16))
                ldv = P.bump("dq") * 16
                wkey = "wcb_free" if dstc is wcb else "wcb2_free"
                P.add("vector", (lambda dstc=dstc: (lambda e: e.tensor_copy(
                    dstc[:], wc32[:])))(),
                    waits=[("dq", ldv), ("pe", P.cnt.get(wkey, 0))],
                    inc=("vd", 1))
                P.cnt["cast_vd"] = P.bump("vd")
                P.cnt[f"castvd_{wnm}"] = P.val("vd")
                if wnm == "wq":
                    for t5 in range(4):
                        w0 = [("vd", P.val("vd")), ("dq", xnown_ld),
                              ("sd", P.val("sd"))]
                        if hp > 0:
                            w0.append(("pe", flash_prev_pe))
                        for ic in range(8):
                            P.add("tensor", (lambda t5=t5, ic=ic: (lambda e: e.matmul(
                                out=pb[:, 0:512], lhsT=wcb[:, ic * 128:(ic + 1) * 128],
                                rhs=xnown[:, ic * 2048 + t5 * 512:ic * 2048 + (t5 + 1) * 512],
                                start=(ic == 0), stop=(ic == 7))))(),
                                waits=(w0 if ic == 0 else []),
                                inc=(("pe", 1) if ic == 7 else None))
                        penow = P.bump("pe")
                        P.cnt["wcb_free"] = penow
                        for h2 in range(2):
                            P.add("scalar", (lambda h2=h2, t5=t5: (lambda e: e.activation(
                                out=qt[h2][0:64, t5 * 512:(t5 + 1) * 512],
                                in_=pb[h2 * 64:(h2 + 1) * 64, 0:512],
                                func=AF.Copy, scale=0.125)))(),
                                waits=[("pe", penow)], inc=("sd", 1))
                            P.bump("sd")
                    for h2 in range(2):
                        P.add("sync", (lambda h2=h2: (lambda e: e.dma_start(
                            out=qt[h2][64:72, :], in_=qramp_d)))(),
                            waits=[("pe", flash_prev_pe)], inc=("dq", 16))
                        P.bump("dq")
            P.add("sync", (lambda hp=hp: (lambda e: e.dma_start(
                out=bcol.rearrange("q (h c) -> q h c", h=2),
                in_=bcol_d[:, 2 * hp:2 * hp + 2, :])))(),
                waits=[("sd", P.cnt.get(f"expend_{hp - 1}", 0))], inc=("dq", 16))
            bcol_ld = P.bump("dq") * 16
            for h2 in range(2):
                P.add("sync", (lambda h2=h2, hp=hp: (lambda e: e.dma_start(
                    out=kt[h2][64:72, :], in_=kaug_d[2 * hp + h2])))(),
                    waits=[("pe", flash_prev_pe)], inc=("dq", 16))
                P.bump("dq")
                P.add("vector", (lambda h2=h2: (lambda e: e.memset(
                    vt[h2].rearrange("q (k c) -> q k c", c=65)[:, :, 64:65], 1.0)))(),
                    waits=[("pe", flash_prev_pe)], inc=("vd", 1))
                P.cnt["cast_vd"] = P.bump("vd")
            memset_vd = P.val("vd")
            kaug_ld = P.val("dq") * 16

            for kb in range(8):
                P.add("sync", (lambda kb=kb: (lambda e: e.dma_start(
                    out=xnkb2[kb % 2].rearrange("q (c t) -> q c t", c=8),
                    in_=xnd[:, :, kb * 1024:(kb + 1) * 1024])))(),
                    waits=[("pe", P.cnt.get(f"kvproj_{hp}_{kb - 2}", 0))], inc=("dq", 16))
                kbld = P.bump("dq") * 16
                for t5 in range(2):
                    w0 = [("dq", kbld), ("vd", memset_vd), ("sd", P.val("sd"))]
                    for ic in range(8):
                        P.add("tensor", (lambda t5=t5, ic=ic, kb=kb: (lambda e: e.matmul(
                            out=pb[:, 0:512], lhsT=wcb[:, ic * 128:(ic + 1) * 128],
                            rhs=xnkb2[kb % 2][:, ic * 1024 + t5 * 512:ic * 1024 + (t5 + 1) * 512],
                            start=(ic == 0), stop=(ic == 7))))(),
                            waits=(w0 if ic == 0 else []),
                            inc=(("pe", 1) if ic == 7 else None))
                        w0 = []
                    penow = P.bump("pe")
                    P.cnt["wcb_free"] = penow
                    for h2 in range(2):
                        P.add("scalar", (lambda h2=h2, kb=kb, t5=t5: (lambda e: e.activation(
                            out=kt[h2][0:64, kb * 1024 + t5 * 512:kb * 1024 + (t5 + 1) * 512],
                            in_=pb[h2 * 64:(h2 + 1) * 64, 0:512], func=AF.Copy)))(),
                            waits=[("pe", penow)], inc=("sd", 1))
                        P.bump("sd")
                for kc8 in range(8):
                    reg = 512 if kc8 % 2 == 0 else 0
                    w0 = [("sd", P.cnt.get(f"vproj_sd_{hp}_{kb}_{kc8 - 2}", P.cnt.get(f"vproj_base_{hp}_{kb}", 0)))]
                    if kc8 == 0:
                        P.cnt[f"vproj_base_{hp}_{kb}"] = P.val("sd")
                        w0 = [("sd", P.val("sd"))]
                    for ic in range(8):
                        P.add("tensor", (lambda kc8=kc8, ic=ic, reg=reg, kb=kb: (lambda e: e.matmul(
                            out=pb[:, reg:reg + 128],
                            lhsT=xnkb2[kb % 2][:, ic * 1024 + kc8 * 128:ic * 1024 + (kc8 + 1) * 128],
                            rhs=wcb2[:, ic * 128:(ic + 1) * 128],
                            start=(ic == 0), stop=(ic == 7))))(),
                            waits=(w0 if ic == 0 else []),
                            inc=(("pe", 1) if ic == 7 else None))
                        w0 = []
                    penow = P.bump("pe")
                    P.cnt["wcb2_free"] = penow
                    kcg = kb * 8 + kc8
                    for h2 in range(2):
                        P.add("scalar", (lambda h2=h2, kcg=kcg, reg=reg: (lambda e: e.activation(
                            out=vt[h2][:, kcg * 65:kcg * 65 + 64],
                            in_=pb[:, reg + h2 * 64:reg + (h2 + 1) * 64], func=AF.Copy)))(),
                            waits=[("pe", penow)], inc=("sd", 1))
                        P.cnt[f"vproj_sd_{hp}_{kb}_{kc8}"] = P.bump("sd")
                P.cnt[f"kvproj_{hp}_{kb}"] = P.val("pe")
            proj_sd = P.val("sd")

            for qbh in range(2):
                pairs = [(h2, qbh * 2 + qb2) for h2 in range(2) for qb2 in range(2)]
                ntile = 0
                tile_info = {}
                for kc in range(64):
                    for i4, (h2, qbg) in enumerate(pairs):
                        n = ntile
                        ntile += 1
                        q0g = QBS[qbg]
                        d0 = q0g - 128 * kc
                        if d0 >= 128:
                            kind, rows = "L", 68
                        elif d0 <= -512:
                            kind, rows = "R", 72
                        else:
                            kind, rows = "S", 64
                        h = 2 * hp + h2
                        bcidx = h2 * 256 + qbg * 64 + kc
                        w1 = [("sd", P.cnt.get(f"exp_{hp}_{qbh}_{n - 2}", 0))]
                        if n < 4:
                            w1.append(("sd", P.val("sd")))
                            w1.append(("dq", bcol_ld))
                            w1.append(("dq", kaug_ld))
                            w1.append(("vd", P.cnt.get(f"fin_{hp}_{qbh - 1}", P.cnt.get(f"fin_{hp - 1}_1", 0))))
                        P.add("tensor", (lambda h2=h2, qbg=qbg, kc=kc, rows=rows, n=n: (lambda e: e.matmul(
                            out=pp[n % 2][:, :], lhsT=kt[h2][0:rows, kc * 128:(kc + 1) * 128],
                            rhs=qt[h2][0:rows, (qbg % 4) * 512:(qbg % 4) * 512 + 512],
                            start=True, stop=True)))(),
                            waits=w1, inc=("pe", 1))
                        mm1pe = P.bump("pe")
                        expw = [("pe", mm1pe)]
                        if kind == "S":
                            didx = -d0 // 128
                            P.add("sync", (lambda didx=didx: (lambda e: e.dma_start(
                                out=dts[:], in_=dtile_d[didx])))(),
                                waits=[("vd", P.cnt.get("dts_vd", 0))], inc=("dq", 16))
                            dldv = P.bump("dq") * 16
                            sl = float(slopes[h])
                            P.add("vector", (lambda n=n, sl=sl: (lambda e: e.scalar_tensor_tensor(
                                out=pp[n % 2][:, :], in0=dts[:], scalar=-sl,
                                in1=pp[n % 2][:, :], op0=ALU.mult, op1=ALU.add)))(),
                                waits=[("pe", mm1pe), ("dq", dldv)], inc=("vd", 1))
                            P.cnt["dts_vd"] = P.bump("vd")
                            expw.append(("vd", P.val("vd")))
                        expw.append(("pe", P.cnt.get(f"av_{hp}_{qbh}_{n - 2}", 0)))
                        P.add("scalar", (lambda n=n, bcidx=bcidx: (lambda e: e.activation(
                            out=pbuf[n % 2][:, :], in_=pp[n % 2][:, :], func=AF.Exp,
                            bias=bcol[:, bcidx:bcidx + 1])))(),
                            waits=expw, inc=("sd", 1))
                        P.cnt[f"exp_{hp}_{qbh}_{n}"] = P.bump("sd")
                        P.add("tensor", (lambda h2=h2, kc=kc, n=n, i4=i4: (lambda e: e.matmul(
                            out=po[i4][:, :], lhsT=vt[h2][:, kc * 65:(kc + 1) * 65],
                            rhs=pbuf[n % 2][:, :],
                            start=(kc == 0), stop=(kc == 63))))(),
                            waits=[("sd", P.cnt[f"exp_{hp}_{qbh}_{n}"])], inc=("pe", 1))
                        P.cnt[f"av_{hp}_{qbh}_{n}"] = P.bump("pe")
                P.cnt[f"expend_{hp}"] = P.val("sd")
                for i4, (h2, qbg) in enumerate(pairs):
                    lastav = P.cnt[f"av_{hp}_{qbh}_{(63 * 4) + i4}"]
                    P.add("scalar", (lambda i4=i4: (lambda e: e.copy(
                        osb[:], po[i4][:, :])))(),
                        waits=[("pe", lastav), ("vd", P.cnt.get("fin_busy", 0))], inc=("sd", 1))
                    sdn = P.bump("sd")
                    P.add("vector", lambda e: e.reciprocal(out=dnr[:], in_=osb[64:65, :]),
                          waits=[("sd", sdn)], inc=("vd", 1))
                    vdn = P.bump("vd")
                    P.add("tensor", lambda e: e.matmul(
                        out=pp[0][0:64, :], lhsT=ones_row[0:1, 0:64], rhs=dnr[:],
                        start=True, stop=True),
                        waits=[("vd", vdn), ("sd", P.cnt.get("fin_sd2", 0))], inc=("pe", 1))
                    pen = P.bump("pe")
                    h = 2 * hp + h2
                    P.add("vector", (lambda h2=h2, hp=hp, qbg=qbg: (lambda e: e.tensor_tensor(
                        out=onorm[h2 * 64:(h2 + 1) * 64,
                                  hp * 2048 + qbg * 512:hp * 2048 + (qbg + 1) * 512],
                        in0=osb[0:64, :], in1=pp[0][0:64, :], op=ALU.mult)))(),
                        waits=[("pe", pen)], inc=("vd", 1))
                    P.cnt["fin_busy"] = P.bump("vd")
                    P.cnt["fin_sd2"] = P.val("sd")
                P.cnt[f"fin_{hp}_{qbh}"] = P.val("vd")
            P.cnt[f"flashend_{hp}"] = P.val("pe")

        onorm_vd = P.val("vd")
        if debug:
            P.add("sync", lambda e: e.dma_start(
                out=dbg_xn, in_=xnown.rearrange("q (c t) -> q c t", c=8)),
                waits=[("vd", onorm_vd)], inc=("dq", 16))
            P.bump("dq")
            P.add("sync", lambda e: e.dma_start(
                out=dbg_on, in_=onorm.rearrange("q (c t) -> q c t", c=8)),
                inc=("dq", 16))
            P.bump("dq")
        for th in range(2):
            toff = th * 1024
            for oc in range(8):
                P.add("sync", (lambda oc=oc: (lambda e: e.dma_start(
                    out=wc32.rearrange("q (c o) -> q c o", c=8), in_=wo_d[oc])))(),
                    waits=[("pe", P.cnt.get("womm_prev", 0))], inc=("dq", 16))
                ldv = P.bump("dq") * 16
                P.add("vector", lambda e: e.tensor_copy(wcb[:], wc32[:]),
                      waits=[("dq", ldv)], inc=("vd", 1))
                cvd = P.bump("vd")
                for t5 in range(2):
                    w0 = [("vd", cvd), ("vd", onorm_vd),
                          ("sd", P.cnt.get(f"wo_sd_{th}_{oc - 1}_{t5}", P.cnt.get("wo_base", 0)))]
                    if oc == 0:
                        w0 += [("vd", P.val("vd")), ("sd", P.val("sd")),
                               ("dq", P.val("dq") * 16)]
                    for ic in range(8):
                        P.add("tensor", (lambda oc=oc, t5=t5, ic=ic, toff=toff: (lambda e: e.matmul(
                            out=pb[:, (t5 % 2) * 512:(t5 % 2) * 512 + 512],
                            lhsT=wcb[:, ic * 128:(ic + 1) * 128],
                            rhs=onorm[:, ic * 2048 + toff + t5 * 512:ic * 2048 + toff + (t5 + 1) * 512],
                            start=(ic == 0), stop=(ic == 7))))(),
                            waits=(w0 if ic == 0 else []),
                            inc=(("pe", 1) if ic == 7 else None))
                        w0 = []
                    penow = P.bump("pe")
                    P.cnt["womm_prev"] = penow
                    P.add("scalar", (lambda oc=oc, t5=t5: (lambda e: e.activation(
                        out=yvh[:, oc * 1024 + t5 * 512:oc * 1024 + (t5 + 1) * 512],
                        in_=pb[:, (t5 % 2) * 512:(t5 % 2) * 512 + 512],
                        func=AF.Identity, bias=bo_sb[:, oc:oc + 1])))(),
                        waits=[("pe", penow)], inc=("sd", 1))
                    P.cnt[f"wo_sd_{th}_{oc}_{t5}"] = P.bump("sd")
            if debug and th == 1:
                P.add("sync", lambda e: e.dma_start(
                    out=dbg_y, in_=yvh.rearrange("q (c t) -> q c t", c=8)),
                    waits=[("sd", P.val("sd"))], inc=("dq", 16))
                P.bump("dq")
            emit_ln_stats(P, yvh, 8, 1024, ones_col, xsq4, mrow, rrow, srow,
                          pb[0:1, :], pb[32:33, :], eps_col,
                          [("sd", P.val("sd"))], CS=1024, off=0)
            statvd = P.val("vd")
            if debug and th == 1:
                P.add("sync", lambda e: e.dma_start(out=dbg_st[:, 0:1024], in_=mrow[:]),
                      waits=[("vd", statvd)], inc=("dq", 16))
                P.bump("dq")
                P.add("sync", lambda e: e.dma_start(out=dbg_st[:, 1024:2048], in_=rrow[:]),
                      inc=("dq", 16))
                P.bump("dq")
            P.add("sync", (lambda toff=toff: (lambda e: e.dma_start(
                out=xblk[:, 0:8192].rearrange("q (c t) -> q c t", c=8),
                in_=xb[:, :, qsl + toff:qsl + toff + 1024])))(),
                waits=[("pe", P.cnt.get(f"flashend_7", 0))], inc=("dq", 16))
            x0ld = P.bump("dq") * 16
            for q5 in range(2):
                emit_bcast_row(P, mrow[:, q5 * 512:(q5 + 1) * 512], pb[:, 0:512],
                               512, ones_row, [("vd", P.val("vd"))])
                emit_bcast_row(P, rrow[:, q5 * 512:(q5 + 1) * 512], pb[:, 512:1024],
                               512, ones_row, [])
                bpe = P.val("pe")
                for ic in range(8):
                    o2 = ic * 1024 + q5 * 512
                    P.add("vector", (lambda o2=o2: (lambda e: e.tensor_tensor(
                        out=xsq4[0][:, 0:512], in0=yvh[:, o2:o2 + 512],
                        in1=pb[:, 0:512], op=ALU.subtract)))(),
                        waits=([("pe", bpe), ("dq", x0ld)] if ic == 0 else []))
                    P.add("vector", (lambda o2=o2: (lambda e: e.tensor_tensor(
                        out=xsq4[1][:, 0:512], in0=xsq4[0][:, 0:512],
                        in1=pb[:, 512:1024], op=ALU.mult)))())
                    P.add("vector", (lambda o2=o2: (lambda e: e.tensor_tensor(
                        out=yvh[:, o2:o2 + 512], in0=xsq4[1][:, 0:512],
                        in1=xblk[:, o2:o2 + 512], op=ALU.add)))(),
                        inc=(("vd", 1) if ic == 7 else None))
                P.bump("vd")
            vout = P.val("vd")
            if debug and th == 1:
                P.add("sync", lambda e: e.dma_start(
                    out=dbg_y2, in_=yvh.rearrange("q (c t) -> q c t", c=8)),
                    waits=[("vd", vout)], inc=("dq", 16))
                P.bump("dq")
                P.add("sync", lambda e: e.dma_start(
                    out=dbg_x0, in_=xblk[:, 0:8192].rearrange("q (c t) -> q c t", c=8)),
                    inc=("dq", 16))
                P.bump("dq")
            P.add("sync", (lambda th=th, toff=toff: (lambda e: e.dma_start(
                out=xout[:, :, toff:toff + 1024],
                in_=yvh.rearrange("q (c t) -> q c t", c=8))))(),
                waits=[("vd", vout)], inc=("dq", 16))
            P.bump("dq")

        P.add("sync", None, waits=[("dq", P.val("dq") * 16)])
        P.run_all(block, sems)
    return nc


def _prep_l0attn_inputs(inputs):
    slopes = _alibi_slopes16()
    com = {}
    com["wq"] = _chunk_w(np.ascontiguousarray(inputs["L0_attn_Wq"].T))
    com["wk"] = _chunk_w(np.ascontiguousarray(inputs["L0_attn_Wk"].T))
    com["wv"] = _chunk_w(np.ascontiguousarray(inputs["L0_attn_Wv"].T))
    com["wo"] = _chunk_w(np.ascontiguousarray(inputs["L0_attn_Wo"].T))
    com["bo"] = np.ascontiguousarray(inputs["L0_attn_bo"].reshape(8, 128).T)
    qi = np.arange(2048) % 512
    a = (qi >> 4).astype(np.float32)
    bq = (qi & 15).astype(np.float32)
    com["qramp"] = np.ascontiguousarray(
        np.stack([a, bq, a, bq, a, bq, a, bq]).astype(BF))
    ka = np.zeros((16, 8, S), np.float64)
    for h in range(16):
        sl = float(slopes[h])
        shi = float(np.float32(BF(sl)))
        slo = sl - shi
        vals = [-16 * shi, -shi, -16 * slo, -slo, 32 * shi, 2 * shi, 32 * slo, 2 * slo]
        for j, v in enumerate(vals):
            ka[h, j, :] = v
    com["kaug"] = ka.astype(BF)
    dt = np.zeros((4, 128, 512), np.float32)
    for idx in range(4):
        d0 = -128 * idx
        kp = np.arange(128)[:, None]
        qi2 = np.arange(512)[None, :]
        dt[idx] = np.abs(d0 + qi2 - kp)
    com["dtile"] = dt
    return com


def _prep_l0attn_bcol(qsl):
    slopes = _alibi_slopes16()
    bc = np.zeros((16, 4, 64, 128), np.float32)
    kp = np.arange(128)
    for h in range(16):
        sl = slopes[h]
        for qb in range(4):
            q0g = qsl + qb * 512
            for kc in range(64):
                d0 = q0g - 128 * kc
                if d0 >= 128:
                    bc[h, qb, kc] = sl * (kp - d0)
                elif d0 <= -512:
                    bc[h, qb, kc] = sl * (d0 - kp)
    return np.ascontiguousarray(bc.transpose(3, 0, 1, 2).reshape(128, 16, 256))


def launch_l0attn(x0, inputs):
    com = _prep_l0attn_inputs(inputs)
    xfm = [_feat_major(x0[b]) for b in range(B)]
    results = [None] * NCORES
    for qi4 in range(4):
        qsl = qi4 * 2048
        nc = build_l0attn_core(qsl)
        cores = [qi4, 4 + qi4]
        in_maps = []
        for c in cores:
            m = dict(com)
            m["bcol"] = _prep_l0attn_bcol(qsl)
            m["xb"] = xfm[c // 4]
            in_maps.append(m)
        res = run_bass_kernel_spmd(nc, in_maps, core_ids=cores)
        for i, c in enumerate(cores):
            results[c] = res.results[i]["xout"]
    out = np.concatenate([results[c] for c in range(NCORES)], axis=2)
    return _unfeat_major(out).reshape(B, S, D)


U32 = mybir.dt.uint32
RMS_EPS = 1.1920929e-07


def build_l1ffn():
    nc = bass.Bass("TRN2", target_bir_lowering=False, debug=False)
    xin = nc.dram_tensor("xin", [128, 8, T_CORE], F32, kind="ExternalInput").ap()
    xout = nc.dram_tensor("xout", [128, 8, T_CORE], F32, kind="ExternalOutput").ap()
    rg_d = nc.dram_tensor("rg", [8, 32, 128, 8, 128], F32, kind="ExternalInput").ap()
    ru_d = nc.dram_tensor("ru", [8, 32, 128, 8, 128], F32, kind="ExternalInput").ap()
    rd_d = nc.dram_tensor("rd", [8, 8, 128, 32, 128], F32, kind="ExternalInput").ap()
    sg_d = nc.dram_tensor("sg", [64, 128, 8, 128], F32, kind="ExternalInput").ap()
    su_d = nc.dram_tensor("su", [64, 128, 8, 128], F32, kind="ExternalInput").ap()
    sd_d = nc.dram_tensor("sd", [8, 128, 64, 128], F32, kind="ExternalInput").ap()
    rwt_d = nc.dram_tensor("rwt", [128, 8, 8], F32, kind="ExternalInput").ap()
    tril_d = nc.dram_tensor("tril", [128, 128], F32, kind="ExternalInput").ap()
    tcol_d = nc.dram_tensor("tcol", [8, 1], F32, kind="ExternalInput").ap()
    idn_d = nc.dram_tensor("idn", [128, 128], F32, kind="ExternalInput").ap()
    tot_bounce = nc.dram_tensor("tot_bounce", [1, 8], F32).ap()
    tot_gath = nc.dram_tensor("tot_gath", [8, 8], F32).ap()
    wtd = nc.dram_tensor("wtd", [1, 16384], F32, kind="Internal").ap()

    with ExitStack() as ctx:
        sb = lambda n, shp, dt=F32: ctx.enter_context(nc.sbuf_tensor(n, shp, dt))
        racc = sb("racc", [128, 8 * 2048])
        xb = sb("xb", [128, 8 * 2048], BF16)
        hh = sb("hh", [128, 32 * 1024], BF16)
        xsq = [hh[:, 0:2048].bitcast(F32), hh[:, 2048:4096].bitcast(F32)]
        xf3 = sb("xf3", [128, 8 * 1024])
        gt2 = [xf3[:, 0:512].bitcast(BF16), xf3[:, 512:1024].bitcast(BF16)]
        wscr = xf3[:, 1024:1536]
        wra = sb("wra", [128, 8 * 128])
        wrb = sb("wrb", [128, 8 * 128])
        wca = sb("wca", [128, 8 * 128], BF16)
        wcb_ = sb("wcb_", [128, 8 * 128], BF16)
        wrd = xf3[:, 1536:3584]
        wcd = xf3[:, 3584:4608].bitcast(BF16)
        wbt = xf3[:, 4608:6656]
        wrow = xf3[0:1, 6656:7680]
        lgt = sb("lgt", [128, 16 * 8])
        selt = sb("selt", [128, 16 * 8])
        wdns = sb("wdns", [128, 16 * 8])
        wcols = sb("wcols", [128, 128])
        rmsc = sb("rmsc", [128, 16])
        i1f = sb("i1f", [128, 2])
        rw12 = sb("rw12", [128, 2])
        mx8 = sb("mx8", [128, 8])
        mi8 = ctx.enter_context(nc.sbuf_tensor("mi8", [128, 8], U32))
        miu = ctx.enter_context(nc.sbuf_tensor("miu", [128, 8], U32))
        scr8 = sb("scr8", [128, 24])
        eidx = sb("eidx", [128, 8])
        trow = sb("trow", [1, 8])
        brow = sb("brow", [1, 8])
        orow = sb("orow", [1, 8])
        crow = sb("crow", [1, 8])
        tg8 = sb("tg8", [8, 8])
        tril = hh[:, 14336:14592].bitcast(F32)
        tcol = sb("tcol_sb", [8, 1])
        idn = hh[:, 14592:14848].bitcast(F32)
        rwt = sb("rwt_sb", [128, 8 * 8])
        rwtb = wca[:, 0:64]
        ones_col = sb("ones_col", [128, 1])
        ones_row = sb("ones_row", [1, 128])
        eps_col = sb("eps_col", [128, 1])
        ep2_col = sb("ep2_col", [128, 1])
        mrow = hh[0:1, 4096:6144].bitcast(F32)
        rrow = hh[0:1, 6144:8192].bitcast(F32)
        srow = hh[0:1, 8192:10240].bitcast(F32)
        rms_row = hh[0:1, 10240:14336].bitcast(F32)

        ps = [ctx.enter_context(nc.psum_tensor(f"ps{i}", [128, 1024], F32)) for i in range(4)]
        sems = {}
        for s_ in ["dq", "pe", "sd", "vd", "vch", "sch", "gch", "cc", "gq"]:
            sems[s_] = ctx.enter_context(nc.semaphore("sem_" + s_))
        block = ctx.enter_context(nc.Block())
        P = Plan()

        P.add("vector", lambda e: e.memset(ones_col[:], 1.0))
        P.add("vector", lambda e: e.memset(eps_col[:], 1e-5))
        P.add("vector", lambda e: e.memset(ep2_col[:], RMS_EPS))
        P.add("vector", lambda e: e.memset(racc[:], 0.0))
        P.add("vector", lambda e: e.memset(orow[:], 0.0))
        P.add("vector", lambda e: e.memset(ones_row[:], 1.0), inc=("vd", 1))
        P.bump("vd")
        for a, d in [(tril_d, tril), (tcol_d, tcol), (idn_d, idn)]:
            P.add("sync", (lambda a=a, d=d: (lambda e: e.dma_start(out=d[:], in_=a)))(),
                  inc=("dq", 16))
            P.bump("dq")
        P.add("sync", lambda e: e.dma_start(
            out=rwt.rearrange("q (c o) -> q c o", c=8), in_=rwt_d), inc=("dq", 16))
        rwt_ld = P.bump("dq") * 16
        P.add("vector", lambda e: e.tensor_copy(rwtb[:], rwt[:]),
              waits=[("dq", rwt_ld)], inc=("vd", 1))
        P.bump("vd")
        P.add("gpsimd", lambda e: e.iota(
            out=eidx[:], pattern=[[1, 8]], base=0, channel_multiplier=0,
            allow_small_or_imprecise_dtypes=True), inc=("gch", 1))
        P.bump("gch")

        for half in range(2):
            off = half * 1024
            P.add("sync", (lambda half=half: (lambda e: e.dma_start(
                out=xf3.rearrange("q (c t) -> q c t", c=8),
                in_=xin[:, :, half * 1024:(half + 1) * 1024])))(),
                waits=[("vd", P.val("vd"))], inc=("dq", 16))
            xldv = P.bump("dq") * 16
            emit_ln_stats(P, xf3, 8, 1024, ones_col, xsq, mrow, rrow, srow,
                          ps[0][0:1, :], ps[1][0:1, :], eps_col,
                          [("dq", xldv), ("sd", P.val("sd")), ("vd", P.val("vd"))],
                          CS=1024, off=0)
            for q5 in range(2):
                emit_bcast_row(P, mrow[:, q5 * 512:(q5 + 1) * 512], ps[2][:, 0:512],
                               512, ones_row, [("vd", P.val("vd"))])
                emit_bcast_row(P, rrow[:, q5 * 512:(q5 + 1) * 512], ps[3][:, 0:512],
                               512, ones_row, [])
                bpe = P.val("pe")
                for ic in range(8):
                    o2 = ic * 1024 + q5 * 512
                    od = ic * 2048 + off + q5 * 512
                    P.add("vector", (lambda o2=o2: (lambda e: e.tensor_tensor(
                        out=xsq[0][:, 0:512], in0=xf3[:, o2:o2 + 512],
                        in1=ps[2][:, 0:512], op=ALU.subtract)))(),
                        waits=([("pe", bpe)] if ic == 0 else []))
                    P.add("vector", (lambda od=od: (lambda e: e.tensor_tensor(
                        out=xb[:, od:od + 512], in0=xsq[0][:, 0:512],
                        in1=ps[3][:, 0:512], op=ALU.mult)))(),
                        inc=(("vd", 1) if ic == 7 else None))
                P.bump("vd")
        xb_vd = P.val("vd")

        for half in range(2):
            off = half * 1024
            pb0 = P.val("pe")
            sd0 = P.val("sd")
            for ic in range(8):
                w = [("vd", xb_vd)] if ic == 0 else []
                if ic >= 2:
                    w.append(("pe", pb0 + ic - 1))
                P.add("scalar", (lambda ic=ic, off=off: (lambda e: e.activation(
                    out=xsq[ic % 2][:, 0:1024], in_=xb[:, ic * 2048 + off:ic * 2048 + off + 1024],
                    func=AF.Square)))(), waits=w, inc=("sd", 1))
                P.bump("sd")
                for s5 in range(2):
                    P.add("tensor", (lambda ic=ic, s5=s5: (lambda e: e.matmul(
                        out=ps[0][0:1, s5 * 512:(s5 + 1) * 512], lhsT=ones_col[:, 0:1],
                        rhs=xsq[ic % 2][:, s5 * 512:(s5 + 1) * 512],
                        start=(ic == 0), stop=(ic == 7))))(),
                        waits=([("sd", sd0 + ic + 1)] if s5 == 0 else []),
                        inc=(("pe", 1) if s5 == 1 else None))
                P.bump("pe")
            pe_done = P.val("pe")
            P.add("vector", (lambda off=off: (lambda e: e.tensor_scalar(
                out=srow[:, 0:1024], in0=ps[0][0:1, 0:1024], scalar1=1.0 / D,
                scalar2=None, op0=ALU.mult)))(), waits=[("pe", pe_done)], inc=("vd", 1))
            P.bump("vd")
            P.add("scalar", lambda e: e.activation(
                out=srow[:, 0:1024], in_=srow[:, 0:1024], func=AF.Sqrt,
                bias=ep2_col[0:1, 0:1]), waits=[("vd", P.val("vd"))], inc=("sd", 1))
            P.bump("sd")
            P.add("vector", (lambda off=off: (lambda e: e.reciprocal(
                out=rms_row[:, off:off + 1024], in_=srow[:, 0:1024])))(),
                waits=[("sd", P.val("sd"))], inc=("vd", 1))
            P.bump("vd")
        rms_vd = P.val("vd")
        for t in range(16):
            P.add("tensor", (lambda t=t: (lambda e: e.matmul(
                out=ps[1][:, t:t + 1], lhsT=rms_row[:, t * 128:(t + 1) * 128],
                rhs=ones_row[0:1, 0:1], start=True, stop=True)))(),
                waits=([("vd", rms_vd)] if t == 0 else []),
                inc=(("pe", 1) if t == 15 else None))
        P.bump("pe")
        P.add("vector", lambda e: e.tensor_copy(rmsc[:], ps[1][:, 0:16]),
              waits=[("pe", P.val("pe"))], inc=("vd", 1))
        P.bump("vd")

        for t in range(16):
            w0 = [("vd", P.val("vd"))]
            for ic in range(8):
                P.add("tensor", (lambda t=t, ic=ic: (lambda e: e.matmul(
                    out=ps[2][:, (t % 2) * 8:(t % 2) * 8 + 8],
                    lhsT=xb[:, ic * 2048 + t * 128:ic * 2048 + (t + 1) * 128],
                    rhs=rwtb[:, ic * 8:(ic + 1) * 8],
                    start=(ic == 0), stop=(ic == 7))))(),
                    waits=(w0 if ic == 0 else []) + ([("vd", P.cnt[f"rt_{t - 2}"])] if ic == 0 and t >= 2 else []),
                    inc=(("pe", 1) if ic == 7 else None))
                w0 = []
            pen = P.bump("pe")
            P.add("vector", (lambda t=t: (lambda e: e.tensor_scalar(
                out=lgt[:, t * 8:(t + 1) * 8], in0=ps[2][:, (t % 2) * 8:(t % 2) * 8 + 8],
                scalar1=rmsc[:, t:t + 1], scalar2=None, op0=ALU.mult)))(),
                waits=[("pe", pen)], inc=("vd", 1))
            P.cnt[f"rt_{t}"] = P.bump("vd")
        for t in range(16):
            P.add("vector", (lambda t=t: (lambda e: e.max(
                out=mx8[:], in_=lgt[:, t * 8:(t + 1) * 8])))())
            P.add("vector", (lambda t=t: (lambda e: e.max_index(
                out=mi8[:], in_max=mx8[:], in_values=lgt[:, t * 8:(t + 1) * 8])))())
            P.add("vector", lambda e: e.tensor_scalar(
                out=miu[:], in0=mi8[:], scalar1=65535, scalar2=None, op0=ALU.bitwise_and))
            P.add("vector", lambda e: e.tensor_copy(i1f[:, 0:2], miu[:, 0:2]))
            P.add("vector", lambda e: e.tensor_tensor(
                out=scr8[:, 0:1], in0=mx8[:, 0:1], in1=mx8[:, 1:2], op=ALU.subtract),
                inc=("vd", 1))
            P.bump("vd")
            P.add("scalar", lambda e: e.activation(
                out=rw12[:, 0:1], in_=scr8[:, 0:1], func=AF.Sigmoid),
                waits=[("vd", P.val("vd"))], inc=("sd", 1))
            P.bump("sd")
            P.add("vector", lambda e: e.tensor_scalar(
                out=rw12[:, 1:2], in0=rw12[:, 0:1], scalar1=-1.0, scalar2=1.0,
                op0=ALU.mult, op1=ALU.add), waits=[("sd", P.val("sd"))])
            P.add("vector", lambda e: e.tensor_scalar(
                out=scr8[:, 8:16], in0=eidx[:], scalar1=i1f[:, 0:1], scalar2=None,
                op0=ALU.is_equal), waits=[("gch", 1)])
            P.add("vector", lambda e: e.tensor_scalar(
                out=scr8[:, 16:24], in0=eidx[:], scalar1=i1f[:, 1:2], scalar2=None,
                op0=ALU.is_equal))
            P.add("vector", (lambda t=t: (lambda e: e.tensor_tensor(
                out=selt[:, t * 8:(t + 1) * 8], in0=scr8[:, 8:16], in1=scr8[:, 16:24],
                op=ALU.add)))())
            P.add("vector", lambda e: e.tensor_scalar(
                out=scr8[:, 0:8], in0=scr8[:, 8:16], scalar1=rw12[:, 0:1], scalar2=None,
                op0=ALU.mult))
            P.add("vector", (lambda t=t: (lambda e: e.scalar_tensor_tensor(
                out=wdns[:, t * 8:(t + 1) * 8], in0=scr8[:, 16:24],
                scalar=rw12[:, 1:2], in1=scr8[:, 0:8], op0=ALU.mult, op1=ALU.add)))(),
                inc=("vd", 1))
            P.bump("vd")
        sel_vd = P.val("vd")
        for t in range(16):
            P.add("tensor", (lambda t=t: (lambda e: e.matmul(
                out=ps[3][0:1, 0:8], lhsT=ones_col[:, 0:1],
                rhs=selt[:, t * 8:(t + 1) * 8],
                start=(t == 0), stop=(t == 15))))(),
                waits=([("vd", sel_vd)] if t == 0 else []),
                inc=(("pe", 1) if t == 15 else None))
        P.bump("pe")
        P.add("vector", lambda e: e.tensor_copy(trow[:], ps[3][0:1, 0:8]),
              waits=[("pe", P.val("pe"))], inc=("vd", 1))
        tot_vd = P.bump("vd")
        P.add("sync", lambda e: e.dma_start(out=tot_bounce, in_=trow[:]),
              waits=[("vd", tot_vd)], inc=("dq", 16))
        P.bump("dq")
        P.add("gpsimd", lambda e: e.collective_compute(
            "AllGather", ALU.bypass, replica_groups=[list(range(NCORES))],
            ins=[tot_bounce.opt()], outs=[tot_gath.opt()]),
            waits=[("dq", P.val("dq") * 16)], inc=("cc", 1))
        P.add("sync", lambda e: e.dma_start(out=tg8[:], in_=tot_gath),
              waits=[("cc", 1)], inc=("dq", 16))
        tg_ld = P.bump("dq") * 16
        P.add("tensor", lambda e: e.matmul(
            out=ps[3][0:1, 8:16], lhsT=tcol[:], rhs=tg8[:], start=True, stop=True),
            waits=[("dq", tg_ld)], inc=("pe", 1))
        P.bump("pe")
        P.add("vector", lambda e: e.tensor_copy(brow[:], ps[3][0:1, 8:16]),
              waits=[("pe", P.val("pe"))], inc=("vd", 1))
        P.bump("vd")

        for t in range(16):
            P.add("vector", lambda e: e.tensor_tensor(
                out=crow[:], in0=brow[:], in1=orow[:], op=ALU.add),
                waits=[("vd", P.val("vd"))], inc=("vd", 1))
            P.bump("vd")
            cvd = P.val("vd")
            P.add("tensor", (lambda t=t: (lambda e: e.matmul(
                out=ps[2][:, (t % 2) * 8:(t % 2) * 8 + 8], lhsT=tril[:],
                rhs=selt[:, t * 8:(t + 1) * 8], start=True, stop=False)))(),
                waits=[("vd", cvd)] + ([("vd", P.cnt[f"wef_{t - 2}"])] if t >= 2 else []),
                inc=None)
            P.add("tensor", (lambda t=t: (lambda e: e.matmul(
                out=ps[2][:, (t % 2) * 8:(t % 2) * 8 + 8], lhsT=ones_row[0:1, :],
                rhs=crow[:], start=False, stop=True)))(), inc=("pe", 1))
            pen = P.bump("pe")
            P.add("vector", (lambda t=t: (lambda e: e.tensor_scalar(
                out=scr8[:, 0:8], in0=ps[2][:, (t % 2) * 8:(t % 2) * 8 + 8],
                scalar1=2560.0, scalar2=None, op0=ALU.is_lt)))(),
                waits=[("pe", pen)])
            P.add("vector", (lambda t=t: (lambda e: e.tensor_tensor(
                out=wcols.rearrange("p (e u) -> p e u", u=16)[:, :, t],
                in0=scr8[:, 0:8], in1=wdns[:, t * 8:(t + 1) * 8], op=ALU.mult)))(),
                inc=("vd", 1))
            P.cnt[f"wef_{t}"] = P.bump("vd")
            P.add("tensor", (lambda t=t: (lambda e: e.matmul(
                out=ps[3][0:1, 16:24], lhsT=ones_col[:, 0:1],
                rhs=selt[:, t * 8:(t + 1) * 8], start=True, stop=True)))(),
                inc=("pe", 1))
            pen2 = P.bump("pe")
            P.add("vector", lambda e: e.tensor_tensor(
                out=orow[:], in0=orow[:], in1=ps[3][0:1, 16:24], op=ALU.add),
                waits=[("pe", pen2)], inc=("vd", 1))
            P.bump("vd")
        P.add("tensor", lambda e: e.matmul(
            out=ps[0][:, 0:128], lhsT=wcols[:], rhs=idn[:], start=True, stop=True),
            waits=[("vd", P.val("vd")), ("sd", P.val("sd"))], inc=("pe", 1))
        P.bump("pe")
        P.add("vector", lambda e: e.tensor_copy(xsq[0][:, 0:128], ps[0][:, 0:128]),
              waits=[("pe", P.val("pe"))], inc=("vd", 1))
        P.bump("vd")
        P.add("sync", lambda e: e.dma_start(
            out=wtd.rearrange("a (p f) -> (a p) f", p=128), in_=xsq[0][:, 0:128]),
            waits=[("vd", P.val("vd"))], inc=("dq", 16))
        wtd_ld = P.bump("dq") * 16

        def wdma(dram_slc, dst, nic, wkey):
            w = [("pe", P.cnt.get(wkey, 0))]
            P.add("sync", (lambda dram_slc=dram_slc, dst=dst, nic=nic: (lambda e: e.dma_start(
                out=dst[:, 0:nic * 128].rearrange("q (c o) -> q c o", c=nic),
                in_=dram_slc)))(), waits=w, inc=("dq", 16))
            return P.bump("dq") * 16

        for ei in range(8):
            for wh in range(2):
                P.add("sync", (lambda ei=ei, wh=wh: (lambda e: e.dma_start(
                    out=wrow[:], in_=wtd[:, ei * 2048 + wh * 1024:ei * 2048 + (wh + 1) * 1024])))(),
                    waits=[("dq", wtd_ld), ("pe", P.cnt.get(f"wbtmm_{ei}_{wh - 1}", 0))],
                    inc=("dq", 16))
                wr_ld = P.bump("dq") * 16
                for q2 in range(2):
                    q4 = wh * 2 + q2
                    P.add("tensor", (lambda q2=q2: (lambda e: e.matmul(
                        out=ps[2][:, 0:512], lhsT=ones_row[0:1, :],
                        rhs=wrow[:, q2 * 512:(q2 + 1) * 512], start=True, stop=True)))(),
                        waits=[("dq", wr_ld), ("vd", P.val("vd"))],
                        inc=("pe", 1))
                    pen = P.bump("pe")
                    P.cnt[f"wbtmm_{ei}_{wh}"] = pen
                    P.add("vector", (lambda q4=q4: (lambda e: e.tensor_copy(
                        wbt[:, q4 * 512:(q4 + 1) * 512], ps[2][:, 0:512])))(),
                        waits=[("pe", pen)], inc=("vd", 1))
                    P.cnt[f"wbt_{ei}"] = P.bump("vd")
            for th in range(2):
                toff = th * 1024
                for oc in range(32):
                    la = wdma(rg_d[ei, oc], wra, 8, "wra_pe")
                    lb = wdma(ru_d[ei, oc], wrb, 8, "wrb_pe")
                    P.add("vector", lambda e: e.tensor_copy(wca[:], wra[:]),
                          waits=[("dq", la)], inc=("vd", 1))
                    cva = P.bump("vd")
                    P.add("scalar", lambda e: e.copy(wcb_[:], wrb[:]),
                          waits=[("dq", lb)], inc=("sd", 1))
                    cvb = P.bump("sd")
                    psA = ps[oc % 2]
                    psB = ps[2 + oc % 2]
                    mmw = [("vd", cva), ("sd", cvb), ("vd", xb_vd)]
                    if f"hsd_{ei}_{th}_{oc - 2}" in P.cnt:
                        mmw.append(("sd", P.cnt[f"hsd_{ei}_{th}_{oc - 2}"]))
                    if f"hvd_{ei}_{th}_{oc - 2}" in P.cnt:
                        mmw.append(("vd", P.cnt[f"hvd_{ei}_{th}_{oc - 2}"]))
                    if oc == 0:
                        mmw += [("vd", P.val("vd")), ("sd", P.val("sd"))]
                    for s5 in range(2):
                        for ic in range(8):
                            P.add("tensor", (lambda oc=oc, s5=s5, ic=ic, toff=toff, psA=psA: (lambda e: e.matmul(
                                out=psA[:, s5 * 512:(s5 + 1) * 512],
                                lhsT=wca[:, ic * 128:(ic + 1) * 128],
                                rhs=xb[:, ic * 2048 + toff + s5 * 512:ic * 2048 + toff + (s5 + 1) * 512],
                                start=(ic == 0), stop=(ic == 7))))(),
                                waits=(mmw if (s5 == 0 and ic == 0) else []))
                            mmw = []
                    for s5 in range(2):
                        for ic in range(8):
                            P.add("tensor", (lambda oc=oc, s5=s5, ic=ic, toff=toff, psB=psB: (lambda e: e.matmul(
                                out=psB[:, s5 * 512:(s5 + 1) * 512],
                                lhsT=wcb_[:, ic * 128:(ic + 1) * 128],
                                rhs=xb[:, ic * 2048 + toff + s5 * 512:ic * 2048 + toff + (s5 + 1) * 512],
                                start=(ic == 0), stop=(ic == 7))))(),
                                inc=(("pe", 1) if (s5 == 1 and ic == 7) else None))
                    pen = P.bump("pe")
                    P.cnt["wra_pe"] = pen
                    P.cnt["wrb_pe"] = pen
                    P.add("scalar", (lambda oc=oc, psA=psA: (lambda e: e.activation(
                        out=gt2[oc % 2][:, 0:1024], in_=psA[:, 0:1024],
                        func=AF.Silu)))(), waits=[("pe", pen)], inc=("sd", 1))
                    P.cnt[f"hsd_{ei}_{th}_{oc}"] = P.bump("sd")
                    P.add("vector", (lambda oc=oc, psB=psB: (lambda e: e.scalar_tensor_tensor(
                        out=hh[:, oc * 1024:(oc + 1) * 1024], in0=psB[:, 0:1024],
                        scalar=1.0, in1=gt2[oc % 2][:, 0:1024],
                        op0=ALU.mult, op1=ALU.mult)))(),
                        waits=[("sd", P.val("sd"))], inc=("vd", 1))
                    P.cnt[f"hvd_{ei}_{th}_{oc}"] = P.bump("vd")
                for oc2 in range(8):
                    for hf in range(2):
                        lc = wdma(rd_d[ei, oc2][:, hf * 16:(hf + 1) * 16, :], wrd, 16, "wrd_pe")
                        P.add("vector", lambda e: e.tensor_copy(wcd[:], wrd[:]),
                              waits=[("dq", lc)], inc=("vd", 1))
                        cvc = P.bump("vd")
                        mmw = [("vd", cvc), ("vd", P.cnt[f"hvd_{ei}_{th}_31"])]
                        if oc2 == 0 and hf == 0:
                            mmw.append(("sd", P.val("sd")))
                        if f"dacc_{ei}_{th}_{oc2 - 1}" in P.cnt:
                            mmw.append(("vd", P.cnt[f"dacc_{ei}_{th}_{oc2 - 1}"]))
                        for s5 in range(2):
                            for ic in range(16):
                                icg = hf * 16 + ic
                                P.add("tensor", (lambda oc2=oc2, s5=s5, ic=ic, icg=icg: (lambda e: e.matmul(
                                    out=ps[2 + oc2 % 2][:, s5 * 512:(s5 + 1) * 512],
                                    lhsT=wcd[:, ic * 128:(ic + 1) * 128],
                                    rhs=hh[:, icg * 1024 + s5 * 512:icg * 1024 + (s5 + 1) * 512],
                                    start=(icg == 0), stop=(icg == 31))))(),
                                    waits=(mmw if (s5 == 0 and ic == 0) else []),
                                    inc=(("pe", 1) if (s5 == 1 and ic == 15) else None))
                                mmw = []
                        P.cnt["wrd_pe"] = P.bump("pe")
                    pen = P.val("pe")
                    for s5 in range(2):
                        P.add("vector", (lambda oc2=oc2, s5=s5, toff=toff: (lambda e: e.tensor_tensor(
                            out=wscr[:, 0:512], in0=ps[2 + oc2 % 2][:, s5 * 512:(s5 + 1) * 512],
                            in1=wbt[:, toff + s5 * 512:toff + (s5 + 1) * 512], op=ALU.mult)))(),
                            waits=[("pe", pen)] if s5 == 0 else [])
                        P.add("vector", (lambda oc2=oc2, s5=s5, toff=toff: (lambda e: e.tensor_tensor(
                            out=racc[:, oc2 * 2048 + toff + s5 * 512:oc2 * 2048 + toff + (s5 + 1) * 512],
                            in0=racc[:, oc2 * 2048 + toff + s5 * 512:oc2 * 2048 + toff + (s5 + 1) * 512],
                            in1=wscr[:, 0:512], op=ALU.add)))(),
                            inc=(("vd", 1) if s5 == 1 else None))
                    P.cnt[f"dacc_{ei}_{th}_{oc2}"] = P.bump("vd")

        for th in range(2):
            toff = th * 1024
            for q in range(4):
                for oc in range(16):
                    ocg = q * 16 + oc
                    la = wdma(sg_d[ocg], wra, 8, "wra_pe")
                    lb = wdma(su_d[ocg], wrb, 8, "wrb_pe")
                    P.add("vector", lambda e: e.tensor_copy(wca[:], wra[:]),
                          waits=[("dq", la)], inc=("vd", 1))
                    cva = P.bump("vd")
                    P.add("scalar", lambda e: e.copy(wcb_[:], wrb[:]),
                          waits=[("dq", lb)], inc=("sd", 1))
                    cvb = P.bump("sd")
                    psA = ps[oc % 2]
                    psB = ps[2 + oc % 2]
                    mmw = [("vd", cva), ("sd", cvb)]
                    if f"shsd_{th}_{q}_{oc - 2}" in P.cnt:
                        mmw.append(("sd", P.cnt[f"shsd_{th}_{q}_{oc - 2}"]))
                    if f"shvd_{th}_{q}_{oc - 2}" in P.cnt:
                        mmw.append(("vd", P.cnt[f"shvd_{th}_{q}_{oc - 2}"]))
                    if oc == 0:
                        mmw += [("vd", P.val("vd")), ("sd", P.val("sd"))]
                    for s5 in range(2):
                        for ic in range(8):
                            P.add("tensor", (lambda s5=s5, ic=ic, toff=toff, psA=psA: (lambda e: e.matmul(
                                out=psA[:, s5 * 512:(s5 + 1) * 512],
                                lhsT=wca[:, ic * 128:(ic + 1) * 128],
                                rhs=xb[:, ic * 2048 + toff + s5 * 512:ic * 2048 + toff + (s5 + 1) * 512],
                                start=(ic == 0), stop=(ic == 7))))(),
                                waits=(mmw if (s5 == 0 and ic == 0) else []))
                            mmw = []
                    for s5 in range(2):
                        for ic in range(8):
                            P.add("tensor", (lambda s5=s5, ic=ic, toff=toff, psB=psB: (lambda e: e.matmul(
                                out=psB[:, s5 * 512:(s5 + 1) * 512],
                                lhsT=wcb_[:, ic * 128:(ic + 1) * 128],
                                rhs=xb[:, ic * 2048 + toff + s5 * 512:ic * 2048 + toff + (s5 + 1) * 512],
                                start=(ic == 0), stop=(ic == 7))))(),
                                inc=(("pe", 1) if (s5 == 1 and ic == 7) else None))
                    pen = P.bump("pe")
                    P.cnt["wra_pe"] = pen
                    P.cnt["wrb_pe"] = pen
                    P.add("scalar", (lambda oc=oc, psA=psA: (lambda e: e.activation(
                        out=gt2[oc % 2][:, 0:1024], in_=psA[:, 0:1024],
                        func=AF.Silu)))(), waits=[("pe", pen)], inc=("sd", 1))
                    P.cnt[f"shsd_{th}_{q}_{oc}"] = P.bump("sd")
                    P.add("vector", (lambda oc=oc, psB=psB: (lambda e: e.scalar_tensor_tensor(
                        out=hh[:, oc * 1024:(oc + 1) * 1024], in0=psB[:, 0:1024],
                        scalar=1.0, in1=gt2[oc % 2][:, 0:1024],
                        op0=ALU.mult, op1=ALU.mult)))(),
                        waits=[("sd", P.val("sd"))], inc=("vd", 1))
                    P.cnt[f"shvd_{th}_{q}_{oc}"] = P.bump("vd")
                for oc2 in range(8):
                    lc = wdma(sd_d[oc2][:, q * 16:(q + 1) * 16, :], wrd, 16, "wrd_pe")
                    P.add("vector", lambda e: e.tensor_copy(wcd[:], wrd[:]),
                          waits=[("dq", lc)], inc=("vd", 1))
                    cvc = P.bump("vd")
                    mmw = [("vd", cvc), ("vd", P.cnt[f"shvd_{th}_{q}_15"])]
                    if f"sacc_{th}_{q}_{oc2 - 1}" in P.cnt:
                        mmw.append(("vd", P.cnt[f"sacc_{th}_{q}_{oc2 - 1}"]))
                    if oc2 == 0:
                        mmw.append(("sd", P.val("sd")))
                    for s5 in range(2):
                        for ic in range(16):
                            P.add("tensor", (lambda oc2=oc2, s5=s5, ic=ic: (lambda e: e.matmul(
                                out=ps[2 + oc2 % 2][:, s5 * 512:(s5 + 1) * 512],
                                lhsT=wcd[:, ic * 128:(ic + 1) * 128],
                                rhs=hh[:, ic * 1024 + s5 * 512:ic * 1024 + (s5 + 1) * 512],
                                start=(ic == 0), stop=(ic == 15))))(),
                                waits=(mmw if (s5 == 0 and ic == 0) else []),
                                inc=(("pe", 1) if (s5 == 1 and ic == 15) else None))
                            mmw = []
                    pen = P.bump("pe")
                    P.cnt["wrd_pe"] = pen
                    for s5 in range(2):
                        P.add("vector", (lambda oc2=oc2, s5=s5, toff=toff: (lambda e: e.tensor_tensor(
                            out=racc[:, oc2 * 2048 + toff + s5 * 512:oc2 * 2048 + toff + (s5 + 1) * 512],
                            in0=racc[:, oc2 * 2048 + toff + s5 * 512:oc2 * 2048 + toff + (s5 + 1) * 512],
                            in1=ps[2 + oc2 % 2][:, s5 * 512:(s5 + 1) * 512], op=ALU.add)))(),
                            waits=[("pe", pen)] if s5 == 0 else [],
                            inc=(("vd", 1) if s5 == 1 else None))
                    P.cnt[f"sacc_{th}_{q}_{oc2}"] = P.bump("vd")

        for half in range(2):
            off = half * 1024
            P.add("sync", (lambda half=half: (lambda e: e.dma_start(
                out=xf3.rearrange("q (c t) -> q c t", c=8),
                in_=xin[:, :, half * 1024:(half + 1) * 1024])))(),
                waits=[("vd", P.val("vd")), ("dq", P.val("dq") * 16)], inc=("dq", 16))
            xldv = P.bump("dq") * 16
            emit_ln_stats(P, racc, 8, 1024, ones_col, xsq, mrow, rrow, srow,
                          ps[0][0:1, :], ps[1][0:1, :], eps_col,
                          [("sd", P.val("sd")), ("vd", P.val("vd"))],
                          CS=2048, off=off)
            for q5 in range(2):
                emit_bcast_row(P, mrow[:, q5 * 512:(q5 + 1) * 512], ps[2][:, 0:512],
                               512, ones_row, [("vd", P.val("vd"))])
                emit_bcast_row(P, rrow[:, q5 * 512:(q5 + 1) * 512], ps[3][:, 0:512],
                               512, ones_row, [])
                bpe = P.val("pe")
                for ic in range(8):
                    od = ic * 2048 + off + q5 * 512
                    o2 = ic * 1024 + q5 * 512
                    P.add("vector", (lambda od=od: (lambda e: e.tensor_tensor(
                        out=xsq[0][:, 0:512], in0=racc[:, od:od + 512],
                        in1=ps[2][:, 0:512], op=ALU.subtract)))(),
                        waits=([("pe", bpe), ("dq", xldv)] if ic == 0 else []))
                    P.add("vector", (lambda o2=o2: (lambda e: e.tensor_tensor(
                        out=xsq[1][:, 0:512], in0=xsq[0][:, 0:512],
                        in1=ps[3][:, 0:512], op=ALU.mult)))())
                    P.add("vector", (lambda od=od, o2=o2: (lambda e: e.tensor_tensor(
                        out=racc[:, od:od + 512], in0=xsq[1][:, 0:512],
                        in1=xf3[:, o2:o2 + 512], op=ALU.add)))(),
                        inc=(("vd", 1) if ic == 7 else None))
                P.bump("vd")
            vout = P.val("vd")
            P.add("sync", (lambda half=half: (lambda e: e.dma_start(
                out=xout[:, :, half * 1024:(half + 1) * 1024],
                in_=racc.rearrange("q (c t) -> q c t", c=8)[:, :, half * 1024:(half + 1) * 1024])))(),
                waits=[("vd", vout)], inc=("dq", 16))
            P.bump("dq")

        P.add("sync", None, waits=[("dq", P.val("dq") * 16)])
        P.run_all(block, sems)
    return nc


def _prep_l1ffn_inputs(inputs):
    com = {}
    com["rg"] = np.stack([_chunk_w(np.ascontiguousarray(inputs["L1_ffn_rg"][e])) for e in range(8)])
    com["ru"] = np.stack([_chunk_w(np.ascontiguousarray(inputs["L1_ffn_ru"][e])) for e in range(8)])
    com["rd"] = np.stack([_chunk_w(np.ascontiguousarray(inputs["L1_ffn_rd"][e])) for e in range(8)])
    com["sg"] = _chunk_w(np.ascontiguousarray(inputs["L1_ffn_sg"]))
    com["su"] = _chunk_w(np.ascontiguousarray(inputs["L1_ffn_su"]))
    com["sd"] = _chunk_w(np.ascontiguousarray(inputs["L1_ffn_sd"]))
    rwt = np.ascontiguousarray(inputs["L1_ffn_router_w"].T)
    com["rwt"] = np.ascontiguousarray(rwt.reshape(8, 128, 8).transpose(1, 0, 2))
    com["tril"] = np.triu(np.ones((128, 128), np.float32), 1)
    com["idn"] = np.eye(128, dtype=np.float32)
    return com


def launch_l1ffn(x3, inputs):
    nc = build_l1ffn()
    com = _prep_l1ffn_inputs(inputs)
    xfm = _feat_major(x3.reshape(T_ALL, D))
    in_maps = []
    for c in range(NCORES):
        m = dict(com)
        m["xin"] = np.ascontiguousarray(xfm[:, :, c * T_CORE:(c + 1) * T_CORE])
        tc = np.zeros((8, 1), np.float32)
        tc[:c, 0] = 1.0
        m["tcol"] = tc
        in_maps.append(m)
    res = run_bass_kernel_spmd(nc, in_maps, core_ids=list(range(NCORES)))
    out = np.concatenate([r["xout"] for r in res.results], axis=2)
    return _unfeat_major(out).reshape(B, S, D)



CSC = 64
NCH = S // CSC


def _finish(P, block, sems, xout, xres_d):
    P.add("sync", lambda e: e.dma_start(out=xout, in_=xres_d),
          waits=[("vd", P.val("vd")), ("pe", P.val("pe")),
                 ("sd", P.val("sd"))], inc=("dq", 16))
    P.bump("dq")
    P.add("sync", None, waits=[("dq", P.val("dq") * 16)])
    P.run_all(block, sems)


def build_l1attn(nch=None, tail=True, phases="full"):
    nc = bass.Bass("TRN2", target_bir_lowering=False, debug=False)
    nch = NCH if nch is None else nch
    xb = nc.dram_tensor("xb", [128, 8, S], F32, kind="ExternalInput").ap()
    xout = nc.dram_tensor("xout", [128, 8, T_CORE], F32, kind="ExternalOutput").ap()
    wq_d = nc.dram_tensor("wq", [2, 128, 8, 128], F32, kind="ExternalInput").ap()
    wk_d = nc.dram_tensor("wk", [2, 128, 8, 128], F32, kind="ExternalInput").ap()
    wv_d = nc.dram_tensor("wv", [2, 128, 8, 128], F32, kind="ExternalInput").ap()
    wg_d = nc.dram_tensor("wg", [2, 128, 8, 128], F32, kind="ExternalInput").ap()
    wo_d = nc.dram_tensor("wo", [128, 2, 8, 128], F32, kind="ExternalInput").ap()
    xres_d = nc.dram_tensor("xres", [128, 8, 2048], F32, kind="ExternalInput").ap()
    bq_d = nc.dram_tensor("bq", [128, 2], F32, kind="ExternalInput").ap()
    bk_d = nc.dram_tensor("bk", [128, 2], F32, kind="ExternalInput").ap()
    bv_d = nc.dram_tensor("bv", [128, 2], F32, kind="ExternalInput").ap()
    bg_d = nc.dram_tensor("bg", [128, 2], F32, kind="ExternalInput").ap()
    bo_d = nc.dram_tensor("bo", [128, 8], F32, kind="ExternalInput").ap()
    wsm_d = nc.dram_tensor("wsm", [128, 8, 20], F32, kind="ExternalInput").ap()
    sb1_d = nc.dram_tensor("sb1", [20, 1], F32, kind="ExternalInput").ap()
    sb2_d = nc.dram_tensor("sb2", [20, 1], F32, kind="ExternalInput").ap()
    onrm_d = nc.dram_tensor("onrm", [128, 1], F32, kind="ExternalInput").ap()
    rf_d = nc.dram_tensor("rf", [1, 1], F32, kind="ExternalInput").ap()
    idnb_d = nc.dram_tensor("idnb", [128, 128], BF16, kind="ExternalInput").ap()
    idnf_d = nc.dram_tensor("idnf", [128, 128], F32, kind="ExternalInput").ap()
    bones_d = nc.dram_tensor("bones", [2, 128], F32, kind="ExternalInput").ap()
    xnd = nc.dram_tensor("xnd", [128, 8, S], BF16, kind="Internal").ap()
    qd = nc.dram_tensor("qd", [2, 128, S], BF16, kind="Internal").ap()
    kd = nc.dram_tensor("kd", [2, 128, S], BF16, kind="Internal").ap()
    gd = nc.dram_tensor("gd", [4, 64, S], BF16, kind="Internal").ap()
    kdTb = nc.dram_tensor("kdTb", [2, 2, S, 128], BF16, kind="Internal").ap()
    vdT = nc.dram_tensor("vdT", [4, S, 64], BF16, kind="Internal").ap()
    srd2 = nc.dram_tensor("srd2", [2, 2, 5, S], F32, kind="Internal").ap()
    od1 = nc.dram_tensor("od1", [2, 2 * 64 * CSC], BF16, kind="Internal").ap()
    ypd = nc.dram_tensor("ypd", [4, 128, 8, 2048], F32, kind="Internal").ap()
    yprd = nc.dram_tensor("yprd", [128, 8, 2048], F32, kind="Internal").ap()

    with ExitStack() as ctx:
        sb = lambda n, shp, dt=F32: ctx.enter_context(nc.sbuf_tensor(n, shp, dt))
        bigA = sb("bigA", [128, 16384])
        xblk = bigA
        xnstg = sb("xnstg", [128, 8 * 2048], BF16)
        oTf = [sb(f"oTf{g}", [128, S], BF16) for g in range(2)]
        kqDc = [sb(f"kqD{g}", [128, 4 * CSC], BF16) for g in range(2)]
        fsq = [sb(f"fsq{g}", [2, 8 * CSC]) for g in range(2)]
        UW = [sb(f"UW{g}", [128, 128], BF16) for g in range(2)]
        rvSB = [sb(f"rvSB{g}", [128, 4]) for g in range(2)]
        gbo = [sb(f"gbo{g}", [2, 128], BF16) for g in range(2)]
        g1s = [sb(f"g1s{g}", [2, 64]) for g in range(2)]
        kgSB = [sb(f"kgSB{g}", [128, 128], BF16) for g in range(2)]
        upds = [sb(f"upds{g}", [128, 128]) for g in range(2)]
        nfns = [sb(f"nfns{g}", [128, 2]) for g in range(2)]
        sqj = [sb(f"sqj{g}", [128, 64], BF16) for g in range(2)]
        nrm2 = [sb(f"nrm2{g}", [2, 2]) for g in range(2)]
        inv2 = [sb(f"inv2{g}", [2, 2]) for g in range(2)]
        m1s = [sb(f"m1s{g}", [2, 2]) for g in range(2)]
        scol = [sb(f"scol{g}", [2, 8]) for g in range(2)]
        ggs = [sb(f"ggs{g}", [2, 2]) for g in range(2)]
        ccols = [sb(f"ccol{g}", [128, 4]) for g in range(2)]
        tfs = [sb(f"tfs{g}", [2, 128]) for g in range(2)]
        stg = sb("stg", [128, 128])
        stg2 = sb("stg2", [128, 128])
        stgb = sb("stgb", [128, 512], BF16)
        stgb2 = sb("stgb2", [128, 128], BF16)
        stgb3 = sb("stgb3", [128, 128], BF16)
        wcf = sb("wcf", [128, 8 * 128])
        wcb = sb("wcb", [128, 8 * 128], BF16)
        srt = sb("srt", [20, 512])
        br0 = sb("br0", [1, 512])
        br1 = sb("br1", [1, 512])
        bcl = sb("bcl", [128, 2])
        nrcol = sb("nrcol", [128, 2])
        bcols = sb("bcols", [128, 8])
        sbias = sb("sbias", [20, 1])
        sbias2 = sb("sbias2", [20, 1])
        onrm2 = sb("onrm2", [128, 1])
        rf1 = sb("rf1", [1, 1])
        rfc2 = sb("rfc2", [2, 1])
        rf1s = sb("rf1s", [128, 1])
        idnb = sb("idnb_sb", [128, 128], BF16)
        ones_col = sb("ones_col", [128, 1])
        ones_row = sb("ones_row", [1, 128])
        onesd2 = sb("onesd2", [128, 2])
        eps_col = sb("eps_col", [128, 1])
        ep6_col = sb("ep6_col", [128, 1])
        bo_sb = sb("bo_sb", [128, 8])
        mrow = bigA[0:1, 8192:9216]
        rrow = bigA[0:1, 9216:10240]
        srow = bigA[0:1, 10240:11264]
        xsq = [bigA[:, 11264:12288], bigA[:, 12288:13312]]

        pps = [ctx.enter_context(nc.psum_tensor(f"pp{i}", [128, 512], F32)) for i in range(2)]
        pgr = [ctx.enter_context(nc.psum_tensor(f"bkA{i}", [128, 512], F32)) for i in range(2)]
        bkB = [ctx.enter_context(nc.psum_tensor(f"bkB{i}", [128, 512], F32)) for i in range(2)]
        bkC = [ctx.enter_context(nc.psum_tensor(f"bkC{i}", [128, 512], F32)) for i in range(2)]
        psc = bkC[0]
        sems = {}
        for s_ in ["dq", "pe", "sd", "vd", "vch", "sch", "gch", "cc", "gq"]:
            sems[s_] = ctx.enter_context(nc.semaphore("sem_" + s_))
        block = ctx.enter_context(nc.Block())
        P = Plan()

        P.add("vector", lambda e: e.memset(ones_col[:], 1.0))
        P.add("vector", lambda e: e.memset(ones_row[:], 1.0))
        P.add("vector", lambda e: e.memset(eps_col[:], 1e-5))
        P.add("vector", lambda e: e.memset(ep6_col[:], 1e-6))
        P.add("vector", lambda e: e.memset(onesd2[:], 0.0))
        P.add("vector", lambda e: e.memset(onesd2[0:64, 0:1], 1.0))
        P.add("vector", lambda e: e.memset(onesd2[64:128, 1:2], 1.0),
              inc=("vd", 1))
        P.bump("vd")
        for a, d in [(bq_d, bcols[:, 0:2]), (bk_d, bcols[:, 2:4]), (bv_d, bcols[:, 4:6]),
                     (bg_d, bcols[:, 6:8]), (sb1_d, sbias[:]), (sb2_d, sbias2[:]),
                     (onrm_d, onrm2[:]), (rf_d, rf1[:]), (idnb_d, idnb[:]),
                     (bo_d, bo_sb[:])]:
            P.add("sync", (lambda a=a, d=d: (lambda e: e.dma_start(out=d, in_=a)))(),
                  inc=("dq", 16))
            P.bump("dq")
        P.add("sync", lambda e: e.dma_start(out=rfc2[0:1, :], in_=rf_d), inc=("dq", 16))
        P.bump("dq")
        P.add("sync", lambda e: e.dma_start(out=rfc2[1:2, :], in_=rf_d), inc=("dq", 16))
        cdq = P.bump("dq") * 16
        P.add("vector", lambda e: e.tensor_tensor(
            out=sbias[:], in0=sbias[:], in1=sbias2[:], op=ALU.add),
            waits=[("dq", cdq)], inc=("vd", 1))
        P.bump("vd")
        P.add("tensor", lambda e: e.matmul(
            out=pgr[0][:, 0:1], lhsT=ones_row[:], rhs=rf1[:],
            start=True, stop=True), waits=[("dq", cdq), ("vd", P.val("vd"))],
            inc=("pe", 1))
        P.bump("pe")
        P.add("vector", lambda e: e.tensor_scalar(
            out=rf1s[:], in0=pgr[0][:, 0:1], scalar1=1.0, scalar2=None, op0=ALU.add),
            waits=[("pe", P.val("pe"))], inc=("vd", 1))
        P.bump("vd")
        P.add("vector", lambda e: e.memset(wcb[:, 0:1024], 0.0), inc=("vd", 1))
        zvd = P.bump("vd")
        kdTbf = kdTb.rearrange("a b s c -> (a b s) c")
        for j in range(32):
            P.add("sync", (lambda j=j: (lambda e: e.dma_start(
                out=kdTbf[j * 1024:(j + 1) * 1024, :].rearrange(
                    "(n p) c -> p n c", p=128),
                in_=wcb[:, 0:1024].rearrange("p (n c) -> p n c", n=8))))(),
                waits=([("vd", zvd)] if j == 0 else []), inc=("dq", 16))
            P.bump("dq")

        if phases == "z":
            _finish(P, block, sems, xout, xres_d)
            return nc
        for tb in range(8):
            P.add("sync", (lambda tb=tb: (lambda e: e.dma_start(
                out=xblk[:, 0:8192].rearrange("q (c t) -> q c t", c=8),
                in_=xb[:, :, tb * 1024:(tb + 1) * 1024])))(),
                waits=[("vd", P.val("vd")), ("sd", P.val("sd")),
                       ("pe", P.val("pe"))],
                inc=("dq", 16))
            xldv = P.bump("dq") * 16
            for qr in range(2):
                off = qr * 512
                emit_ln_stats(P, xblk, 8, 512, ones_col, xsq, mrow, rrow, srow,
                              pps[0][0:1, 0:512], pps[0][32:33, 0:512], eps_col,
                              [("dq", xldv), ("sd", P.val("sd")), ("vd", P.val("vd"))],
                              CS=1024, off=off)
                emit_bcast_row(P, mrow[:, 0:512], pps[1][:, 0:512],
                               512, ones_row, [("vd", P.val("vd"))])
                emit_bcast_row(P, rrow[:, 0:512], psc[:, 0:512],
                               512, ones_row, [])
                bpe = P.val("pe")
                for ic in range(8):
                    o2 = ic * 1024 + off
                    P.add("vector", (lambda o2=o2: (lambda e: e.tensor_tensor(
                        out=xsq[0][:, 0:512], in0=xblk[:, o2:o2 + 512],
                        in1=pps[1][:, 0:512], op=ALU.subtract)))(),
                        waits=([("pe", bpe), ("dq", P.val("dq") * 16)] if ic == 0 else []))
                    P.add("vector", (lambda o2=o2: (lambda e: e.tensor_tensor(
                        out=xnstg[:, o2:o2 + 512], in0=xsq[0][:, 0:512],
                        in1=psc[:, 0:512], op=ALU.mult)))(),
                        inc=(("vd", 1) if ic == 7 else None))
                P.bump("vd")
            xnvd = P.val("vd")
            P.add("sync", (lambda tb=tb: (lambda e: e.dma_start(
                out=xnd[:, :, tb * 1024:(tb + 1) * 1024],
                in_=xnstg[:, 0:8192].rearrange("q (c t) -> q c t", c=8))))(),
                waits=[("vd", xnvd)], inc=("dq", 16))
            P.bump("dq")

        if phases == "a":
            _finish(P, block, sems, xout, xres_d)
            return nc
        P.add("sync", lambda e: e.dma_start(
            out=wcf[:, 0:160].rearrange("q (c o) -> q c o", c=8), in_=wsm_d),
            inc=("dq", 16))
        wsm_ld = P.bump("dq") * 16
        P.add("vector", lambda e: e.tensor_copy(wcb[:, 0:160], wcf[:, 0:160]),
              waits=[("dq", wsm_ld)], inc=("vd", 1))
        wsm_cv = P.bump("vd")
        for t5 in range(16):
            if t5 % 4 == 0:
                P.add("sync", (lambda t5=t5: (lambda e: e.dma_start(
                    out=xnstg.rearrange("q (c t) -> q c t", c=8),
                    in_=xnd[:, :, (t5 // 4) * 2048:(t5 // 4 + 1) * 2048])))(),
                    waits=[("pe", P.val("pe")), ("vd", P.val("vd"))], inc=("dq", 16))
                P.bump("dq")
            xn_ld = P.val("dq") * 16
            for ic in range(8):
                P.add("tensor", (lambda t5=t5, ic=ic: (lambda e: e.matmul(
                    out=pps[t5 % 2][0:20, 0:512], lhsT=wcb[:, ic * 20:(ic + 1) * 20],
                    rhs=xnstg[:, ic * 2048 + (t5 % 4) * 512:ic * 2048 + (t5 % 4) * 512 + 512],
                    start=(ic == 0), stop=(ic == 7))))(),
                    waits=([("dq", max(wsm_ld, xn_ld)), ("vd", wsm_cv),
                            ("sd", P.cnt.get(f"smep_{t5 - 2}", 0)),
                            ("dq", P.cnt.get(f"smdq_{t5 - 1}", 0))] if ic == 0 else []),
                    inc=(("pe", 1) if ic == 7 else None))
            pen = P.bump("pe")
            P.add("scalar", (lambda t5=t5: (lambda e: e.activation(
                out=srt[:, 0:512], in_=pps[t5 % 2][0:20, 0:512], func=AF.Sigmoid,
                bias=sbias[:])))(), waits=[("pe", pen), ("vd", P.val("vd"))], inc=("sd", 1))
            P.cnt[f"smep_{t5}"] = P.bump("sd")
            P.add("sync", (lambda t5=t5: (lambda e: e.dma_start(
                out=srd2.rearrange("h g q t -> (h g q) t")[
                    :, t5 * 512:(t5 + 1) * 512],
                in_=srt[:, 0:512])))(),
                waits=[("sd", P.val("sd"))], inc=("dq", 16))
            P.cnt[f"smdq_{t5}"] = P.bump("dq") * 16

        if phases == "sp":
            _finish(P, block, sems, xout, xres_d)
            return nc

        def emit_bigproj(wdram, bcol_off, act, fm_dst, T_dst, normalize, fold, store_T):
            for oc in range(2):
                P.add("sync", (lambda oc=oc, wdram=wdram: (lambda e: e.dma_start(
                    out=wcf.rearrange("q (c o) -> q c o", c=8), in_=wdram[oc])))(),
                    waits=[("pe", P.cnt.get("bp_pe", 0))], inc=("dq", 16))
                wld = P.bump("dq") * 16
                P.add("vector", lambda e: e.tensor_copy(wcb[:], wcf[:]),
                      waits=[("dq", wld), ("pe", P.cnt.get("bp_pe", 0))], inc=("vd", 1))
                cvd = P.bump("vd")
                for t5 in range(16):
                    if t5 % 4 == 0:
                        P.add("sync", (lambda t5=t5: (lambda e: e.dma_start(
                            out=xnstg.rearrange("q (c t) -> q c t", c=8),
                            in_=xnd[:, :, (t5 // 4) * 2048:(t5 // 4 + 1) * 2048])))(),
                            waits=[("pe", P.val("pe"))], inc=("dq", 16))
                        P.bump("dq")
                    xn_ld = P.val("dq") * 16
                    for ic in range(8):
                        P.add("tensor", (lambda t5=t5, ic=ic: (lambda e: e.matmul(
                            out=pps[t5 % 2][:, 0:512], lhsT=wcb[:, ic * 128:(ic + 1) * 128],
                            rhs=xnstg[:, ic * 2048 + (t5 % 4) * 512:ic * 2048 + (t5 % 4) * 512 + 512],
                            start=(ic == 0), stop=(ic == 7))))(),
                            waits=([("dq", xn_ld), ("vd", cvd),
                                    ("sd", P.cnt.get(f"bpsd_{t5 - 2}", 0))] if ic == 0 else []),
                            inc=(("pe", 1) if ic == 7 else None))
                    pen = P.bump("pe")
                    P.cnt["bp_pe"] = pen
                    P.add("scalar", (lambda t5=t5, act=act, oc=oc: (lambda e: e.activation(
                        out=stgb[:, 0:512], in_=pps[t5 % 2][:, 0:512], func=act,
                        bias=bcols[:, bcol_off * 2 + oc:bcol_off * 2 + oc + 1])))(),
                        waits=[("pe", pen), ("dq", P.cnt.get("bpfm_dq", 0)),
                               ("pe", P.cnt.get("bpT_pe", 0))], inc=("sd", 1))
                    P.cnt[f"bpsd_{t5}"] = P.bump("sd")
                    sde = P.val("sd")
                    if T_dst is None:
                        for h2 in range(2):
                            P.add("sync", (lambda oc=oc, h2=h2, t5=t5: (lambda e: e.dma_start(
                                out=fm_dst[oc * 2 + h2][:, t5 * 512:(t5 + 1) * 512],
                                in_=stgb[h2 * 64:(h2 + 1) * 64, 0:512])))(),
                                waits=([("sd", sde)] if h2 == 0 else []), inc=("dq", 16))
                            P.cnt["bpfm_dq"] = P.bump("dq") * 16
                        continue
                    br_ld = P.val("dq") * 16
                    for tt in range(4):
                        P.add("tensor", (lambda tt=tt, t5=t5: (lambda e: e.matmul(
                            out=pgr[0][:, 64:192], lhsT=stgb[:, tt * 128:(tt + 1) * 128],
                            rhs=idnb[:], start=True, stop=True)))(),
                            waits=[("sd", sde), ("vd", P.cnt.get("bpT_vd", 0)),
                                   ("dq", br_ld)],
                            inc=("pe", 1))
                        pT = P.bump("pe")
                        P.cnt["bp_pe"] = pT
                        P.cnt["bpT_pe"] = pT
                        if fold == "beta":
                            for h2b in range(2):
                                P.add("sync", (lambda oc=oc, t5=t5, tt=tt, h2b=h2b: (lambda e: e.dma_start(
                                    out=bcl[:, h2b:h2b + 1],
                                    in_=srd2[h2b:h2b + 1, oc, 0,
                                             t5 * 512 + tt * 128:t5 * 512 + (tt + 1) * 128
                                             ].rearrange("r (p t) -> (r p) t", t=1))))(),
                                    waits=([("vd", P.cnt.get("bpT_vd", 0))] if h2b == 0 else []),
                                    inc=("dq", 16))
                                P.bump("dq")
                            bcl_ld = P.val("dq") * 16
                        if normalize:
                            for h2 in range(2):
                                P.add("scalar", (lambda h2=h2: (lambda e: e.activation(
                                    out=sqj[0][:, 0:64], in_=pgr[0][:, 64 + h2 * 64:128 + h2 * 64],
                                    func=AF.Square, accum_out=nfns[0][:, h2:h2 + 1])))(),
                                    waits=[("pe", pT), ("vd", P.cnt.get("bpT_vd", 0))], inc=("sd", 1))
                                P.bump("sd")
                            P.add("scalar", lambda e: e.activation(
                                out=nfns[1][:, 0:2], in_=nfns[0][:, 0:2], func=AF.Sqrt,
                                bias=ep6_col[:]), inc=("sd", 1))
                            sqd = P.bump("sd")
                            P.add("vector", lambda e: e.reciprocal(
                                out=nrcol[:], in_=nfns[1][:, 0:2]),
                                waits=[("sd", sqd)], inc=("vd", 1))
                            P.bump("vd")
                            fcol = nrcol
                        else:
                            fcol = bcl if fold == "beta" else None
                        for h2 in range(2):
                            if fold == "rf1":
                                sc = rf1s[:, 0:1]
                            else:
                                sc = fcol[:, h2:h2 + 1]
                            P.add("vector", (lambda h2=h2, sc=sc: (lambda e: e.tensor_scalar(
                                out=stgb2[:, h2 * 64:(h2 + 1) * 64],
                                in0=pgr[0][:, 64 + h2 * 64:128 + h2 * 64],
                                scalar1=sc, scalar2=None, op0=ALU.mult)))(),
                                waits=[("pe", pT),
                                       ("dq", P.cnt.get("bpst_dq", 0))] +
                                      ([("sd", P.val("sd"))] if normalize else []),
                                inc=(("vd", 1) if h2 == 1 else None))
                        P.cnt["bpT_vd"] = P.bump("vd")
                        tvd = P.val("vd")
                        if store_T and fold == "beta":
                            for h2 in range(2):
                                P.add("vector", (lambda h2=h2: (lambda e: e.tensor_scalar(
                                    out=stgb3[:, h2 * 64:(h2 + 1) * 64],
                                    in0=stgb2[:, h2 * 64:(h2 + 1) * 64],
                                    scalar1=bcl[:, h2:h2 + 1], scalar2=None,
                                    op0=ALU.mult)))(),
                                    waits=([("dq", bcl_ld)] if h2 == 0 else []),
                                    inc=(("vd", 1) if h2 == 1 else None))
                            P.cnt["bpT_vd"] = P.bump("vd")
                            tvd = P.val("vd")
                        if store_T:
                            tsrc = stgb3 if fold == "beta" else stgb2
                            for h2 in range(2):
                                P.add("sync", (lambda oc=oc, h2=h2, t5=t5, tt=tt, tsrc=tsrc: (lambda e: e.dma_start(
                                    out=T_dst[oc * 2 + h2][t5 * 512 + tt * 128:t5 * 512 + (tt + 1) * 128, :],
                                    in_=tsrc[:, h2 * 64:(h2 + 1) * 64])))(),
                                    waits=([("vd", tvd)] if h2 == 0 else []), inc=("dq", 16))
                                P.cnt["bpst_dq"] = P.bump("dq") * 16
                        if fm_dst is not None:
                            P.add("tensor", (lambda: (lambda e: e.matmul(
                                out=pgr[1][:, 64:192], lhsT=stgb2[:],
                                rhs=idnb[:], start=True, stop=True)))(),
                                waits=[("vd", tvd)], inc=("pe", 1))
                            pT3 = P.bump("pe")
                            P.cnt["bp_pe"] = pT3
                            P.add("vector", lambda e: e.tensor_copy(
                                stgb2[:], pgr[1][:, 64:192]),
                                waits=[("pe", pT3)], inc=("vd", 1))
                            P.cnt["bpT_vd"] = P.bump("vd")
                            for h2 in range(2):
                                P.add("sync", (lambda oc=oc, h2=h2, t5=t5, tt=tt: (lambda e: e.dma_start(
                                    out=fm_dst[oc * 2 + h2][:, t5 * 512 + tt * 128:t5 * 512 + (tt + 1) * 128],
                                    in_=stgb2[h2 * 64:(h2 + 1) * 64, :])))(),
                                    waits=([("vd", P.val("vd"))] if h2 == 0 else []),
                                    inc=("dq", 16))
                                P.cnt["bpfm_dq"] = P.cnt["bpst_dq"] = P.bump("dq") * 16

        qdl = [qd[h // 2][(h % 2) * 64:(h % 2) * 64 + 64, :] for h in range(4)]
        kdl = [kd[h // 2][(h % 2) * 64:(h % 2) * 64 + 64, :] for h in range(4)]
        gdl = [gd[h] for h in range(4)]
        kdTl = [kdTb[h // 2][h % 2, :, (h % 2) * 64:(h % 2) * 64 + 64]
                for h in range(4)]
        vdTl = [vdT[h] for h in range(4)]
        emit_bigproj(wg_d, 3, AF.Identity, gdl, None, False, None, False)
        if phases == "p1":
            _finish(P, block, sems, xout, xres_d)
            return nc
        emit_bigproj(wq_d, 0, AF.Silu, qdl, vdTl, True, None, False)
        if phases == "p2":
            _finish(P, block, sems, xout, xres_d)
            return nc
        if phases == "p3n":
            emit_bigproj(wk_d, 1, AF.Silu, kdl, vdTl, True, None, True)
        elif phases in ("p3b", "p3b1", "p3b2"):
            emit_bigproj(wk_d, 1, AF.Silu, kdl, vdTl, True, "beta", False)
        else:
            emit_bigproj(wk_d, 1, AF.Silu, kdl,
                         vdTl if phases == "p3v" else kdTl, True, "beta", True)
        if phases in ("p3", "p3v", "p3n", "p3b", "p3b1", "p3b2"):
            _finish(P, block, sems, xout, xres_d)
            return nc
        emit_bigproj(wv_d, 2, AF.Silu, None, vdTl, False, "rf1", True)

        if phases == "ab":
            _finish(P, block, sems, xout, xres_d)
            return nc
        NI = CSC
        VB = 33 * 2
        PB = 7 * 2
        SBI = 3 * 2
        Vp = 20
        Pp = 1
        GQI = 14
        VC = Vp + NI * VB
        PC = Pp + NI * PB + 4
        SC_ = NI * SBI
        IC0, IC2, IC1, IC3 = 0, 1, 2, 3

        odall = nc.dram_tensor("odall", [2, NCH * 2 + 2, 64 * CSC], BF16,
                               kind="Internal").ap()
        sstA = [sb(f"sstA{g}", [2, 5 * CSC]) for g in range(2)]
        kst = [sb(f"kst{g}", [128, CSC], BF16) for g in range(2)]
        qst = [sb(f"qst{g}", [128, CSC], BF16) for g in range(2)]
        prod2 = sb("prod2", [128, 2 * CSC])
        gbof = [sb(f"gbof{g}", [2, 128]) for g in range(2)]
        scolf = [sb(f"scolf{g}", [2, 8]) for g in range(2)]
        tok8 = [sb(f"tok8{g}", [2, 8]) for g in range(2)]
        vstg = [sb(f"vstg{g}", [2, 64]) for g in range(2)]
        krow = [sb(f"krow{g}", [2, 128]) for g in range(2)]
        qrow = [sb(f"qrow{g}", [2, 128]) for g in range(2)]
        bones2f = sb("bones2f", [2, 128])
        P.add("sync", lambda e: e.dma_start(out=bones2f[:], in_=bones_d),
              inc=("dq", 16))
        P.bump("dq")
        dumm = sb("dumm", [1, 1])
        dumms = sb("dumms", [1, 1])
        dummd = sb("dummd", [1, 1])
        dummg = sb("dummg", [1, 1])
        idnf = sb("idnf_sb", [128, 128])
        krws2 = [bigA[0:2, 0:4096].bitcast(BF16), bigA[0:2, 4096:8192].bitcast(BF16)]
        vrws2 = [bigA[0:2, 8192:10240].bitcast(BF16), bigA[0:2, 10240:12288].bitcast(BF16)]
        orws2 = [bigA[0:2, 12288:14336].bitcast(BF16), bigA[0:2, 14336:16384].bitcast(BF16)]

        P.add("sync", lambda e: e.dma_start(out=idnf[:], in_=idnf_d), inc=("dq", 16))
        P.bump("dq")
        P.add("vector", lambda e: e.memset(kqDc[0][:], 0.0))
        P.add("vector", lambda e: e.memset(kqDc[1][:], 0.0))
        P.add("vector", lambda e: e.memset(krws2[0][:], 0.0),
              waits=[("pe", P.val("pe")), ("sd", P.val("sd"))])
        P.add("vector", lambda e: e.memset(krws2[1][:], 0.0))
        P.add("vector", lambda e: e.memset(orws2[0][:], 0.0))
        P.add("vector", lambda e: e.memset(orws2[1][:], 0.0))
        P.add("vector", lambda e: e.memset(fsq[0][:], 0.0))
        P.add("vector", lambda e: e.memset(fsq[1][:], 0.0))
        P.add("vector", lambda e: e.memset(nfns[0][:], 0.0))
        P.add("vector", lambda e: e.memset(nfns[1][:], 0.0))
        for g in range(2):
            P.add("vector", (lambda g=g: (lambda e: e.memset(UW[g][:], 0.0)))())
            P.add("vector", (lambda g=g: (lambda e: e.memset(inv2[g][:], 1.0)))())
        P.add("vector", lambda e: e.memset(dumm[:], 0.0), inc=("vd", 1))
        P.bump("vd")

        B0 = {"dq": P.val("dq") * 16, "pe": P.val("pe"), "sd": P.val("sd"),
              "vd": P.val("vd"), "gq": P.val("gq") * 16}
        import kernel as _KK; _KK._B0dump = dict(B0)

        def scan_gpsimd(e):
            e.wait_ge(sems["dq"], B0["dq"])
            with e.Fori(0, nch) as ch:
                gqb = B0["gq"] + ch * (GQI * 16)
                chVC = ch * VC
                vcb = B0["vd"] + chVC
                vprev = chVC
                pall = B0["pe"] + ch * PC
                csl = bass.ds(ch * CSC, CSC)
                tsch = bass.ts(ch, 1)
                ch2m = ch * 2 - 2
                e.wait_ge(sems["gq"], gqb)
                e.wait_ge(sems["vd"], vcb)
                e.dma_start(out=odall[0][bass.ds(ch * 2, 2), :], in_=orws2[0][:]
                            ).then_inc(sems["gq"], 16)
                e.dma_start(out=odall[1][bass.ds(ch * 2, 2), :], in_=orws2[1][:]
                            ).then_inc(sems["gq"], 16)
                e.wait_ge(sems["vd"], vprev)
                for g in range(2):
                    e.dma_start(out=kst[g][:], in_=kd[g][:, csl]
                                ).then_inc(sems["gq"], 16)
                    e.dma_start(out=qst[g][:], in_=qd[g][:, csl]
                                ).then_inc(sems["gq"], 16)
                    if g == 0:
                        e.wait_ge(sems["pe"], pall)
                    e.dma_start(
                        out=krws2[g][:].rearrange("r (t c) -> r t c", c=128),
                        in_=kdTb[g][:, csl, :]).then_inc(sems["gq"], 16)
                    e.dma_start(
                        out=vrws2[g][0:1, :].rearrange("r (t b) -> r t b", b=64),
                        in_=vdT[2 * g].rearrange("(c t) b -> c t b", t=CSC)[tsch, :, :]
                        ).then_inc(sems["gq"], 16)
                    e.dma_start(
                        out=vrws2[g][1:2, :].rearrange("r (t b) -> r t b", b=64),
                        in_=vdT[2 * g + 1].rearrange("(c t) b -> c t b", t=CSC)[tsch, :, :]
                        ).then_inc(sems["gq"], 16)
                    e.dma_start(
                        out=sstA[g][:].rearrange("p (q t) -> p q t", q=5),
                        in_=srd2[:, g, :, csl]).then_inc(sems["gq"], 16)
            e.wait_ge(sems["vd"], B0["vd"] + nch * VC)
            e.dma_start(out=odall[0][nch * 2:nch * 2 + 2, :], in_=orws2[0][:]
                        ).then_inc(sems["gq"], 16)
            e.dma_start(out=odall[1][nch * 2:nch * 2 + 2, :], in_=orws2[1][:]
                        ).then_inc(sems["gq"], 16)
            return e.dma_start(out=dummg[:], in_=rf_d)

        def scan_tensor(e):
            e.wait_ge(sems["dq"], B0["dq"])
            with e.Fori(0, nch) as ch:
                vcb = B0["vd"] + ch * VC
                scb = B0["sd"] + ch * SC_
                vpr = vcb + Vp
                e.wait_ge(sems["vd"], vcb + 18)
                e.matmul(out=pgr[0][0:2, 64:192], lhsT=onesd2[:],
                         rhs=prod2[:], start=True, stop=True
                         ).then_inc(sems["pe"], 1)
                with e.Fori(0, NI) as it:
                    vit = vpr + it * VB
                    sit = scb + it * SBI
                    it4 = bass.ts(it, 4)
                    it128 = bass.ts(it, 128)
                    streams = []
                    for g in range(2):
                        L = []

                        def W(sem, val):
                            L.append(("w", sem, val))

                        def O(fn):
                            L.append(("o", fn))

                        W("vd", vit + (g - 23))
                        W("vd", vpr)
                        O((lambda g=g: lambda: e.matmul(
                            out=pgr[g][:, 0:4], lhsT=UW[g][:],
                            rhs=kqDc[g][:, it4], start=True, stop=True))())
                        W("sd", sit + (3 * g - 4))
                        O((lambda g=g: lambda: e.matmul(
                            out=bkB[g][0:2, 4:6], lhsT=onesd2[:],
                            rhs=nfns[g][:], start=True, stop=True))())
                        w5 = vit + (13 + g)
                        W("vd", w5)
                        O((lambda g=g: lambda: e.matmul(
                            out=pgr[g][0:2, 64:192], lhsT=rvSB[g][:, 0:2],
                            rhs=idnf[:], start=True, stop=True))())
                        O((lambda g=g: lambda: e.matmul(
                            out=pgr[g][32:34, 64:192], lhsT=rvSB[g][:, 2:4],
                            rhs=idnf[:], start=True, stop=True))())
                        W("vd", w5)
                        O((lambda g=g: lambda: e.matmul(
                            out=bkB[g][:, 0:4], lhsT=bones2f[:],
                            rhs=scolf[g][:, 0:4], start=True, stop=True))())
                        W("vd", vit + (27 + g))
                        O((lambda g=g: lambda: e.matmul(
                            out=bkC[g][:, 0:128], lhsT=gbo[g][:],
                            rhs=krws2[g][:, it128], start=True, stop=True))())
                        W("vd", vit + (29 + g))
                        O((lambda g=g: lambda: e.matmul(
                            out=bkC[g][:, 128:256],
                            lhsT=kgSB[g][:], rhs=idnb[:], start=True, stop=True))())
                        streams.append(L)
                    idx = [0, 0]
                    while idx[0] < len(streams[0]) or idx[1] < len(streams[1]):
                        for g in range(2):
                            while idx[g] < len(streams[g]):
                                ent = streams[g][idx[g]]
                                idx[g] += 1
                                if ent[0] == "w":
                                    e.wait_ge(sems[ent[1]], ent[2])
                                else:
                                    ent[1]().then_inc(sems["pe"], 1)
                                    break
                for _pd in range(2):
                    e.matmul(out=pgr[0][0:1, 0:1], lhsT=rf1[:],
                             rhs=ones_row[0:1, 0:1], start=True, stop=True
                             ).then_inc(sems["pe"], 1)
                for g in range(2):
                    e.wait_ge(sems["sd"], scb + (SC_ + 3 * g - 4))
                    e.matmul(out=bkB[g][0:2, 4:6], lhsT=onesd2[:],
                             rhs=nfns[g][:], start=True, stop=True
                             ).then_inc(sems["pe"], 1)
            return e.matmul(out=pgr[0][0:1, 0:1], lhsT=rf1[:],
                            rhs=ones_row[0:1, 0:1], start=True, stop=True)

        def scan_scalar(e):
            with e.Fori(0, nch) as ch:
                peb = B0["pe"] + ch * PC + Pp
                vcb = B0["vd"] + ch * VC + Vp
                with e.Fori(0, NI) as it:
                    vit = vcb + it * VB
                    pit = peb + it * PB
                    for g in range(2):
                        pib = g * 7
                        if phases == "abl_s":
                            e.wait_ge(sems["vd"], vit + (41 + g))
                            e.copy(dumms[:], eps_col[0:1, 0:1]).then_inc(sems["sd"], 1)
                            e.wait_ge(sems["vd"], vit + (43 + g))
                            e.copy(dumms[:], eps_col[0:1, 0:1]).then_inc(sems["sd"], 1)
                            e.wait_ge(sems["pe"], pit + (13 + g))
                            e.copy(dumms[:], eps_col[0:1, 0:1]).then_inc(sems["sd"], 1)
                            continue
                        e.wait_ge(sems["vd"], vit + (41 + g))
                        e.wait_ge(sems["pe"], pit + (3 + g))
                        e.activation(out=sqj[g][:, 0:64],
                                     in_=UW[g][:, 0:64],
                                     func=AF.Square,
                                     accum_out=nfns[g][:, 0:1]
                                     ).then_inc(sems["sd"], 1)
                        e.wait_ge(sems["vd"], vit + (43 + g))
                        e.activation(out=sqj[g][:, 0:64],
                                     in_=UW[g][:, 64:128],
                                     func=AF.Square,
                                     accum_out=nfns[g][:, 1:2]
                                     ).then_inc(sems["sd"], 1)
                        e.wait_ge(sems["pe"], pit + (17 + g))
                        e.activation(out=nrm2[g][:], in_=bkB[g][0:2, 4:6],
                                     func=AF.Sqrt, scale=1.0 / 4096.0
                                     ).then_inc(sems["sd"], 1)
            return e.copy(dumms[:], eps_col[0:1, 0:1])

        def scan_vector(e):
            e.wait_ge(sems["dq"], B0["dq"])
            with e.Fori(0, nch) as ch:
                gqe = B0["gq"] + (ch * (GQI * 16) + GQI * 16)
                peb = B0["pe"] + ch * PC + Pp
                scb = B0["sd"] + ch * SC_

                def V(ins):
                    ins.then_inc(sems["vd"], 1)

                e.wait_ge(sems["gq"], gqe)
                for g in range(2):
                    V(e.tensor_copy(
                        kqDc[g][0:64, :].rearrange("p (t j) -> p t j", j=4)[:, :, 0:1],
                        kst[g][0:64, :].rearrange("p (t j) -> p t j", j=1)))
                    V(e.tensor_copy(
                        kqDc[g][64:128, :].rearrange("p (t j) -> p t j", j=4)[:, :, 1:2],
                        kst[g][64:128, :].rearrange("p (t j) -> p t j", j=1)))
                    V(e.tensor_copy(
                        kqDc[g][0:64, :].rearrange("p (t j) -> p t j", j=4)[:, :, 2:3],
                        qst[g][0:64, :].rearrange("p (t j) -> p t j", j=1)))
                    V(e.tensor_copy(
                        kqDc[g][64:128, :].rearrange("p (t j) -> p t j", j=4)[:, :, 3:4],
                        qst[g][64:128, :].rearrange("p (t j) -> p t j", j=1)))
                    for si, dj in [(0, 0), (1, 1), (2, 4), (3, 5)]:
                        V(e.tensor_copy(
                            fsq[g][:, :].rearrange("p (t j) -> p t j", j=8)[:, :, dj:dj + 1],
                            sstA[g][:, (1 + si) * CSC:(2 + si) * CSC].rearrange(
                                "p (t j) -> p t j", j=1)))
                    V(e.tensor_tensor(out=prod2[:, g * CSC:(g + 1) * CSC],
                                      in0=qst[g][:], in1=kst[g][:], op=ALU.mult))
                for g in range(2):
                    if g == 0:
                        e.wait_ge(sems["pe"], peb)
                    V(e.tensor_tensor(
                        out=fsq[g][:, :].rearrange("p (t j) -> p t j", j=8)[:, :, 2:3],
                        in0=pgr[0][0:2, 64 + g * CSC:64 + (g + 1) * CSC].rearrange(
                            "p (t j) -> p t j", j=1),
                        in1=sstA[g][:, 0:CSC].rearrange("p (t j) -> p t j", j=1),
                        op=ALU.mult))
                with e.Fori(0, NI) as it:
                    pit = peb + it * PB
                    sit = scb + it * SBI
                    it8 = bass.ts(it, 8)
                    it64 = bass.ts(it, 64)
                    streams = []
                    for g in range(2):
                        pib = g * 7
                        fsq_sl = fsq[g][:, it8]
                        vrws_sl = vrws2[g][:, it64]
                        orws_sl = orws2[g][:, it64]
                        L = []

                        def W(sem, val):
                            L.append(("w", sem, val))

                        if phases.startswith("abl_v") or phases in ("v6a", "v6b"):
                            kcut = (6 if phases in ("v6a", "v6b")
                                    else int(phases[5:]) if len(phases) > 5 else -1)
                            kctr = [0]

                            def O(fn, kcut=kcut, kctr=kctr):
                                j = kctr[0]
                                kctr[0] += 1
                                if j <= kcut:
                                    L.append(("o", fn))
                                else:
                                    L.append(("o", lambda: e.memset(dumm[:], 0.0)))
                        else:
                            def O(fn):
                                L.append(("o", fn))

                        W("pe", pit + (1 + g))
                        O((lambda g=g: lambda: e.tensor_copy(rvSB[g][:], pgr[g][:, 0:4]))())
                        O((lambda g=g, fsq_sl=fsq_sl: lambda: e.tensor_copy(tok8[g][:], fsq_sl))())
                        O((lambda g=g, vrws_sl=vrws_sl: lambda: e.tensor_copy(vstg[g][:], vrws_sl))())
                        O((lambda g=g: lambda: e.tensor_tensor(
                            out=scolf[g][:, 0:2], in0=tok8[g][:, 0:2],
                            in1=inv2[g][:], op=ALU.mult))())
                        O((lambda g=g: lambda: e.tensor_scalar(
                            out=scolf[g][:, 2:3], in0=scolf[g][:, 1:2],
                            scalar1=rfc2[:, 0:1], scalar2=None, op0=ALU.mult))())
                        O((lambda g=g: lambda: e.tensor_scalar(
                            out=scolf[g][:, 3:4], in0=scolf[g][:, 0:1],
                            scalar1=rfc2[:, 0:1], scalar2=None, op0=ALU.mult))())
                        O((lambda g=g: lambda: e.tensor_scalar(
                            out=scolf[g][:, 4:8], in0=scolf[g][:, 0:4],
                            scalar1=-1.0, scalar2=None, op0=ALU.mult))())
                        W("pe", pit + (7 + g))
                        O((lambda g=g: lambda: e.tensor_copy(
                            krow[g][:], pgr[g][0:2, 64:192]))())
                        O((lambda g=g: lambda: e.tensor_copy(
                            qrow[g][:], pgr[g][32:34, 64:192]))())
                        O((lambda g=g: lambda: e.scalar_tensor_tensor(
                            out=g1s[g][:], in0=krow[g][:, 0:64],
                            scalar=scolf[g][:, 4 + IC0:5 + IC0], in1=vstg[g][:],
                            op0=ALU.mult, op1=ALU.add))())
                        O((lambda g=g: lambda: e.scalar_tensor_tensor(
                            out=gbof[g][:, 0:64], in0=krow[g][:, 64:128],
                            scalar=scolf[g][:, 4 + IC1:5 + IC1], in1=g1s[g][:],
                            op0=ALU.mult, op1=ALU.add))())
                        O((lambda g=g: lambda: e.scalar_tensor_tensor(
                            out=g1s[g][:], in0=krow[g][:, 0:64],
                            scalar=scolf[g][:, 4 + IC3:5 + IC3], in1=vstg[g][:],
                            op0=ALU.mult, op1=ALU.add))())
                        O((lambda g=g: lambda: e.scalar_tensor_tensor(
                            out=gbof[g][:, 64:128], in0=krow[g][:, 64:128],
                            scalar=scolf[g][:, 4 + IC2:5 + IC2], in1=g1s[g][:],
                            op0=ALU.mult, op1=ALU.add))())
                        O((lambda g=g: lambda: e.tensor_copy(gbo[g][:], gbof[g][:]))())
                        W("pe", pit + (11 + g))
                        O((lambda g=g: lambda: e.tensor_copy(
                            kgSB[g][:], bkC[g][:, 0:128]))())
                        O((lambda g=g: lambda: e.tensor_copy(
                            ccols[g][:], bkB[g][:, 0:4]))())
                        O((lambda g=g: lambda: e.tensor_scalar(
                            out=upds[g][:, 0:64], in0=UW[g][:, 64:128],
                            scalar1=ccols[g][:, IC1:IC1 + 1], scalar2=None,
                            op0=ALU.mult))())
                        O((lambda g=g: lambda: e.scalar_tensor_tensor(
                            out=upds[g][:, 0:64], in0=UW[g][:, 0:64],
                            scalar=ccols[g][:, IC0:IC0 + 1], in1=upds[g][:, 0:64],
                            op0=ALU.mult, op1=ALU.add))())
                        O((lambda g=g: lambda: e.tensor_scalar(
                            out=upds[g][:, 64:128], in0=UW[g][:, 0:64],
                            scalar1=ccols[g][:, IC3:IC3 + 1], scalar2=None,
                            op0=ALU.mult))())
                        O((lambda g=g: lambda: e.scalar_tensor_tensor(
                            out=upds[g][:, 64:128], in0=UW[g][:, 64:128],
                            scalar=ccols[g][:, IC2:IC2 + 1], in1=upds[g][:, 64:128],
                            op0=ALU.mult, op1=ALU.add))())
                        W("pe", pit + (13 + g))
                        O((lambda g=g: lambda: e.tensor_tensor(
                            out=UW[g][:, 0:64], in0=upds[g][:, 0:64],
                            in1=bkC[g][:, 128:192],
                            op=ALU.add))())
                        O((lambda g=g: lambda: e.tensor_tensor(
                            out=UW[g][:, 64:128], in0=upds[g][:, 64:128],
                            in1=bkC[g][:, 192:256],
                            op=ALU.add))())
                        W("sd", sit + (g * 3 + 3))
                        O((lambda g=g: lambda: e.tensor_scalar(
                            out=m1s[g][:], in0=nrm2[g][:], scalar1=1.0,
                            scalar2=None, op0=ALU.max))())
                        O((lambda g=g: lambda: e.reciprocal(
                            out=inv2[g][:], in_=m1s[g][:]))())
                        O((lambda g=g: lambda: e.tensor_tensor(
                            out=tfs[g][:, 0:2], in0=tok8[g][:, 4:6],
                            in1=inv2[g][:], op=ALU.mult))())
                        O((lambda g=g: lambda: e.tensor_scalar(
                            out=g1s[g][:], in0=qrow[g][:, 0:64],
                            scalar1=scolf[g][:, IC0:IC0 + 1], scalar2=None,
                            op0=ALU.mult))())
                        O((lambda g=g: lambda: e.scalar_tensor_tensor(
                            out=g1s[g][:], in0=qrow[g][:, 64:128],
                            scalar=scolf[g][:, IC1:IC1 + 1], in1=g1s[g][:],
                            op0=ALU.mult, op1=ALU.add))())
                        O((lambda g=g: lambda: e.scalar_tensor_tensor(
                            out=g1s[g][:], in0=gbof[g][:, 0:64],
                            scalar=tok8[g][:, 2:3], in1=g1s[g][:],
                            op0=ALU.mult, op1=ALU.add))())
                        O((lambda g=g: lambda: e.tensor_scalar(
                            out=gbof[g][:, 0:64], in0=g1s[g][:],
                            scalar1=tfs[g][:, 0:1], scalar2=None, op0=ALU.mult))())
                        O((lambda g=g: lambda: e.tensor_scalar(
                            out=g1s[g][:], in0=qrow[g][:, 0:64],
                            scalar1=scolf[g][:, IC3:IC3 + 1], scalar2=None,
                            op0=ALU.mult))())
                        O((lambda g=g: lambda: e.scalar_tensor_tensor(
                            out=g1s[g][:], in0=qrow[g][:, 64:128],
                            scalar=scolf[g][:, IC2:IC2 + 1], in1=g1s[g][:],
                            op0=ALU.mult, op1=ALU.add))())
                        O((lambda g=g: lambda: e.scalar_tensor_tensor(
                            out=g1s[g][:], in0=gbof[g][:, 64:128],
                            scalar=tok8[g][:, 2:3], in1=g1s[g][:],
                            op0=ALU.mult, op1=ALU.add))())
                        O((lambda g=g, orws_sl=orws_sl: lambda: e.scalar_tensor_tensor(
                            out=orws_sl, in0=g1s[g][:],
                            scalar=tfs[g][:, 1:2], in1=gbof[g][:, 0:64],
                            op0=ALU.mult, op1=ALU.add))())
                        streams.append(L)
                    idx = [0, 0]
                    opn = [0, 0]
                    while idx[0] < len(streams[0]) or idx[1] < len(streams[1]):
                        for g in range(2):
                            while idx[g] < len(streams[g]):
                                ent = streams[g][idx[g]]
                                idx[g] += 1
                                if ent[0] == "w":
                                    e.wait_ge(sems[ent[1]], ent[2])
                                else:
                                    ent[1]().then_inc(sems["vd"], 1)
                                    opn[g] += 1
                                    break
            return e.memset(dumm[:], 0.0)

        P.add("gpsimd", scan_gpsimd, inc=("gq", 16))
        P.add("gpsimd", None, inc=("gq", nch * GQI * 16 + 32))
        P.cnt["gq"] = P.val("gq") + nch * GQI + 3
        P.add("tensor", scan_tensor, inc=("pe", 1))
        P.add("tensor", None, inc=("pe", nch * PC))
        P.cnt["pe"] = P.val("pe") + nch * PC + 1
        P.add("scalar", scan_scalar, inc=("sd", 1))
        P.add("scalar", None, inc=("sd", nch * SC_))
        P.cnt["sd"] = P.val("sd") + nch * SC_ + 1
        P.add("vector", scan_vector, inc=("vd", 1))
        P.add("vector", None, inc=("vd", nch * VC))
        P.cnt["vd"] = P.val("vd") + nch * VC + 1
        if tail and nch < NCH:
            P.add("vector", lambda e: e.memset(oTf[0][:], 0.0))
            P.add("vector", lambda e: e.memset(oTf[1][:], 0.0), inc=("vd", 1))
            P.bump("vd")
        if not tail:
            P.add("sync", lambda e: e.dma_start(out=xout, in_=xres_d),
                  waits=[("gq", P.val("gq") * 16), ("vd", P.val("vd")),
                         ("pe", P.val("pe")), ("sd", P.val("sd"))],
                  inc=("dq", 16))
            P.bump("dq")
            P.add("sync", None, waits=[("dq", P.val("dq") * 16)])
            P.run_all(block, sems)
            return nc
        otkB = [sb(f"otkB{i}", [64, 512], BF16) for i in range(2)]
        gstgB = [sb(f"gstgB{i}", [128, 256], BF16) for i in range(2)]
        ystgf = [sb(f"ystgf{i}", [128, 512]) for i in range(2)]
        for b4 in range(nch // 4):
            for g in range(2):
                e2 = b4 * 2 + g
                par = e2 % 2
                key = f"c2pe_{e2 - 2}"
                ei0 = b4 * 8 + g * 4
                P.add("sync", (lambda b4=b4, g=g, par=par: (lambda e: e.dma_start(
                    out=otkB[par][:, :].rearrange("t (c h b) -> t c h b", c=4, h=2),
                    in_=odall[g][8 * b4 + 2:8 * b4 + 10, :].rearrange(
                        "(c h) (t b) -> t c h b", h=2, b=64))))(),
                    waits=[("pe", P.cnt.get(key, 0))] +
                          ([("gq", P.val("gq") * 16)] if b4 == 0 and g == 0 else []),
                    inc=("dq", 16))
                P.bump("dq")
                P.add("sync", (lambda b4=b4, g=g, par=par: (lambda e: e.dma_start(
                    out=gstgB[par][0:64, :],
                    in_=gd[2 * g][:, b4 * 256:(b4 + 1) * 256])))(),
                    waits=[("sd", P.cnt.get(f"c2sl_{e2 - 2}", 0))], inc=("dq", 16))
                P.bump("dq")
                P.add("sync", (lambda b4=b4, g=g, par=par: (lambda e: e.dma_start(
                    out=gstgB[par][64:128, :],
                    in_=gd[2 * g + 1][:, b4 * 256:(b4 + 1) * 256])))(),
                    inc=("dq", 16))
                gld = P.bump("dq") * 16
                for c4 in range(4):
                    ch = 4 * b4 + c4
                    ei = ei0 + c4
                    pp = ch % 2
                    P.add("tensor", (lambda par=par, c4=c4, pp=pp: (lambda e: e.matmul(
                        out=pps[pp][:, 0:64], lhsT=otkB[par][:, c4 * 128:(c4 + 1) * 128],
                        rhs=idnb[0:64, 0:64], start=True, stop=True)))(),
                        waits=[("dq", gld), ("vd", P.cnt.get(f"c2m1_{ei - 2}", 0))],
                        inc=("pe", 1))
                    tpe = P.bump("pe")
                    P.add("scalar", (lambda pp=pp: (lambda e: e.activation(
                        out=stg[:, 0:64], in_=pps[pp][:, 0:64], func=AF.Square)))(),
                        waits=[("pe", tpe)], inc=("sd", 1))
                    P.bump("sd")
                    P.add("tensor", (lambda pp=pp: (lambda e: e.matmul(
                        out=pps[pp][0:2, 128:192], lhsT=onesd2[:], rhs=stg[:, 0:64],
                        start=True, stop=True)))(),
                        waits=[("sd", P.val("sd"))], inc=("pe", 1))
                    nrpe = P.bump("pe")
                    P.add("scalar", (lambda pp=pp: (lambda e: e.activation(
                        out=tfs[0][:, 0:64], in_=pps[pp][0:2, 128:192], func=AF.Sqrt,
                        scale=1.0 / 64.0, bias=eps_col[0:2, 0:1])))(),
                        waits=[("pe", nrpe),
                               ("vd", P.cnt.get(f"c2rc_{ei - 1}", 0))], inc=("sd", 1))
                    P.bump("sd")
                    P.add("vector", lambda e: e.reciprocal(
                        out=tfs[1][:, 0:64], in_=tfs[0][:, 0:64]),
                        waits=[("sd", P.val("sd")),
                               ("pe", P.cnt.get(f"c2bc_{ei - 1}", 0))], inc=("vd", 1))
                    P.cnt[f"c2rc_{ei}"] = P.bump("vd")
                    P.add("tensor", (lambda pp=pp: (lambda e: e.matmul(
                        out=pps[pp][:, 64:128], lhsT=bones2f[:], rhs=tfs[1][:, 0:64],
                        start=True, stop=True)))(),
                        waits=[("vd", P.val("vd"))], inc=("pe", 1))
                    P.cnt[f"c2bc_{ei}"] = P.cnt[f"c2pe_{e2}"] = P.bump("pe")
                    P.add("scalar", (lambda par=par, c4=c4: (lambda e: e.activation(
                        out=stg2[:, 0:64], in_=gstgB[par][:, c4 * 64:(c4 + 1) * 64],
                        func=AF.Silu)))(),
                        waits=[("vd", P.cnt.get(f"c2m3_{ei - 1}", 0))], inc=("sd", 1))
                    P.cnt[f"c2sl_{e2}"] = P.bump("sd")
                    sil_sd = P.val("sd")
                    P.add("vector", (lambda pp=pp: (lambda e: e.tensor_scalar(
                        out=stg[:, 64:128], in0=pps[pp][:, 0:64],
                        scalar1=onrm2[:, 0:1], scalar2=None, op0=ALU.mult)))(),
                        waits=[("pe", P.val("pe"))], inc=("vd", 1))
                    P.bump("vd")
                    P.add("vector", (lambda pp=pp: (lambda e: e.tensor_tensor(
                        out=stg[:, 64:128], in0=stg[:, 64:128],
                        in1=pps[pp][:, 64:128], op=ALU.mult)))(),
                        inc=("vd", 1))
                    P.cnt[f"c2m1_{ei}"] = P.bump("vd")
                    P.add("vector", (lambda g=g, ch=ch: (lambda e: e.tensor_tensor(
                        out=oTf[g][:, ch * 64:(ch + 1) * 64], in0=stg[:, 64:128],
                        in1=stg2[:, 0:64], op=ALU.mult)))(),
                        waits=[("sd", sil_sd)], inc=("vd", 1))
                    P.cnt[f"c2m3_{ei}"] = P.bump("vd")

        wof = bigA[:, 0:2048]
        wob = bigA[:, 8192:9216].bitcast(BF16)
        ybufv = bigA[:, 4096:8192]
        xrbuf = bigA[:, 0:4096]
        P.add("sync", lambda e: e.dma_start(
            out=wof, in_=wo_d.rearrange("p a b c -> p (a b c)")), inc=("dq", 16))
        wold = P.bump("dq") * 16
        P.add("vector", lambda e: e.tensor_copy(wob[:], wof),
              waits=[("dq", wold)], inc=("vd", 1))
        wcst = P.bump("vd")
        for q in range(4):
            for t5 in range(4):
                for oc in range(8):
                    po = oc % 2
                    j_ = (q * 4 + t5) * 8 + oc
                    for ic in range(2):
                        P.add("tensor", (lambda q=q, t5=t5, oc=oc, ic=ic, po=po: (lambda e: e.matmul(
                            out=pps[po][:, 0:512],
                            lhsT=wob[:, (ic * 8 + oc) * 128:(ic * 8 + oc + 1) * 128],
                            rhs=oTf[ic][:, q * 2048 + t5 * 512:q * 2048 + (t5 + 1) * 512],
                            start=(ic == 0), stop=(ic == 1))))(),
                            waits=([("vd", max(wcst, P.cnt.get(f"dev_{j_ - 2}", 0))),
                                    ("sd", P.val("sd"))] if ic == 0 else []),
                            inc=(("pe", 1) if ic == 1 else None))
                    mpe = P.bump("pe")
                    P.add("vector", (lambda po=po: (lambda e: e.tensor_copy(
                        ystgf[po][:], pps[po][:, 0:512])))(),
                        waits=[("pe", mpe),
                               ("dq", P.cnt.get(f"dyp_{j_ - 2}", 0))],
                        inc=("vd", 1))
                    P.cnt[f"dev_{j_}"] = P.bump("vd")
                    P.add("sync", (lambda q=q, t5=t5, oc=oc, po=po: (lambda e: e.dma_start(
                        out=ypd[q][:, oc, t5 * 512:(t5 + 1) * 512],
                        in_=ystgf[po][:])))(),
                        waits=[("vd", P.val("vd"))], inc=("dq", 16))
                    P.cnt[f"dyp_{j_}"] = P.bump("dq") * 16
        P.add("gpsimd", lambda e: e.collective_compute(
            "ReduceScatter", ALU.add,
            replica_groups=[[0, 1, 2, 3], [4, 5, 6, 7]],
            ins=[ypd.opt()], outs=[yprd.opt()]),
            waits=[("dq", P.val("dq") * 16)], inc=("cc", 1))
        for t5 in range(4):
            P.add("sync", (lambda t5=t5: (lambda e: e.dma_start(
                out=ybufv.rearrange("p (c t) -> p c t", c=8),
                in_=yprd[:, :, t5 * 512:(t5 + 1) * 512])))(),
                waits=[("cc", 1), ("vd", P.val("vd")), ("pe", P.val("pe"))],
                inc=("dq", 16))
            P.bump("dq")
            P.add("sync", (lambda t5=t5: (lambda e: e.dma_start(
                out=xrbuf.rearrange("p (c t) -> p c t", c=8),
                in_=xres_d[:, :, t5 * 512:(t5 + 1) * 512])))(),
                inc=("dq", 16))
            yld = P.bump("dq") * 16
            for oc in range(8):
                P.add("vector", (lambda oc=oc: (lambda e: e.tensor_scalar(
                    out=ybufv[:, oc * 512:(oc + 1) * 512],
                    in0=ybufv[:, oc * 512:(oc + 1) * 512],
                    scalar1=bo_sb[:, oc:oc + 1], scalar2=None, op0=ALU.add)))(),
                    waits=([("dq", yld)] if oc == 0 else []),
                    inc=(("vd", 1) if oc == 7 else None))
            P.bump("vd")
            emit_ln_stats(P, ybufv, 8, 512, ones_col, xsq, mrow, rrow, srow,
                          pps[0][0:1, 0:512], pps[0][32:33, 0:512], eps_col,
                          [("vd", P.val("vd")), ("sd", P.val("sd"))], CS=512)
            emit_bcast_row(P, mrow[:, 0:512], pps[1][:, 0:512], 512, ones_row,
                           [("vd", P.val("vd"))])
            emit_bcast_row(P, rrow[:, 0:512], psc[:, 0:512], 512, ones_row, [])
            bpe = P.val("pe")
            for oc in range(8):
                P.add("vector", (lambda oc=oc: (lambda e: e.tensor_tensor(
                    out=xsq[0][:, 0:512], in0=ybufv[:, oc * 512:(oc + 1) * 512],
                    in1=pps[1][:, 0:512], op=ALU.subtract)))(),
                    waits=([("pe", bpe)] if oc == 0 else []))
                P.add("vector", lambda e: e.tensor_tensor(
                    out=xsq[0][:, 0:512], in0=xsq[0][:, 0:512],
                    in1=psc[:, 0:512], op=ALU.mult))
                P.add("vector", (lambda oc=oc: (lambda e: e.tensor_tensor(
                    out=ybufv[:, oc * 512:(oc + 1) * 512], in0=xsq[0][:, 0:512],
                    in1=xrbuf[:, oc * 512:(oc + 1) * 512], op=ALU.add)))(),
                    inc=(("vd", 1) if oc == 7 else None))
            P.bump("vd")
            P.add("sync", (lambda t5=t5: (lambda e: e.dma_start(
                out=xout[:, :, t5 * 512:(t5 + 1) * 512],
                in_=ybufv.rearrange("p (c t) -> p c t", c=8))))(),
                waits=[("vd", P.val("vd"))], inc=("dq", 16))
            P.bump("dq")
        P.add("sync", None, waits=[("dq", P.val("dq") * 16)])
        P.run_all(block, sems)
    return nc


def _prep_l1attn_inputs(inputs, c):
    b = c // 4
    hg = c % 4
    hs = [4 * hg + i for i in range(4)]
    m = {}

    def headcols(W):
        return np.concatenate([W[:, h * 64:(h + 1) * 64] for h in hs], axis=1)

    for nm, key in [("wq", "L1_attn_Wq"), ("wk", "L1_attn_Wk"),
                    ("wv", "L1_attn_Wv"), ("wg", "L1_attn_Wg")]:
        m[nm] = _chunk_w(headcols(inputs[key]))
    for nm, key in [("bq", "L1_attn_bq"), ("bk", "L1_attn_bk"),
                    ("bv", "L1_attn_bv"), ("bg", "L1_attn_bg")]:
        bb_ = np.concatenate([inputs[key][h * 64:(h + 1) * 64] for h in hs])
        m[nm] = np.ascontiguousarray(bb_.reshape(2, 128).T)
    wo_rows = np.concatenate([inputs["L1_attn_Wo"][h * 64:(h + 1) * 64, :]
                              for h in hs], axis=0)
    m["wo"] = np.ascontiguousarray(
        wo_rows.reshape(2, 128, 8, 128).transpose(1, 0, 2, 3))
    m["bo"] = np.ascontiguousarray(inputs["L1_attn_bo"].reshape(8, 128).T)
    cols = []
    b1 = []
    b2 = []
    sml = [("L1_attn_Wb", "L1_attn_bb", None),
           ("L1_attn_Wfd", "L1_attn_bfd", "L1_attn_fast_decay_bias"),
           ("L1_attn_Wsd", "L1_attn_bsd", "L1_attn_slow_decay_bias"),
           ("L1_attn_Wfg", "L1_attn_bfg", None),
           ("L1_attn_Wsg", "L1_attn_bsg", None)]
    for j in range(2):
        for g in range(2):
            for wk_, bk_, dk_ in sml:
                h = hs[2 * g + j]
                cols.append(inputs[wk_][:, h])
                b1.append(inputs[bk_][h])
                b2.append(inputs[dk_][h] if dk_ is not None else np.float32(0.0))
    wsm = np.stack(cols, axis=1)
    m["wsm"] = np.ascontiguousarray(wsm.reshape(8, 128, 20).transpose(1, 0, 2))
    m["sb1"] = np.array(b1, np.float32).reshape(20, 1)
    m["sb2"] = np.array(b2, np.float32).reshape(20, 1)
    m["onrm"] = np.tile(np.asarray(inputs["L1_attn_onorm_w"], np.float32), 2
                        ).reshape(128, 1)
    m["rf"] = np.asarray(inputs["L1_attn_resonance_flux"],
                         np.float32).reshape(1, 1)
    m["idnb"] = np.eye(128, dtype=np.float32).astype(BF)
    m["idnf"] = np.eye(128, dtype=np.float32)
    bn = np.zeros((2, 128), np.float32)
    bn[0, 0:64] = 1.0
    bn[1, 64:128] = 1.0
    m["bones"] = bn
    return m


def launch_l1attn(x2, inputs):
    nc = build_l1attn()
    x2 = np.asarray(x2, np.float32).reshape(2, S, D)
    percore = []
    for c in range(8):
        m = _prep_l1attn_inputs(inputs, c)
        xfm = _feat_major(x2[c // 4])
        m["xb"] = xfm
        m["xres"] = np.ascontiguousarray(
            xfm[:, :, (c % 4) * 2048:(c % 4 + 1) * 2048])
        percore.append(m)
    res = run_bass_kernel_spmd(nc, percore, core_ids=list(range(8)))
    x3 = np.zeros((2, S, D), np.float32)
    for c in range(8):
        x3[c // 4, (c % 4) * 2048:(c % 4 + 1) * 2048] = _unfeat_major(
            res.results[c]["xout"])
    return x3


# version 04
# speedup vs baseline: 1.0055x; 1.0055x over previous
import numpy as np
import ml_dtypes
from contextlib import ExitStack
import concourse.bass as bass
import concourse.mybir as mybir
from concourse.bass_utils import run_bass_kernel_spmd

F32 = mybir.dt.float32
BF16 = mybir.dt.bfloat16
AF = mybir.ActivationFunctionType
ALU = mybir.AluOpType
AX = mybir.AxisListType

NCORES = 8
D = 1024
B = 2
S = 8192
T_ALL = B * S
T_CORE = T_ALL // NCORES

BF = ml_dtypes.bfloat16


class Plan:
    ENGINES = ["sync", "tensor", "scalar", "vector", "gpsimd"]

    def __init__(self):
        self.ops = {e: [] for e in self.ENGINES}
        self.cnt = {}

    def add(self, eng, fn, waits=(), inc=None):
        self.ops[eng].append([fn, list(waits), inc])

    def bump(self, sem, amt=1):
        v = self.cnt.get(sem, 0) + amt
        self.cnt[sem] = v
        return v

    def val(self, sem):
        return self.cnt.get(sem, 0)

    CHAIN = {"vector": "vch", "scalar": "sch", "gpsimd": "gch", "sync": "dq"}

    def run_all(self, block, sems):
        def runner(eng_name):
            chain = self.CHAIN.get(eng_name)
            def go(e):
                cnt2 = {}
                last = None
                for fn, waits, inc in self.ops[eng_name]:
                    for s, v in waits:
                        e.wait_ge(sems[s], v)
                    if fn is None:
                        if inc is not None and inc[0] is not None:
                            cnt2[inc[0]] = cnt2.get(inc[0], 0) + inc[1]
                            last = (inc[0], cnt2[inc[0]])
                        continue
                    if chain and last is not None:
                        e.wait_ge(sems[last[0]], last[1])
                    ins = fn(e)
                    up = inc if inc is not None else (chain, 1)
                    if up[0] is not None:
                        ins.then_inc(sems[up[0]], up[1])
                        cnt2[up[0]] = cnt2.get(up[0], 0) + up[1]
                        last = (up[0], cnt2[up[0]])
                    n = 0
            return go
        block.sync(runner("sync"))
        block.tensor(runner("tensor"))
        block.scalar(runner("scalar"))
        block.vector(runner("vector"))
        block.gpsimd(runner("gpsimd"))


def _chunk_w(w, pin=128):
    In, Out = w.shape
    nic, noc = In // pin, Out // pin
    return np.ascontiguousarray(w.reshape(nic, pin, noc, pin).transpose(2, 1, 0, 3))


def _feat_major(x2d):
    T, Din = x2d.shape
    nic = Din // 128
    return np.ascontiguousarray(x2d.T.reshape(nic, 128, T).transpose(1, 0, 2))


def _unfeat_major(xfm):
    p, nic, T = xfm.shape
    return np.ascontiguousarray(xfm.transpose(1, 0, 2).reshape(nic * 128, T).T)


def emit_ln_stats(P, src, nic, TP, ones_col, xsq, mrow, rrow, srow,
                  ps_sum, ps_sq, eps_col, extra_wait, CS=None, off=0):
    NT5 = TP // 512
    if CS is None:
        CS = TP
    for s5 in range(NT5):
        for ic in range(nic):
            P.add("tensor",
                  (lambda s5=s5, ic=ic: (lambda e: e.matmul(
                      out=ps_sum[:, s5 * 512:(s5 + 1) * 512], lhsT=ones_col[:, 0:1],
                      rhs=src[:, ic * CS + off + s5 * 512: ic * CS + off + (s5 + 1) * 512],
                      start=(ic == 0), stop=(ic == nic - 1))))(),
                  waits=(list(extra_wait) if (s5 == 0 and ic == 0) else []),
                  inc=(("pe", 1) if (s5 == NT5 - 1 and ic == nic - 1) else None))
    P.bump("pe")
    pb0 = P.val("pe")
    sd0 = P.val("sd")
    for ic in range(nic):
        w = list(extra_wait) if ic == 0 else []
        if ic >= 2:
            w.append(("pe", pb0 + ic - 1))
        P.add("scalar",
              (lambda ic=ic: (lambda e: e.activation(
                  out=xsq[ic % 2][:, 0:TP], in_=src[:, ic * CS + off:ic * CS + off + TP],
                  func=AF.Square)))(),
              waits=w, inc=("sd", 1))
        P.bump("sd")
        for s5 in range(NT5):
            P.add("tensor",
                  (lambda s5=s5, ic=ic: (lambda e: e.matmul(
                      out=ps_sq[:, s5 * 512:(s5 + 1) * 512], lhsT=ones_col[:, 0:1],
                      rhs=xsq[ic % 2][:, s5 * 512:(s5 + 1) * 512],
                      start=(ic == 0), stop=(ic == nic - 1))))(),
                  waits=([("sd", sd0 + ic + 1)] if s5 == 0 else []),
                  inc=(("pe", 1) if s5 == NT5 - 1 else None))
        P.bump("pe")
    pe_done = P.val("pe")
    P.add("vector", lambda e: e.tensor_scalar(
        out=mrow[:, 0:TP], in0=ps_sum[:, 0:TP], scalar1=1.0 / D, scalar2=None,
        op0=ALU.mult), waits=[("pe", pe_done)])
    P.add("vector", lambda e: e.tensor_scalar(
        out=srow[:, 0:TP], in0=ps_sq[:, 0:TP], scalar1=1.0 / D, scalar2=None,
        op0=ALU.mult))
    P.add("vector", lambda e: e.tensor_tensor(
        out=rrow[:, 0:TP], in0=mrow[:, 0:TP], in1=mrow[:, 0:TP], op=ALU.mult))
    P.add("vector", lambda e: e.tensor_tensor(
        out=srow[:, 0:TP], in0=srow[:, 0:TP], in1=rrow[:, 0:TP], op=ALU.subtract),
        inc=("vd", 1))
    P.bump("vd")
    P.add("scalar", lambda e: e.activation(
        out=srow[:, 0:TP], in_=srow[:, 0:TP], func=AF.Sqrt, bias=eps_col[0:1, 0:1]),
        waits=[("vd", P.val("vd"))], inc=("sd", 1))
    P.bump("sd")
    P.add("vector", lambda e: e.reciprocal(out=rrow[:, 0:TP], in_=srow[:, 0:TP]),
        waits=[("sd", P.val("sd"))], inc=("vd", 1))
    P.bump("vd")


def emit_bcast_row(P, row, ps_dst, TP, ones_row, wait):
    NT5 = TP // 512
    for s5 in range(NT5):
        P.add("tensor",
              (lambda s5=s5: (lambda e: e.matmul(
                  out=ps_dst[:, s5 * 512:(s5 + 1) * 512], lhsT=ones_row[0:1, :],
                  rhs=row[:, s5 * 512:(s5 + 1) * 512], start=True, stop=True)))(),
              waits=(list(wait) if s5 == 0 else []),
              inc=(("pe", 1) if s5 == NT5 - 1 else None))
    P.bump("pe")



def build_l0ffn(TP=1024):
    nc = bass.Bass("TRN2", target_bir_lowering=False, debug=False)
    NPASS = T_CORE // TP
    xin = nc.dram_tensor("xin", [128, 8, T_CORE], F32, kind="ExternalInput").ap()
    xout = nc.dram_tensor("xout", [128, 8, T_CORE], F32, kind="ExternalOutput").ap()
    wdr = {}
    for nm, shp in [("w1", [32, 128, 8, 128]), ("w2", [32, 128, 8, 128]),
                    ("h1", [32, 128, 32, 128]), ("h2", [32, 128, 32, 128]),
                    ("w3", [8, 128, 32, 128])]:
        wdr[nm] = nc.dram_tensor(nm, shp, BF16, kind="ExternalInput").ap()
    scl_d = nc.dram_tensor("scl", [128, 136], F32, kind="ExternalInput").ap()

    with ExitStack() as ctx:
        sb = lambda n, shp, dt=F32: ctx.enter_context(nc.sbuf_tensor(n, shp, dt))
        xf = sb("xf", [128, 8 * TP])
        xnb = sb("xnb", [128, 8 * TP], BF16)
        hB = sb("hB", [128, 32 * TP], BF16)
        hC = sb("hC", [128, 32 * TP], BF16)
        yv = hC[:, 0:16 * TP].bitcast(F32)
        wbuf = [sb(f"wbuf{i}", [128, 40 * 128], BF16) for i in range(2)]
        xsq = [hB[:, 0:2 * TP].bitcast(F32), hB[:, 2 * TP:4 * TP].bitcast(F32)]
        scl = sb("scl_sb", [128, 136])
        ones_col = sb("ones_col", [128, 1])
        ones_row = sb("ones_row", [1, 128])
        eps_col = sb("eps_col", [128, 1])
        mrow = hC[0:1, 16 * TP:18 * TP].bitcast(F32)
        rrow = hC[0:1, 18 * TP:20 * TP].bitcast(F32)
        srow = hC[0:1, 20 * TP:22 * TP].bitcast(F32)
        gtile = hC[:, 30 * TP:32 * TP]

        ps = [ctx.enter_context(nc.psum_tensor(f"ps{i}", [128, TP], F32))
              for i in range(4)]
        sems = {}
        for s in ["dq", "pe", "sd", "vd", "vch", "sch", "gch"]:
            sems[s] = ctx.enter_context(nc.semaphore("sem_" + s))
        block = ctx.enter_context(nc.Block())

        P = Plan()
        SCL_OFF = {"w1": 0, "w2": 32, "h1": 64, "h2": 96, "w3": 128}

        P.add("vector", lambda e: e.memset(ones_col[:], 1.0))
        P.add("vector", lambda e: e.memset(eps_col[:], 1e-5))
        P.add("vector", lambda e: e.memset(ones_row[:], 1.0), inc=("vd", 1))
        P.bump("vd")
        P.add("sync", lambda e: e.dma_start(out=scl[:], in_=scl_d), inc=("dq", 16))
        P.bump("dq")

        STAGES = [("B", "w1", "w2", 8, 32, xnb, hB),
                  ("C", "h1", None, 32, 32, hB, hC),
                  ("D", "h2", None, 32, 32, hC, hB),
                  ("E", "w3", None, 32, 8, hB, yv)]

        for p in range(NPASS):
            P.add("sync",
                  (lambda p=p: (lambda e: e.dma_start(
                      out=xf.rearrange("q (c t) -> q c t", c=8),
                      in_=xin[:, :, p * TP:(p + 1) * TP])))(),
                  waits=([("vd", P.cnt.get("vd_pass_end", 0))] if p > 0 else []),
                  inc=("dq", 16))
            xldv = P.bump("dq") * 16
            emit_ln_stats(P, xf, 8, TP, ones_col, xsq, mrow, rrow, srow,
                          ps[0][0:1, :], ps[1][0:1, :], eps_col,
                          extra_wait=[("dq", xldv), ("sd", P.val("sd"))])
            emit_bcast_row(P, mrow, ps[2], TP, ones_row, wait=[("vd", P.val("vd"))])
            emit_bcast_row(P, rrow, ps[3], TP, ones_row, wait=[])
            bcast_pe = P.val("pe")
            for ic in range(8):
                P.add("vector",
                      (lambda ic=ic: (lambda e: e.tensor_tensor(
                          out=xsq[0][:, 0:TP], in0=xf[:, ic * TP:(ic + 1) * TP],
                          in1=ps[2][:, 0:TP], op=ALU.subtract)))(),
                      waits=([("pe", bcast_pe)] if ic == 0 else []))
                P.add("vector",
                      (lambda ic=ic: (lambda e: e.tensor_tensor(
                          out=xnb[:, ic * TP:(ic + 1) * TP], in0=xsq[0][:, 0:TP],
                          in1=ps[3][:, 0:TP], op=ALU.mult)))(),
                      inc=(("vd", 1) if ic == 7 else None))
            xnb_vd = P.bump("vd")

            for nm, wA, wB, nic, noc, src, dst in STAGES:
                for oc in range(noc):
                    ibuf = oc % 2
                    wg = []
                    widx = P.cnt.get("wgroup_idx", 0)
                    if widx >= 2:
                        wg.append(("pe", P.cnt[f"wbuf_pe_{widx - 2}"]))
                    P.add("sync",
                          (lambda wA=wA, oc=oc, ibuf=ibuf, nic=nic: (lambda e: e.dma_start(
                              out=wbuf[ibuf][:, 0:nic * 128].rearrange(
                                  "q (c o) -> q c o", c=nic),
                              in_=wdr[wA][oc])))(),
                          waits=wg, inc=("dq", 16))
                    P.bump("dq")
                    if wB is not None:
                        P.add("sync",
                              (lambda wB=wB, oc=oc, ibuf=ibuf, nic=nic: (lambda e: e.dma_start(
                                  out=wbuf[ibuf][:, 32 * 128:(32 + nic) * 128].rearrange(
                                      "q (c o) -> q c o", c=nic),
                                  in_=wdr[wB][oc])))(),
                              inc=("dq", 16))
                        P.bump("dq")
                    wd_here = P.val("dq") * 16

                    psA = ps[oc % 2]
                    psB = ps[2 + oc % 2]
                    mmw = [("dq", wd_here)]
                    if f"sd_at_{nm}_{oc - 2}" in P.cnt:
                        mmw.append(("sd", P.cnt[f"sd_at_{nm}_{oc - 2}"]))
                    if nm == "B":
                        if f"vd_at_{nm}_{oc - 2}" in P.cnt:
                            mmw.append(("vd", P.cnt[f"vd_at_{nm}_{oc - 2}"]))
                        if oc in (0, 1):
                            mmw.append(("vd", xnb_vd))
                    if nm == "C" and oc in (0, 1):
                        mmw.append(("vd", P.cnt[f"vd_at_B_31"]))
                    if nm == "D" and oc in (0, 1):
                        mmw.append(("sd", P.cnt[f"sd_at_C_31"]))
                    if nm == "E" and oc in (0, 1):
                        mmw.append(("sd", P.cnt[f"sd_at_D_31"]))
                    NT5 = TP // 512
                    for s5 in range(NT5):
                        for ic in range(nic):
                            last = (s5 == NT5 - 1) and (ic == nic - 1) and (wB is None)
                            P.add("tensor",
                                  (lambda ic=ic, s5=s5, nic=nic, psA=psA, src=src, ibuf=ibuf:
                                   (lambda e: e.matmul(
                                       out=psA[:, s5 * 512:(s5 + 1) * 512],
                                       lhsT=wbuf[ibuf][:, ic * 128:(ic + 1) * 128],
                                       rhs=src[:, ic * TP + s5 * 512:ic * TP + (s5 + 1) * 512],
                                       start=(ic == 0), stop=(ic == nic - 1))))(),
                                  waits=(mmw if (ic == 0 and s5 == 0) else []),
                                  inc=(("pe", 1) if last else None))
                            mmw = []
                    if wB is not None:
                        for s5 in range(NT5):
                            for ic in range(nic):
                                last = (s5 == NT5 - 1) and (ic == nic - 1)
                                P.add("tensor",
                                      (lambda ic=ic, s5=s5, nic=nic, psB=psB, src=src, ibuf=ibuf:
                                       (lambda e: e.matmul(
                                           out=psB[:, s5 * 512:(s5 + 1) * 512],
                                           lhsT=wbuf[ibuf][:, (32 + ic) * 128:(33 + ic) * 128],
                                           rhs=src[:, ic * TP + s5 * 512:ic * TP + (s5 + 1) * 512],
                                           start=(ic == 0), stop=(ic == nic - 1))))(),
                                      inc=(("pe", 1) if last else None))
                    pe_now = P.bump("pe")
                    P.cnt[f"wbuf_pe_{widx}"] = pe_now
                    P.cnt["wgroup_idx"] = widx + 1

                    if nm == "B":
                        P.add("scalar",
                              (lambda oc=oc, psA=psA: (lambda e: e.activation(
                                  out=gtile[:, (oc % 2) * TP:(oc % 2 + 1) * TP],
                                  in_=psA[:, 0:TP], func=AF.Silu,
                                  scale=scl[:, SCL_OFF["w1"] + oc: SCL_OFF["w1"] + oc + 1])))(),
                              waits=[("pe", pe_now)], inc=("sd", 1))
                        sdv = P.bump("sd")
                        P.cnt[f"sd_at_{nm}_{oc}"] = sdv
                        P.add("vector",
                              (lambda oc=oc, psB=psB, dst=dst: (lambda e: e.scalar_tensor_tensor(
                                  out=dst[:, oc * TP:(oc + 1) * TP], in0=psB[:, 0:TP],
                                  scalar=scl[:, SCL_OFF["w2"] + oc: SCL_OFF["w2"] + oc + 1],
                                  in1=gtile[:, (oc % 2) * TP:(oc % 2 + 1) * TP],
                                  op0=ALU.mult, op1=ALU.mult)))(),
                              waits=[("sd", sdv)], inc=("vd", 1))
                        P.cnt[f"vd_at_{nm}_{oc}"] = P.bump("vd")
                    elif nm in ("C", "D"):
                        P.add("scalar",
                              (lambda oc=oc, psA=psA, dst=dst, wA=wA: (lambda e: e.activation(
                                  out=dst[:, oc * TP:(oc + 1) * TP], in_=psA[:, 0:TP],
                                  func=AF.Gelu,
                                  scale=scl[:, SCL_OFF[wA] + oc: SCL_OFF[wA] + oc + 1])))(),
                              waits=[("pe", pe_now)], inc=("sd", 1))
                        P.cnt[f"sd_at_{nm}_{oc}"] = P.bump("sd")
                    else:
                        P.add("scalar",
                              (lambda oc=oc, psA=psA, dst=dst: (lambda e: e.activation(
                                  out=dst[:, oc * TP:(oc + 1) * TP], in_=psA[:, 0:TP],
                                  func=AF.Copy,
                                  scale=scl[:, SCL_OFF["w3"] + oc: SCL_OFF["w3"] + oc + 1])))(),
                              waits=[("pe", pe_now)], inc=("sd", 1))
                        P.cnt[f"sd_at_{nm}_{oc}"] = P.bump("sd")

            emit_ln_stats(P, yv, 8, TP, ones_col, xsq, mrow, rrow, srow,
                          ps[0][0:1, :], ps[1][0:1, :], eps_col,
                          extra_wait=[("sd", P.val("sd"))])
            emit_bcast_row(P, mrow, ps[2], TP, ones_row, wait=[("vd", P.val("vd"))])
            emit_bcast_row(P, rrow, ps[3], TP, ones_row, wait=[])
            bpe = P.val("pe")
            for ic in range(8):
                P.add("vector",
                      (lambda ic=ic: (lambda e: e.tensor_tensor(
                          out=xsq[0][:, 0:TP], in0=yv[:, ic * TP:(ic + 1) * TP],
                          in1=ps[2][:, 0:TP], op=ALU.subtract)))(),
                      waits=([("pe", bpe)] if ic == 0 else []))
                P.add("vector",
                      (lambda ic=ic: (lambda e: e.tensor_tensor(
                          out=xsq[1][:, 0:TP], in0=xsq[0][:, 0:TP],
                          in1=ps[3][:, 0:TP], op=ALU.mult)))())
                P.add("vector",
                      (lambda ic=ic: (lambda e: e.tensor_tensor(
                          out=yv[:, ic * TP:(ic + 1) * TP], in0=xsq[1][:, 0:TP],
                          in1=xf[:, ic * TP:(ic + 1) * TP], op=ALU.add)))(),
                      inc=(("vd", 1) if ic == 7 else None))
            vout = P.bump("vd")
            P.cnt["vd_pass_end"] = vout
            P.add("sync",
                  (lambda p=p: (lambda e: e.dma_start(
                      out=xout[:, :, p * TP:(p + 1) * TP],
                      in_=yv.rearrange("q (c t) -> q c t", c=8))))(),
                  waits=[("vd", vout)], inc=("dq", 16))
            P.bump("dq")

        P.add("sync", None, waits=[("dq", P.val("dq") * 16)])
        P.run_all(block, sems)

    return nc


def _prep_l0ffn_weights(inputs):
    w = {}
    w["w1"] = _chunk_w(inputs["L0_ffn_w1_w"].T.astype(BF))
    w["w2"] = _chunk_w(inputs["L0_ffn_w2_w"].T.astype(BF))
    w["h1"] = _chunk_w(inputs["L0_ffn_h1_w"].T.astype(BF))
    w["h2"] = _chunk_w(inputs["L0_ffn_h2_w"].T.astype(BF))
    w["w3"] = _chunk_w(inputs["L0_ffn_w3_w"].T.astype(BF))
    scl = np.zeros((128, 136), np.float32)
    scl[:, 0:32] = inputs["L0_ffn_w1_s"].reshape(32, 128).T
    scl[:, 32:64] = inputs["L0_ffn_w2_s"].reshape(32, 128).T
    scl[:, 64:96] = inputs["L0_ffn_h1_s"].reshape(32, 128).T
    scl[:, 96:128] = inputs["L0_ffn_h2_s"].reshape(32, 128).T
    scl[:, 128:136] = inputs["L0_ffn_w3_s"].reshape(8, 128).T
    w["scl"] = scl
    return w


def launch_l0ffn(x1, inputs):
    nc = build_l0ffn()
    w = _prep_l0ffn_weights(inputs)
    xfm = _feat_major(x1.reshape(T_ALL, D))
    in_maps = []
    for c in range(NCORES):
        m = dict(w)
        m["xin"] = np.ascontiguousarray(xfm[:, :, c * T_CORE:(c + 1) * T_CORE])
        in_maps.append(m)
    res = run_bass_kernel_spmd(nc, in_maps, core_ids=list(range(NCORES)))
    out = np.concatenate([r["xout"] for r in res.results], axis=2)
    return _unfeat_major(out).reshape(B, S, D)


def kernel(**inputs):
    x0 = np.asarray(inputs["L0_attn_q"], np.float32)
    x1 = launch_l0attn(x0, inputs)
    x2 = launch_l0ffn(x1, inputs)
    x3 = launch_l1attn(x2, inputs)
    x4 = launch_l1ffn(x3, inputs)
    return np.asarray(x4, np.float32).reshape(2, S, D)



def _alibi_slopes16():
    start = 2.0 ** (-2.0 ** (-(np.log2(16) - 3)))
    return np.array([start * start ** (i) for i in range(16)], dtype=np.float64)


def build_l0attn_core(qsl, nhp=8, debug=False):
    slopes = _alibi_slopes16()
    nc = bass.Bass("TRN2", target_bir_lowering=False, debug=False)
    xb = nc.dram_tensor("xb", [128, 8, S], F32, kind="ExternalInput").ap()
    xout = nc.dram_tensor("xout", [128, 8, T_CORE], F32, kind="ExternalOutput").ap()
    wq_d = nc.dram_tensor("wq", [8, 128, 8, 128], F32, kind="ExternalInput").ap()
    wk_d = nc.dram_tensor("wk", [8, 128, 8, 128], F32, kind="ExternalInput").ap()
    wv_d = nc.dram_tensor("wv", [8, 128, 8, 128], F32, kind="ExternalInput").ap()
    wo_d = nc.dram_tensor("wo", [8, 128, 8, 128], F32, kind="ExternalInput").ap()
    bo_d = nc.dram_tensor("bo", [128, 8], F32, kind="ExternalInput").ap()
    bcol_d = nc.dram_tensor("bcol", [128, 16, 256], F32, kind="ExternalInput").ap()
    dtile_d = nc.dram_tensor("dtile", [4, 128, 512], F32, kind="ExternalInput").ap()
    qramp_d = nc.dram_tensor("qramp", [8, 2048], BF16, kind="ExternalInput").ap()
    kaug_d = nc.dram_tensor("kaug", [16, 8, S], BF16, kind="ExternalInput").ap()
    xnd = nc.dram_tensor("xnd", [128, 8, S], BF16, kind="Internal").ap()
    if debug:
        dbg_xn = nc.dram_tensor("dbg_xn", [128, 8, 2048], BF16, kind="ExternalOutput").ap()
        dbg_on = nc.dram_tensor("dbg_on", [128, 8, 2048], BF16, kind="ExternalOutput").ap()
        dbg_y = nc.dram_tensor("dbg_y", [128, 8, 1024], F32, kind="ExternalOutput").ap()
        dbg_st = nc.dram_tensor("dbg_st", [1, 2048], F32, kind="ExternalOutput").ap()
        dbg_y2 = nc.dram_tensor("dbg_y2", [128, 8, 1024], F32, kind="ExternalOutput").ap()
        dbg_x0 = nc.dram_tensor("dbg_x0", [128, 8, 1024], F32, kind="ExternalOutput").ap()

    with ExitStack() as ctx:
        sb = lambda n, shp, dt=F32: ctx.enter_context(nc.sbuf_tensor(n, shp, dt))
        bigA = sb("bigA", [128, 20544])
        xblk = bigA
        kt = [bigA[0:72, 0:4096].bitcast(BF16), bigA[0:72, 4096:8192].bitcast(BF16)]
        vt = [bigA[:, 8192:10272].bitcast(BF16), bigA[:, 10272:12352].bitcast(BF16)]
        xnkb2 = [bigA[:, 12352:16448].bitcast(BF16), bigA[:, 16448:20544].bitcast(BF16)]
        xnown = sb("xnown", [128, 8 * 2048], BF16)
        yvh = xnown.bitcast(F32)
        qthp = sb("qthp", [128, 4096])
        qt = [qthp[0:72, 0:1024].bitcast(BF16), qthp[0:72, 1024:2048].bitcast(BF16)]
        xsq = [qthp[:, 0:1024], qthp[:, 1024:2048]]
        xsq4 = xsq
        onorm = sb("onorm", [128, 8 * 2048], BF16)
        pbuf = [sb(f"pbuf{i}", [128, 512], BF16) for i in range(2)]
        wc32 = sb("wc32", [128, 8 * 128])
        wcb = sb("wcb", [128, 8 * 128], BF16)
        wcb2 = sb("wcb2", [128, 8 * 128], BF16)
        bcol = sb("bcol_sb", [128, 2 * 4 * 64])
        dts = sb("dts", [128, 512])
        osb = sb("osb", [65, 512])
        dnr = sb("dnr", [1, 512])
        ones_col = sb("ones_col", [128, 1])
        ones_row = sb("ones_row", [1, 128])
        eps_col = sb("eps_col", [128, 1])
        bo_sb = sb("bo_sb", [128, 8])
        mrow = sb("mrow", [1, 1024])
        rrow = sb("rrow", [1, 1024])
        srow = sb("srow", [1, 1024])

        pp = [ctx.enter_context(nc.psum_tensor(f"pp{i}", [128, 512], F32)) for i in range(2)]
        po = [ctx.enter_context(nc.psum_tensor(f"po{i}", [65, 512], F32)) for i in range(4)]
        pb = ctx.enter_context(nc.psum_tensor("pb", [128, 1024], F32))
        sems = {}
        for s_ in ["dq", "pe", "sd", "vd", "vch", "sch", "gch"]:
            sems[s_] = ctx.enter_context(nc.semaphore("sem_" + s_))
        block = ctx.enter_context(nc.Block())
        P = Plan()

        P.add("vector", lambda e: e.memset(ones_col[:], 1.0))
        P.add("vector", lambda e: e.memset(ones_row[:], 1.0))
        P.add("vector", lambda e: e.memset(eps_col[:], 1e-5))
        P.add("vector", lambda e: e.memset(onorm[:], 0.0), inc=("vd", 1))
        P.bump("vd")
        P.add("sync", lambda e: e.dma_start(out=bo_sb[:], in_=bo_d), inc=("dq", 16))
        P.bump("dq")

        for tb in range(4):
            P.add("sync", (lambda tb=tb: (lambda e: e.dma_start(
                out=xblk[:, 0:16384].rearrange("q (c t) -> q c t", c=8),
                in_=xb[:, :, tb * 2048:(tb + 1) * 2048])))(),
                waits=([("dq", P.val("dq") * 16)] if tb else []), inc=("dq", 16))
            xldv = P.bump("dq") * 16
            for half in range(2):
                off = half * 1024
                emit_ln_stats(P, xblk, 8, 1024, ones_col, xsq, mrow, rrow, srow,
                              pb[0:1, :], pb[32:33, :], eps_col,
                              [("dq", xldv), ("sd", P.val("sd")), ("vd", P.val("vd"))],
                              CS=2048, off=off)
                statvd = P.val("vd")
                for q5 in range(2):
                    emit_bcast_row(P, mrow[:, q5 * 512:(q5 + 1) * 512], pb[:, 0:512],
                                   512, ones_row, [("vd", P.val("vd"))])
                    emit_bcast_row(P, rrow[:, q5 * 512:(q5 + 1) * 512], pb[:, 512:1024],
                                   512, ones_row, [])
                    bpe = P.val("pe")
                    for ic in range(8):
                        o2 = ic * 2048 + off + q5 * 512
                        P.add("vector", (lambda o2=o2: (lambda e: e.tensor_tensor(
                            out=xsq[0][:, 0:512], in0=xblk[:, o2:o2 + 512],
                            in1=pb[:, 0:512], op=ALU.subtract)))(),
                            waits=([("pe", bpe), ("dq", P.val("dq") * 16)] if ic == 0 else []))
                        P.add("vector", (lambda o2=o2: (lambda e: e.tensor_tensor(
                            out=xnown[:, o2:o2 + 512], in0=xsq[0][:, 0:512],
                            in1=pb[:, 512:1024], op=ALU.mult)))(),
                            inc=(("vd", 1) if ic == 7 else None))
                    P.bump("vd")
            xnvd = P.val("vd")
            P.add("sync", (lambda tb=tb: (lambda e: e.dma_start(
                out=xnd[:, :, tb * 2048:(tb + 1) * 2048],
                in_=xnown.rearrange("q (c t) -> q c t", c=8))))(),
                waits=[("vd", xnvd)], inc=("dq", 16))
            P.bump("dq")
        P.add("sync", (lambda: (lambda e: e.dma_start(
            out=xnown.rearrange("q (c t) -> q c t", c=8),
            in_=xnd[:, :, qsl:qsl + 2048])))(),
            waits=[("dq", P.val("dq") * 16)], inc=("dq", 16))
        xnown_ld = P.bump("dq") * 16

        QBS = [qsl + i * 512 for i in range(4)]

        for hp in range(nhp):
            flash_prev_pe = P.cnt.get(f"flashend_{hp - 1}", 0)
            for wnm, wdram, dstc in [("wq", wq_d, wcb), ("wk", wk_d, wcb), ("wv", wv_d, wcb2)]:
                P.add("sync", (lambda wdram=wdram, hp=hp: (lambda e: e.dma_start(
                    out=wc32.rearrange("q (c o) -> q c o", c=8), in_=wdram[hp])))(),
                    waits=[("vd", P.cnt.get("cast_vd", 0))], inc=("dq", 16))
                ldv = P.bump("dq") * 16
                wkey = "wcb_free" if dstc is wcb else "wcb2_free"
                P.add("vector", (lambda dstc=dstc: (lambda e: e.tensor_copy(
                    dstc[:], wc32[:])))(),
                    waits=[("dq", ldv), ("pe", P.cnt.get(wkey, 0))],
                    inc=("vd", 1))
                P.cnt["cast_vd"] = P.bump("vd")
                P.cnt[f"castvd_{wnm}"] = P.val("vd")
                if wnm == "wq":
                    for t5 in range(4):
                        w0 = [("vd", P.val("vd")), ("dq", xnown_ld),
                              ("sd", P.val("sd"))]
                        if hp > 0:
                            w0.append(("pe", flash_prev_pe))
                        for ic in range(8):
                            P.add("tensor", (lambda t5=t5, ic=ic: (lambda e: e.matmul(
                                out=pb[:, 0:512], lhsT=wcb[:, ic * 128:(ic + 1) * 128],
                                rhs=xnown[:, ic * 2048 + t5 * 512:ic * 2048 + (t5 + 1) * 512],
                                start=(ic == 0), stop=(ic == 7))))(),
                                waits=(w0 if ic == 0 else []),
                                inc=(("pe", 1) if ic == 7 else None))
                        penow = P.bump("pe")
                        P.cnt["wcb_free"] = penow
                        for h2 in range(2):
                            P.add("scalar", (lambda h2=h2, t5=t5: (lambda e: e.activation(
                                out=qt[h2][0:64, t5 * 512:(t5 + 1) * 512],
                                in_=pb[h2 * 64:(h2 + 1) * 64, 0:512],
                                func=AF.Copy, scale=0.125)))(),
                                waits=[("pe", penow)], inc=("sd", 1))
                            P.bump("sd")
                    for h2 in range(2):
                        P.add("sync", (lambda h2=h2: (lambda e: e.dma_start(
                            out=qt[h2][64:72, :], in_=qramp_d)))(),
                            waits=[("pe", flash_prev_pe)], inc=("dq", 16))
                        P.bump("dq")
            P.add("sync", (lambda hp=hp: (lambda e: e.dma_start(
                out=bcol.rearrange("q (h c) -> q h c", h=2),
                in_=bcol_d[:, 2 * hp:2 * hp + 2, :])))(),
                waits=[("sd", P.cnt.get(f"expend_{hp - 1}", 0))], inc=("dq", 16))
            bcol_ld = P.bump("dq") * 16
            for h2 in range(2):
                P.add("sync", (lambda h2=h2, hp=hp: (lambda e: e.dma_start(
                    out=kt[h2][64:72, :], in_=kaug_d[2 * hp + h2])))(),
                    waits=[("pe", flash_prev_pe)], inc=("dq", 16))
                P.bump("dq")
                P.add("vector", (lambda h2=h2: (lambda e: e.memset(
                    vt[h2].rearrange("q (k c) -> q k c", c=65)[:, :, 64:65], 1.0)))(),
                    waits=[("pe", flash_prev_pe)], inc=("vd", 1))
                P.cnt["cast_vd"] = P.bump("vd")
            memset_vd = P.val("vd")
            kaug_ld = P.val("dq") * 16

            for kb in range(8):
                P.add("sync", (lambda kb=kb: (lambda e: e.dma_start(
                    out=xnkb2[kb % 2].rearrange("q (c t) -> q c t", c=8),
                    in_=xnd[:, :, kb * 1024:(kb + 1) * 1024])))(),
                    waits=[("pe", P.cnt.get(f"kvproj_{hp}_{kb - 2}", 0))], inc=("dq", 16))
                kbld = P.bump("dq") * 16
                for t5 in range(2):
                    w0 = [("dq", kbld), ("vd", memset_vd), ("sd", P.val("sd"))]
                    for ic in range(8):
                        P.add("tensor", (lambda t5=t5, ic=ic, kb=kb: (lambda e: e.matmul(
                            out=pb[:, 0:512], lhsT=wcb[:, ic * 128:(ic + 1) * 128],
                            rhs=xnkb2[kb % 2][:, ic * 1024 + t5 * 512:ic * 1024 + (t5 + 1) * 512],
                            start=(ic == 0), stop=(ic == 7))))(),
                            waits=(w0 if ic == 0 else []),
                            inc=(("pe", 1) if ic == 7 else None))
                        w0 = []
                    penow = P.bump("pe")
                    P.cnt["wcb_free"] = penow
                    for h2 in range(2):
                        P.add("scalar", (lambda h2=h2, kb=kb, t5=t5: (lambda e: e.activation(
                            out=kt[h2][0:64, kb * 1024 + t5 * 512:kb * 1024 + (t5 + 1) * 512],
                            in_=pb[h2 * 64:(h2 + 1) * 64, 0:512], func=AF.Copy)))(),
                            waits=[("pe", penow)], inc=("sd", 1))
                        P.bump("sd")
                for kc8 in range(8):
                    reg = 512 if kc8 % 2 == 0 else 0
                    w0 = [("sd", P.cnt.get(f"vproj_sd_{hp}_{kb}_{kc8 - 2}", P.cnt.get(f"vproj_base_{hp}_{kb}", 0)))]
                    if kc8 == 0:
                        P.cnt[f"vproj_base_{hp}_{kb}"] = P.val("sd")
                        w0 = [("sd", P.val("sd"))]
                    for ic in range(8):
                        P.add("tensor", (lambda kc8=kc8, ic=ic, reg=reg, kb=kb: (lambda e: e.matmul(
                            out=pb[:, reg:reg + 128],
                            lhsT=xnkb2[kb % 2][:, ic * 1024 + kc8 * 128:ic * 1024 + (kc8 + 1) * 128],
                            rhs=wcb2[:, ic * 128:(ic + 1) * 128],
                            start=(ic == 0), stop=(ic == 7))))(),
                            waits=(w0 if ic == 0 else []),
                            inc=(("pe", 1) if ic == 7 else None))
                        w0 = []
                    penow = P.bump("pe")
                    P.cnt["wcb2_free"] = penow
                    kcg = kb * 8 + kc8
                    for h2 in range(2):
                        P.add("scalar", (lambda h2=h2, kcg=kcg, reg=reg: (lambda e: e.activation(
                            out=vt[h2][:, kcg * 65:kcg * 65 + 64],
                            in_=pb[:, reg + h2 * 64:reg + (h2 + 1) * 64], func=AF.Copy)))(),
                            waits=[("pe", penow)], inc=("sd", 1))
                        P.cnt[f"vproj_sd_{hp}_{kb}_{kc8}"] = P.bump("sd")
                P.cnt[f"kvproj_{hp}_{kb}"] = P.val("pe")
            proj_sd = P.val("sd")

            for qbh in range(2):
                pairs = [(h2, qbh * 2 + qb2) for h2 in range(2) for qb2 in range(2)]
                ntile = 0
                tile_info = {}
                for kc in range(64):
                    for i4, (h2, qbg) in enumerate(pairs):
                        n = ntile
                        ntile += 1
                        q0g = QBS[qbg]
                        d0 = q0g - 128 * kc
                        if d0 >= 128:
                            kind, rows = "L", 68
                        elif d0 <= -512:
                            kind, rows = "R", 72
                        else:
                            kind, rows = "S", 64
                        h = 2 * hp + h2
                        bcidx = h2 * 256 + qbg * 64 + kc
                        w1 = [("sd", P.cnt.get(f"exp_{hp}_{qbh}_{n - 2}", 0))]
                        if n < 4:
                            w1.append(("sd", P.val("sd")))
                            w1.append(("dq", bcol_ld))
                            w1.append(("dq", kaug_ld))
                            w1.append(("vd", P.cnt.get(f"fin_{hp}_{qbh - 1}", P.cnt.get(f"fin_{hp - 1}_1", 0))))
                        P.add("tensor", (lambda h2=h2, qbg=qbg, kc=kc, rows=rows, n=n: (lambda e: e.matmul(
                            out=pp[n % 2][:, :], lhsT=kt[h2][0:rows, kc * 128:(kc + 1) * 128],
                            rhs=qt[h2][0:rows, (qbg % 4) * 512:(qbg % 4) * 512 + 512],
                            start=True, stop=True)))(),
                            waits=w1, inc=("pe", 1))
                        mm1pe = P.bump("pe")
                        expw = [("pe", mm1pe)]
                        if kind == "S":
                            didx = -d0 // 128
                            P.add("sync", (lambda didx=didx: (lambda e: e.dma_start(
                                out=dts[:], in_=dtile_d[didx])))(),
                                waits=[("vd", P.cnt.get("dts_vd", 0))], inc=("dq", 16))
                            dldv = P.bump("dq") * 16
                            sl = float(slopes[h])
                            P.add("vector", (lambda n=n, sl=sl: (lambda e: e.scalar_tensor_tensor(
                                out=pp[n % 2][:, :], in0=dts[:], scalar=-sl,
                                in1=pp[n % 2][:, :], op0=ALU.mult, op1=ALU.add)))(),
                                waits=[("pe", mm1pe), ("dq", dldv)], inc=("vd", 1))
                            P.cnt["dts_vd"] = P.bump("vd")
                            expw.append(("vd", P.val("vd")))
                        expw.append(("pe", P.cnt.get(f"av_{hp}_{qbh}_{n - 2}", 0)))
                        P.add("scalar", (lambda n=n, bcidx=bcidx: (lambda e: e.activation(
                            out=pbuf[n % 2][:, :], in_=pp[n % 2][:, :], func=AF.Exp,
                            bias=bcol[:, bcidx:bcidx + 1])))(),
                            waits=expw, inc=("sd", 1))
                        P.cnt[f"exp_{hp}_{qbh}_{n}"] = P.bump("sd")
                        P.add("tensor", (lambda h2=h2, kc=kc, n=n, i4=i4: (lambda e: e.matmul(
                            out=po[i4][:, :], lhsT=vt[h2][:, kc * 65:(kc + 1) * 65],
                            rhs=pbuf[n % 2][:, :],
                            start=(kc == 0), stop=(kc == 63))))(),
                            waits=[("sd", P.cnt[f"exp_{hp}_{qbh}_{n}"])], inc=("pe", 1))
                        P.cnt[f"av_{hp}_{qbh}_{n}"] = P.bump("pe")
                P.cnt[f"expend_{hp}"] = P.val("sd")
                for i4, (h2, qbg) in enumerate(pairs):
                    lastav = P.cnt[f"av_{hp}_{qbh}_{(63 * 4) + i4}"]
                    P.add("scalar", (lambda i4=i4: (lambda e: e.copy(
                        osb[:], po[i4][:, :])))(),
                        waits=[("pe", lastav), ("vd", P.cnt.get("fin_busy", 0))], inc=("sd", 1))
                    sdn = P.bump("sd")
                    P.add("vector", lambda e: e.reciprocal(out=dnr[:], in_=osb[64:65, :]),
                          waits=[("sd", sdn)], inc=("vd", 1))
                    vdn = P.bump("vd")
                    P.add("tensor", lambda e: e.matmul(
                        out=pp[0][0:64, :], lhsT=ones_row[0:1, 0:64], rhs=dnr[:],
                        start=True, stop=True),
                        waits=[("vd", vdn), ("sd", P.cnt.get("fin_sd2", 0))], inc=("pe", 1))
                    pen = P.bump("pe")
                    h = 2 * hp + h2
                    P.add("vector", (lambda h2=h2, hp=hp, qbg=qbg: (lambda e: e.tensor_tensor(
                        out=onorm[h2 * 64:(h2 + 1) * 64,
                                  hp * 2048 + qbg * 512:hp * 2048 + (qbg + 1) * 512],
                        in0=osb[0:64, :], in1=pp[0][0:64, :], op=ALU.mult)))(),
                        waits=[("pe", pen)], inc=("vd", 1))
                    P.cnt["fin_busy"] = P.bump("vd")
                    P.cnt["fin_sd2"] = P.val("sd")
                P.cnt[f"fin_{hp}_{qbh}"] = P.val("vd")
            P.cnt[f"flashend_{hp}"] = P.val("pe")

        onorm_vd = P.val("vd")
        if debug:
            P.add("sync", lambda e: e.dma_start(
                out=dbg_xn, in_=xnown.rearrange("q (c t) -> q c t", c=8)),
                waits=[("vd", onorm_vd)], inc=("dq", 16))
            P.bump("dq")
            P.add("sync", lambda e: e.dma_start(
                out=dbg_on, in_=onorm.rearrange("q (c t) -> q c t", c=8)),
                inc=("dq", 16))
            P.bump("dq")
        for th in range(2):
            toff = th * 1024
            for oc in range(8):
                P.add("sync", (lambda oc=oc: (lambda e: e.dma_start(
                    out=wc32.rearrange("q (c o) -> q c o", c=8), in_=wo_d[oc])))(),
                    waits=[("pe", P.cnt.get("womm_prev", 0))], inc=("dq", 16))
                ldv = P.bump("dq") * 16
                P.add("vector", lambda e: e.tensor_copy(wcb[:], wc32[:]),
                      waits=[("dq", ldv)], inc=("vd", 1))
                cvd = P.bump("vd")
                for t5 in range(2):
                    w0 = [("vd", cvd), ("vd", onorm_vd),
                          ("sd", P.cnt.get(f"wo_sd_{th}_{oc - 1}_{t5}", P.cnt.get("wo_base", 0)))]
                    if oc == 0:
                        w0 += [("vd", P.val("vd")), ("sd", P.val("sd")),
                               ("dq", P.val("dq") * 16)]
                    for ic in range(8):
                        P.add("tensor", (lambda oc=oc, t5=t5, ic=ic, toff=toff: (lambda e: e.matmul(
                            out=pb[:, (t5 % 2) * 512:(t5 % 2) * 512 + 512],
                            lhsT=wcb[:, ic * 128:(ic + 1) * 128],
                            rhs=onorm[:, ic * 2048 + toff + t5 * 512:ic * 2048 + toff + (t5 + 1) * 512],
                            start=(ic == 0), stop=(ic == 7))))(),
                            waits=(w0 if ic == 0 else []),
                            inc=(("pe", 1) if ic == 7 else None))
                        w0 = []
                    penow = P.bump("pe")
                    P.cnt["womm_prev"] = penow
                    P.add("scalar", (lambda oc=oc, t5=t5: (lambda e: e.activation(
                        out=yvh[:, oc * 1024 + t5 * 512:oc * 1024 + (t5 + 1) * 512],
                        in_=pb[:, (t5 % 2) * 512:(t5 % 2) * 512 + 512],
                        func=AF.Identity, bias=bo_sb[:, oc:oc + 1])))(),
                        waits=[("pe", penow)], inc=("sd", 1))
                    P.cnt[f"wo_sd_{th}_{oc}_{t5}"] = P.bump("sd")
            if debug and th == 1:
                P.add("sync", lambda e: e.dma_start(
                    out=dbg_y, in_=yvh.rearrange("q (c t) -> q c t", c=8)),
                    waits=[("sd", P.val("sd"))], inc=("dq", 16))
                P.bump("dq")
            emit_ln_stats(P, yvh, 8, 1024, ones_col, xsq4, mrow, rrow, srow,
                          pb[0:1, :], pb[32:33, :], eps_col,
                          [("sd", P.val("sd"))], CS=1024, off=0)
            statvd = P.val("vd")
            if debug and th == 1:
                P.add("sync", lambda e: e.dma_start(out=dbg_st[:, 0:1024], in_=mrow[:]),
                      waits=[("vd", statvd)], inc=("dq", 16))
                P.bump("dq")
                P.add("sync", lambda e: e.dma_start(out=dbg_st[:, 1024:2048], in_=rrow[:]),
                      inc=("dq", 16))
                P.bump("dq")
            P.add("sync", (lambda toff=toff: (lambda e: e.dma_start(
                out=xblk[:, 0:8192].rearrange("q (c t) -> q c t", c=8),
                in_=xb[:, :, qsl + toff:qsl + toff + 1024])))(),
                waits=[("pe", P.cnt.get(f"flashend_7", 0))], inc=("dq", 16))
            x0ld = P.bump("dq") * 16
            for q5 in range(2):
                emit_bcast_row(P, mrow[:, q5 * 512:(q5 + 1) * 512], pb[:, 0:512],
                               512, ones_row, [("vd", P.val("vd"))])
                emit_bcast_row(P, rrow[:, q5 * 512:(q5 + 1) * 512], pb[:, 512:1024],
                               512, ones_row, [])
                bpe = P.val("pe")
                for ic in range(8):
                    o2 = ic * 1024 + q5 * 512
                    P.add("vector", (lambda o2=o2: (lambda e: e.tensor_tensor(
                        out=xsq4[0][:, 0:512], in0=yvh[:, o2:o2 + 512],
                        in1=pb[:, 0:512], op=ALU.subtract)))(),
                        waits=([("pe", bpe), ("dq", x0ld)] if ic == 0 else []))
                    P.add("vector", (lambda o2=o2: (lambda e: e.tensor_tensor(
                        out=xsq4[1][:, 0:512], in0=xsq4[0][:, 0:512],
                        in1=pb[:, 512:1024], op=ALU.mult)))())
                    P.add("vector", (lambda o2=o2: (lambda e: e.tensor_tensor(
                        out=yvh[:, o2:o2 + 512], in0=xsq4[1][:, 0:512],
                        in1=xblk[:, o2:o2 + 512], op=ALU.add)))(),
                        inc=(("vd", 1) if ic == 7 else None))
                P.bump("vd")
            vout = P.val("vd")
            if debug and th == 1:
                P.add("sync", lambda e: e.dma_start(
                    out=dbg_y2, in_=yvh.rearrange("q (c t) -> q c t", c=8)),
                    waits=[("vd", vout)], inc=("dq", 16))
                P.bump("dq")
                P.add("sync", lambda e: e.dma_start(
                    out=dbg_x0, in_=xblk[:, 0:8192].rearrange("q (c t) -> q c t", c=8)),
                    inc=("dq", 16))
                P.bump("dq")
            P.add("sync", (lambda th=th, toff=toff: (lambda e: e.dma_start(
                out=xout[:, :, toff:toff + 1024],
                in_=yvh.rearrange("q (c t) -> q c t", c=8))))(),
                waits=[("vd", vout)], inc=("dq", 16))
            P.bump("dq")

        P.add("sync", None, waits=[("dq", P.val("dq") * 16)])
        P.run_all(block, sems)
    return nc


def _prep_l0attn_inputs(inputs):
    slopes = _alibi_slopes16()
    com = {}
    com["wq"] = _chunk_w(np.ascontiguousarray(inputs["L0_attn_Wq"].T))
    com["wk"] = _chunk_w(np.ascontiguousarray(inputs["L0_attn_Wk"].T))
    com["wv"] = _chunk_w(np.ascontiguousarray(inputs["L0_attn_Wv"].T))
    com["wo"] = _chunk_w(np.ascontiguousarray(inputs["L0_attn_Wo"].T))
    com["bo"] = np.ascontiguousarray(inputs["L0_attn_bo"].reshape(8, 128).T)
    qi = np.arange(2048) % 512
    a = (qi >> 4).astype(np.float32)
    bq = (qi & 15).astype(np.float32)
    com["qramp"] = np.ascontiguousarray(
        np.stack([a, bq, a, bq, a, bq, a, bq]).astype(BF))
    ka = np.zeros((16, 8, S), np.float64)
    for h in range(16):
        sl = float(slopes[h])
        shi = float(np.float32(BF(sl)))
        slo = sl - shi
        vals = [-16 * shi, -shi, -16 * slo, -slo, 32 * shi, 2 * shi, 32 * slo, 2 * slo]
        for j, v in enumerate(vals):
            ka[h, j, :] = v
    com["kaug"] = ka.astype(BF)
    dt = np.zeros((4, 128, 512), np.float32)
    for idx in range(4):
        d0 = -128 * idx
        kp = np.arange(128)[:, None]
        qi2 = np.arange(512)[None, :]
        dt[idx] = np.abs(d0 + qi2 - kp)
    com["dtile"] = dt
    return com


def _prep_l0attn_bcol(qsl):
    slopes = _alibi_slopes16()
    bc = np.zeros((16, 4, 64, 128), np.float32)
    kp = np.arange(128)
    for h in range(16):
        sl = slopes[h]
        for qb in range(4):
            q0g = qsl + qb * 512
            for kc in range(64):
                d0 = q0g - 128 * kc
                if d0 >= 128:
                    bc[h, qb, kc] = sl * (kp - d0)
                elif d0 <= -512:
                    bc[h, qb, kc] = sl * (d0 - kp)
    return np.ascontiguousarray(bc.transpose(3, 0, 1, 2).reshape(128, 16, 256))


def launch_l0attn(x0, inputs):
    com = _prep_l0attn_inputs(inputs)
    xfm = [_feat_major(x0[b]) for b in range(B)]
    results = [None] * NCORES
    for qi4 in range(4):
        qsl = qi4 * 2048
        nc = build_l0attn_core(qsl)
        cores = [qi4, 4 + qi4]
        in_maps = []
        for c in cores:
            m = dict(com)
            m["bcol"] = _prep_l0attn_bcol(qsl)
            m["xb"] = xfm[c // 4]
            in_maps.append(m)
        res = run_bass_kernel_spmd(nc, in_maps, core_ids=cores)
        for i, c in enumerate(cores):
            results[c] = res.results[i]["xout"]
    out = np.concatenate([results[c] for c in range(NCORES)], axis=2)
    return _unfeat_major(out).reshape(B, S, D)


U32 = mybir.dt.uint32
RMS_EPS = 1.1920929e-07


def build_l1ffn():
    nc = bass.Bass("TRN2", target_bir_lowering=False, debug=False)
    xin = nc.dram_tensor("xin", [128, 8, T_CORE], F32, kind="ExternalInput").ap()
    xout = nc.dram_tensor("xout", [128, 8, T_CORE], F32, kind="ExternalOutput").ap()
    rg_d = nc.dram_tensor("rg", [8, 32, 128, 8, 128], BF16, kind="ExternalInput").ap()
    ru_d = nc.dram_tensor("ru", [8, 32, 128, 8, 128], BF16, kind="ExternalInput").ap()
    rd_d = nc.dram_tensor("rd", [8, 8, 128, 32, 128], BF16, kind="ExternalInput").ap()
    sg_d = nc.dram_tensor("sg", [64, 128, 8, 128], BF16, kind="ExternalInput").ap()
    su_d = nc.dram_tensor("su", [64, 128, 8, 128], BF16, kind="ExternalInput").ap()
    sd_d = nc.dram_tensor("sd", [8, 128, 64, 128], BF16, kind="ExternalInput").ap()
    rwt_d = nc.dram_tensor("rwt", [128, 8, 8], F32, kind="ExternalInput").ap()
    tril_d = nc.dram_tensor("tril", [128, 128], F32, kind="ExternalInput").ap()
    tcol_d = nc.dram_tensor("tcol", [8, 1], F32, kind="ExternalInput").ap()
    idn_d = nc.dram_tensor("idn", [128, 128], F32, kind="ExternalInput").ap()
    tot_bounce = nc.dram_tensor("tot_bounce", [1, 8], F32).ap()
    tot_gath = nc.dram_tensor("tot_gath", [8, 8], F32).ap()
    wtd = nc.dram_tensor("wtd", [1, 16384], F32, kind="Internal").ap()

    with ExitStack() as ctx:
        sb = lambda n, shp, dt=F32: ctx.enter_context(nc.sbuf_tensor(n, shp, dt))
        racc = sb("racc", [128, 8 * 2048])
        xb = sb("xb", [128, 8 * 2048], BF16)
        hh = sb("hh", [128, 32 * 1024], BF16)
        xsq = [hh[:, 0:2048].bitcast(F32), hh[:, 2048:4096].bitcast(F32)]
        xf3 = sb("xf3", [128, 8 * 1024])
        gt2 = [xf3[:, 0:512].bitcast(BF16), xf3[:, 512:1024].bitcast(BF16)]
        wscr = xf3[:, 1024:1536]
        wra = sb("wra", [128, 8 * 128], BF16)
        wrb = sb("wrb", [128, 8 * 128], BF16)
        wca = sb("wca", [128, 8 * 128], BF16)
        wcb_ = sb("wcb_", [128, 8 * 128], BF16)
        wrd = xf3[:, 1536:2560].bitcast(BF16)
        wcd = xf3[:, 3584:4608].bitcast(BF16)
        wbt = xf3[:, 4608:6656]
        wrow = xf3[0:1, 6656:7680]
        lgt = sb("lgt", [128, 16 * 8])
        selt = sb("selt", [128, 16 * 8])
        wdns = sb("wdns", [128, 16 * 8])
        wcols = sb("wcols", [128, 128])
        rmsc = sb("rmsc", [128, 16])
        i1f = sb("i1f", [128, 2])
        rw12 = sb("rw12", [128, 2])
        mx8 = sb("mx8", [128, 8])
        mi8 = ctx.enter_context(nc.sbuf_tensor("mi8", [128, 8], U32))
        miu = ctx.enter_context(nc.sbuf_tensor("miu", [128, 8], U32))
        scr8 = sb("scr8", [128, 24])
        eidx = sb("eidx", [128, 8])
        trow = sb("trow", [1, 8])
        brow = sb("brow", [1, 8])
        orow = sb("orow", [1, 8])
        crow = sb("crow", [1, 8])
        tg8 = sb("tg8", [8, 8])
        tril = hh[:, 14336:14592].bitcast(F32)
        tcol = sb("tcol_sb", [8, 1])
        idn = hh[:, 14592:14848].bitcast(F32)
        rwt = sb("rwt_sb", [128, 8 * 8])
        rwtb = wca[:, 0:64]
        ones_col = sb("ones_col", [128, 1])
        ones_row = sb("ones_row", [1, 128])
        eps_col = sb("eps_col", [128, 1])
        ep2_col = sb("ep2_col", [128, 1])
        mrow = hh[0:1, 4096:6144].bitcast(F32)
        rrow = hh[0:1, 6144:8192].bitcast(F32)
        srow = hh[0:1, 8192:10240].bitcast(F32)
        rms_row = hh[0:1, 10240:14336].bitcast(F32)

        ps = [ctx.enter_context(nc.psum_tensor(f"ps{i}", [128, 1024], F32)) for i in range(4)]
        sems = {}
        for s_ in ["dq", "pe", "sd", "vd", "vch", "sch", "gch", "cc", "gq"]:
            sems[s_] = ctx.enter_context(nc.semaphore("sem_" + s_))
        block = ctx.enter_context(nc.Block())
        P = Plan()

        P.add("vector", lambda e: e.memset(ones_col[:], 1.0))
        P.add("vector", lambda e: e.memset(eps_col[:], 1e-5))
        P.add("vector", lambda e: e.memset(ep2_col[:], RMS_EPS))
        P.add("vector", lambda e: e.memset(racc[:], 0.0))
        P.add("vector", lambda e: e.memset(orow[:], 0.0))
        P.add("vector", lambda e: e.memset(ones_row[:], 1.0), inc=("vd", 1))
        P.bump("vd")
        for a, d in [(tril_d, tril), (tcol_d, tcol), (idn_d, idn)]:
            P.add("sync", (lambda a=a, d=d: (lambda e: e.dma_start(out=d[:], in_=a)))(),
                  inc=("dq", 16))
            P.bump("dq")
        P.add("sync", lambda e: e.dma_start(
            out=rwt.rearrange("q (c o) -> q c o", c=8), in_=rwt_d), inc=("dq", 16))
        rwt_ld = P.bump("dq") * 16
        P.add("vector", lambda e: e.tensor_copy(rwtb[:], rwt[:]),
              waits=[("dq", rwt_ld)], inc=("vd", 1))
        P.bump("vd")
        P.add("gpsimd", lambda e: e.iota(
            out=eidx[:], pattern=[[1, 8]], base=0, channel_multiplier=0,
            allow_small_or_imprecise_dtypes=True), inc=("gch", 1))
        P.bump("gch")

        for half in range(2):
            off = half * 1024
            P.add("sync", (lambda half=half: (lambda e: e.dma_start(
                out=xf3.rearrange("q (c t) -> q c t", c=8),
                in_=xin[:, :, half * 1024:(half + 1) * 1024])))(),
                waits=[("vd", P.val("vd"))], inc=("dq", 16))
            xldv = P.bump("dq") * 16
            emit_ln_stats(P, xf3, 8, 1024, ones_col, xsq, mrow, rrow, srow,
                          ps[0][0:1, :], ps[1][0:1, :], eps_col,
                          [("dq", xldv), ("sd", P.val("sd")), ("vd", P.val("vd"))],
                          CS=1024, off=0)
            for q5 in range(2):
                emit_bcast_row(P, mrow[:, q5 * 512:(q5 + 1) * 512], ps[2][:, 0:512],
                               512, ones_row, [("vd", P.val("vd"))])
                emit_bcast_row(P, rrow[:, q5 * 512:(q5 + 1) * 512], ps[3][:, 0:512],
                               512, ones_row, [])
                bpe = P.val("pe")
                for ic in range(8):
                    o2 = ic * 1024 + q5 * 512
                    od = ic * 2048 + off + q5 * 512
                    P.add("vector", (lambda o2=o2: (lambda e: e.tensor_tensor(
                        out=xsq[0][:, 0:512], in0=xf3[:, o2:o2 + 512],
                        in1=ps[2][:, 0:512], op=ALU.subtract)))(),
                        waits=([("pe", bpe)] if ic == 0 else []))
                    P.add("vector", (lambda od=od: (lambda e: e.tensor_tensor(
                        out=xb[:, od:od + 512], in0=xsq[0][:, 0:512],
                        in1=ps[3][:, 0:512], op=ALU.mult)))(),
                        inc=(("vd", 1) if ic == 7 else None))
                P.bump("vd")
        xb_vd = P.val("vd")

        for half in range(2):
            off = half * 1024
            pb0 = P.val("pe")
            sd0 = P.val("sd")
            for ic in range(8):
                w = [("vd", xb_vd)] if ic == 0 else []
                if ic >= 2:
                    w.append(("pe", pb0 + ic - 1))
                P.add("scalar", (lambda ic=ic, off=off: (lambda e: e.activation(
                    out=xsq[ic % 2][:, 0:1024], in_=xb[:, ic * 2048 + off:ic * 2048 + off + 1024],
                    func=AF.Square)))(), waits=w, inc=("sd", 1))
                P.bump("sd")
                for s5 in range(2):
                    P.add("tensor", (lambda ic=ic, s5=s5: (lambda e: e.matmul(
                        out=ps[0][0:1, s5 * 512:(s5 + 1) * 512], lhsT=ones_col[:, 0:1],
                        rhs=xsq[ic % 2][:, s5 * 512:(s5 + 1) * 512],
                        start=(ic == 0), stop=(ic == 7))))(),
                        waits=([("sd", sd0 + ic + 1)] if s5 == 0 else []),
                        inc=(("pe", 1) if s5 == 1 else None))
                P.bump("pe")
            pe_done = P.val("pe")
            P.add("vector", (lambda off=off: (lambda e: e.tensor_scalar(
                out=srow[:, 0:1024], in0=ps[0][0:1, 0:1024], scalar1=1.0 / D,
                scalar2=None, op0=ALU.mult)))(), waits=[("pe", pe_done)], inc=("vd", 1))
            P.bump("vd")
            P.add("scalar", lambda e: e.activation(
                out=srow[:, 0:1024], in_=srow[:, 0:1024], func=AF.Sqrt,
                bias=ep2_col[0:1, 0:1]), waits=[("vd", P.val("vd"))], inc=("sd", 1))
            P.bump("sd")
            P.add("vector", (lambda off=off: (lambda e: e.reciprocal(
                out=rms_row[:, off:off + 1024], in_=srow[:, 0:1024])))(),
                waits=[("sd", P.val("sd"))], inc=("vd", 1))
            P.bump("vd")
        rms_vd = P.val("vd")
        for t in range(16):
            P.add("tensor", (lambda t=t: (lambda e: e.matmul(
                out=ps[1][:, t:t + 1], lhsT=rms_row[:, t * 128:(t + 1) * 128],
                rhs=ones_row[0:1, 0:1], start=True, stop=True)))(),
                waits=([("vd", rms_vd)] if t == 0 else []),
                inc=(("pe", 1) if t == 15 else None))
        P.bump("pe")
        P.add("vector", lambda e: e.tensor_copy(rmsc[:], ps[1][:, 0:16]),
              waits=[("pe", P.val("pe"))], inc=("vd", 1))
        P.bump("vd")

        for t in range(16):
            w0 = [("vd", P.val("vd"))]
            for ic in range(8):
                P.add("tensor", (lambda t=t, ic=ic: (lambda e: e.matmul(
                    out=ps[2][:, (t % 2) * 8:(t % 2) * 8 + 8],
                    lhsT=xb[:, ic * 2048 + t * 128:ic * 2048 + (t + 1) * 128],
                    rhs=rwtb[:, ic * 8:(ic + 1) * 8],
                    start=(ic == 0), stop=(ic == 7))))(),
                    waits=(w0 if ic == 0 else []) + ([("vd", P.cnt[f"rt_{t - 2}"])] if ic == 0 and t >= 2 else []),
                    inc=(("pe", 1) if ic == 7 else None))
                w0 = []
            pen = P.bump("pe")
            P.add("vector", (lambda t=t: (lambda e: e.tensor_scalar(
                out=lgt[:, t * 8:(t + 1) * 8], in0=ps[2][:, (t % 2) * 8:(t % 2) * 8 + 8],
                scalar1=rmsc[:, t:t + 1], scalar2=None, op0=ALU.mult)))(),
                waits=[("pe", pen)], inc=("vd", 1))
            P.cnt[f"rt_{t}"] = P.bump("vd")
        for t in range(16):
            P.add("vector", (lambda t=t: (lambda e: e.max(
                out=mx8[:], in_=lgt[:, t * 8:(t + 1) * 8])))())
            P.add("vector", (lambda t=t: (lambda e: e.max_index(
                out=mi8[:], in_max=mx8[:], in_values=lgt[:, t * 8:(t + 1) * 8])))())
            P.add("vector", lambda e: e.tensor_scalar(
                out=miu[:], in0=mi8[:], scalar1=65535, scalar2=None, op0=ALU.bitwise_and))
            P.add("vector", lambda e: e.tensor_copy(i1f[:, 0:2], miu[:, 0:2]))
            P.add("vector", lambda e: e.tensor_tensor(
                out=scr8[:, 0:1], in0=mx8[:, 0:1], in1=mx8[:, 1:2], op=ALU.subtract),
                inc=("vd", 1))
            P.bump("vd")
            P.add("scalar", lambda e: e.activation(
                out=rw12[:, 0:1], in_=scr8[:, 0:1], func=AF.Sigmoid),
                waits=[("vd", P.val("vd"))], inc=("sd", 1))
            P.bump("sd")
            P.add("vector", lambda e: e.tensor_scalar(
                out=rw12[:, 1:2], in0=rw12[:, 0:1], scalar1=-1.0, scalar2=1.0,
                op0=ALU.mult, op1=ALU.add), waits=[("sd", P.val("sd"))])
            P.add("vector", lambda e: e.tensor_scalar(
                out=scr8[:, 8:16], in0=eidx[:], scalar1=i1f[:, 0:1], scalar2=None,
                op0=ALU.is_equal), waits=[("gch", 1)])
            P.add("vector", lambda e: e.tensor_scalar(
                out=scr8[:, 16:24], in0=eidx[:], scalar1=i1f[:, 1:2], scalar2=None,
                op0=ALU.is_equal))
            P.add("vector", (lambda t=t: (lambda e: e.tensor_tensor(
                out=selt[:, t * 8:(t + 1) * 8], in0=scr8[:, 8:16], in1=scr8[:, 16:24],
                op=ALU.add)))())
            P.add("vector", lambda e: e.tensor_scalar(
                out=scr8[:, 0:8], in0=scr8[:, 8:16], scalar1=rw12[:, 0:1], scalar2=None,
                op0=ALU.mult))
            P.add("vector", (lambda t=t: (lambda e: e.scalar_tensor_tensor(
                out=wdns[:, t * 8:(t + 1) * 8], in0=scr8[:, 16:24],
                scalar=rw12[:, 1:2], in1=scr8[:, 0:8], op0=ALU.mult, op1=ALU.add)))(),
                inc=("vd", 1))
            P.bump("vd")
        sel_vd = P.val("vd")
        for t in range(16):
            P.add("tensor", (lambda t=t: (lambda e: e.matmul(
                out=ps[3][0:1, 0:8], lhsT=ones_col[:, 0:1],
                rhs=selt[:, t * 8:(t + 1) * 8],
                start=(t == 0), stop=(t == 15))))(),
                waits=([("vd", sel_vd)] if t == 0 else []),
                inc=(("pe", 1) if t == 15 else None))
        P.bump("pe")
        P.add("vector", lambda e: e.tensor_copy(trow[:], ps[3][0:1, 0:8]),
              waits=[("pe", P.val("pe"))], inc=("vd", 1))
        tot_vd = P.bump("vd")
        P.add("sync", lambda e: e.dma_start(out=tot_bounce, in_=trow[:]),
              waits=[("vd", tot_vd)], inc=("dq", 16))
        P.bump("dq")
        P.add("gpsimd", lambda e: e.collective_compute(
            "AllGather", ALU.bypass, replica_groups=[list(range(NCORES))],
            ins=[tot_bounce.opt()], outs=[tot_gath.opt()]),
            waits=[("dq", P.val("dq") * 16)], inc=("cc", 1))
        P.add("sync", lambda e: e.dma_start(out=tg8[:], in_=tot_gath),
              waits=[("cc", 1)], inc=("dq", 16))
        tg_ld = P.bump("dq") * 16
        P.add("tensor", lambda e: e.matmul(
            out=ps[3][0:1, 8:16], lhsT=tcol[:], rhs=tg8[:], start=True, stop=True),
            waits=[("dq", tg_ld)], inc=("pe", 1))
        P.bump("pe")
        P.add("vector", lambda e: e.tensor_copy(brow[:], ps[3][0:1, 8:16]),
              waits=[("pe", P.val("pe"))], inc=("vd", 1))
        P.bump("vd")

        for t in range(16):
            P.add("vector", lambda e: e.tensor_tensor(
                out=crow[:], in0=brow[:], in1=orow[:], op=ALU.add),
                waits=[("vd", P.val("vd"))], inc=("vd", 1))
            P.bump("vd")
            cvd = P.val("vd")
            P.add("tensor", (lambda t=t: (lambda e: e.matmul(
                out=ps[2][:, (t % 2) * 8:(t % 2) * 8 + 8], lhsT=tril[:],
                rhs=selt[:, t * 8:(t + 1) * 8], start=True, stop=False)))(),
                waits=[("vd", cvd)] + ([("vd", P.cnt[f"wef_{t - 2}"])] if t >= 2 else []),
                inc=None)
            P.add("tensor", (lambda t=t: (lambda e: e.matmul(
                out=ps[2][:, (t % 2) * 8:(t % 2) * 8 + 8], lhsT=ones_row[0:1, :],
                rhs=crow[:], start=False, stop=True)))(), inc=("pe", 1))
            pen = P.bump("pe")
            P.add("vector", (lambda t=t: (lambda e: e.tensor_scalar(
                out=scr8[:, 0:8], in0=ps[2][:, (t % 2) * 8:(t % 2) * 8 + 8],
                scalar1=2560.0, scalar2=None, op0=ALU.is_lt)))(),
                waits=[("pe", pen)])
            P.add("vector", (lambda t=t: (lambda e: e.tensor_tensor(
                out=wcols.rearrange("p (e u) -> p e u", u=16)[:, :, t],
                in0=scr8[:, 0:8], in1=wdns[:, t * 8:(t + 1) * 8], op=ALU.mult)))(),
                inc=("vd", 1))
            P.cnt[f"wef_{t}"] = P.bump("vd")
            P.add("tensor", (lambda t=t: (lambda e: e.matmul(
                out=ps[3][0:1, 16:24], lhsT=ones_col[:, 0:1],
                rhs=selt[:, t * 8:(t + 1) * 8], start=True, stop=True)))(),
                inc=("pe", 1))
            pen2 = P.bump("pe")
            P.add("vector", lambda e: e.tensor_tensor(
                out=orow[:], in0=orow[:], in1=ps[3][0:1, 16:24], op=ALU.add),
                waits=[("pe", pen2)], inc=("vd", 1))
            P.bump("vd")
        P.add("tensor", lambda e: e.matmul(
            out=ps[0][:, 0:128], lhsT=wcols[:], rhs=idn[:], start=True, stop=True),
            waits=[("vd", P.val("vd")), ("sd", P.val("sd"))], inc=("pe", 1))
        P.bump("pe")
        P.add("vector", lambda e: e.tensor_copy(xsq[0][:, 0:128], ps[0][:, 0:128]),
              waits=[("pe", P.val("pe"))], inc=("vd", 1))
        P.bump("vd")
        P.add("sync", lambda e: e.dma_start(
            out=wtd.rearrange("a (p f) -> (a p) f", p=128), in_=xsq[0][:, 0:128]),
            waits=[("vd", P.val("vd"))], inc=("dq", 16))
        wtd_ld = P.bump("dq") * 16

        def wdma(dram_slc, dst, nic, wkey):
            w = [("pe", P.cnt.get(wkey, 0))]
            P.add("sync", (lambda dram_slc=dram_slc, dst=dst, nic=nic: (lambda e: e.dma_start(
                out=dst[:, 0:nic * 128].rearrange("q (c o) -> q c o", c=nic),
                in_=dram_slc)))(), waits=w, inc=("dq", 16))
            return P.bump("dq") * 16

        for ei in range(8):
            for wh in range(2):
                P.add("sync", (lambda ei=ei, wh=wh: (lambda e: e.dma_start(
                    out=wrow[:], in_=wtd[:, ei * 2048 + wh * 1024:ei * 2048 + (wh + 1) * 1024])))(),
                    waits=[("dq", wtd_ld), ("pe", P.cnt.get(f"wbtmm_{ei}_{wh - 1}", 0))],
                    inc=("dq", 16))
                wr_ld = P.bump("dq") * 16
                for q2 in range(2):
                    q4 = wh * 2 + q2
                    P.add("tensor", (lambda q2=q2: (lambda e: e.matmul(
                        out=ps[2][:, 0:512], lhsT=ones_row[0:1, :],
                        rhs=wrow[:, q2 * 512:(q2 + 1) * 512], start=True, stop=True)))(),
                        waits=[("dq", wr_ld), ("vd", P.val("vd"))],
                        inc=("pe", 1))
                    pen = P.bump("pe")
                    P.cnt[f"wbtmm_{ei}_{wh}"] = pen
                    P.add("vector", (lambda q4=q4: (lambda e: e.tensor_copy(
                        wbt[:, q4 * 512:(q4 + 1) * 512], ps[2][:, 0:512])))(),
                        waits=[("pe", pen)], inc=("vd", 1))
                    P.cnt[f"wbt_{ei}"] = P.bump("vd")
            for th in range(2):
                toff = th * 1024
                for oc in range(32):
                    la = wdma(rg_d[ei, oc], wra, 8, "wra_pe")
                    lb = wdma(ru_d[ei, oc], wrb, 8, "wrb_pe")
                    P.add("vector", lambda e: e.tensor_copy(wca[:], wra[:]),
                          waits=[("dq", la)], inc=("vd", 1))
                    cva = P.bump("vd")
                    P.add("scalar", lambda e: e.copy(wcb_[:], wrb[:]),
                          waits=[("dq", lb)], inc=("sd", 1))
                    cvb = P.bump("sd")
                    psA = ps[oc % 2]
                    psB = ps[2 + oc % 2]
                    mmw = [("vd", cva), ("sd", cvb), ("vd", xb_vd)]
                    if f"hsd_{ei}_{th}_{oc - 2}" in P.cnt:
                        mmw.append(("sd", P.cnt[f"hsd_{ei}_{th}_{oc - 2}"]))
                    if f"hvd_{ei}_{th}_{oc - 2}" in P.cnt:
                        mmw.append(("vd", P.cnt[f"hvd_{ei}_{th}_{oc - 2}"]))
                    if oc == 0:
                        mmw += [("vd", P.val("vd")), ("sd", P.val("sd"))]
                    for s5 in range(2):
                        for ic in range(8):
                            P.add("tensor", (lambda oc=oc, s5=s5, ic=ic, toff=toff, psA=psA: (lambda e: e.matmul(
                                out=psA[:, s5 * 512:(s5 + 1) * 512],
                                lhsT=wca[:, ic * 128:(ic + 1) * 128],
                                rhs=xb[:, ic * 2048 + toff + s5 * 512:ic * 2048 + toff + (s5 + 1) * 512],
                                start=(ic == 0), stop=(ic == 7))))(),
                                waits=(mmw if (s5 == 0 and ic == 0) else []))
                            mmw = []
                    for s5 in range(2):
                        for ic in range(8):
                            P.add("tensor", (lambda oc=oc, s5=s5, ic=ic, toff=toff, psB=psB: (lambda e: e.matmul(
                                out=psB[:, s5 * 512:(s5 + 1) * 512],
                                lhsT=wcb_[:, ic * 128:(ic + 1) * 128],
                                rhs=xb[:, ic * 2048 + toff + s5 * 512:ic * 2048 + toff + (s5 + 1) * 512],
                                start=(ic == 0), stop=(ic == 7))))(),
                                inc=(("pe", 1) if (s5 == 1 and ic == 7) else None))
                    pen = P.bump("pe")
                    P.cnt["wra_pe"] = pen
                    P.cnt["wrb_pe"] = pen
                    P.add("scalar", (lambda oc=oc, psA=psA: (lambda e: e.activation(
                        out=gt2[oc % 2][:, 0:1024], in_=psA[:, 0:1024],
                        func=AF.Silu)))(), waits=[("pe", pen)], inc=("sd", 1))
                    P.cnt[f"hsd_{ei}_{th}_{oc}"] = P.bump("sd")
                    P.add("vector", (lambda oc=oc, psB=psB: (lambda e: e.scalar_tensor_tensor(
                        out=hh[:, oc * 1024:(oc + 1) * 1024], in0=psB[:, 0:1024],
                        scalar=1.0, in1=gt2[oc % 2][:, 0:1024],
                        op0=ALU.mult, op1=ALU.mult)))(),
                        waits=[("sd", P.val("sd"))], inc=("vd", 1))
                    P.cnt[f"hvd_{ei}_{th}_{oc}"] = P.bump("vd")
                for oc2 in range(8):
                    for hf in range(2):
                        lc = wdma(rd_d[ei, oc2][:, hf * 16:(hf + 1) * 16, :], wrd, 16, "wrd_pe")
                        P.add("vector", lambda e: e.tensor_copy(wcd[:], wrd[:]),
                              waits=[("dq", lc)], inc=("vd", 1))
                        cvc = P.bump("vd")
                        mmw = [("vd", cvc), ("vd", P.cnt[f"hvd_{ei}_{th}_31"])]
                        if oc2 == 0 and hf == 0:
                            mmw.append(("sd", P.val("sd")))
                        if f"dacc_{ei}_{th}_{oc2 - 1}" in P.cnt:
                            mmw.append(("vd", P.cnt[f"dacc_{ei}_{th}_{oc2 - 1}"]))
                        for s5 in range(2):
                            for ic in range(16):
                                icg = hf * 16 + ic
                                P.add("tensor", (lambda oc2=oc2, s5=s5, ic=ic, icg=icg: (lambda e: e.matmul(
                                    out=ps[2 + oc2 % 2][:, s5 * 512:(s5 + 1) * 512],
                                    lhsT=wcd[:, ic * 128:(ic + 1) * 128],
                                    rhs=hh[:, icg * 1024 + s5 * 512:icg * 1024 + (s5 + 1) * 512],
                                    start=(icg == 0), stop=(icg == 31))))(),
                                    waits=(mmw if (s5 == 0 and ic == 0) else []),
                                    inc=(("pe", 1) if (s5 == 1 and ic == 15) else None))
                                mmw = []
                        P.cnt["wrd_pe"] = P.bump("pe")
                    pen = P.val("pe")
                    for s5 in range(2):
                        P.add("vector", (lambda oc2=oc2, s5=s5, toff=toff: (lambda e: e.tensor_tensor(
                            out=wscr[:, 0:512], in0=ps[2 + oc2 % 2][:, s5 * 512:(s5 + 1) * 512],
                            in1=wbt[:, toff + s5 * 512:toff + (s5 + 1) * 512], op=ALU.mult)))(),
                            waits=[("pe", pen)] if s5 == 0 else [])
                        P.add("vector", (lambda oc2=oc2, s5=s5, toff=toff: (lambda e: e.tensor_tensor(
                            out=racc[:, oc2 * 2048 + toff + s5 * 512:oc2 * 2048 + toff + (s5 + 1) * 512],
                            in0=racc[:, oc2 * 2048 + toff + s5 * 512:oc2 * 2048 + toff + (s5 + 1) * 512],
                            in1=wscr[:, 0:512], op=ALU.add)))(),
                            inc=(("vd", 1) if s5 == 1 else None))
                    P.cnt[f"dacc_{ei}_{th}_{oc2}"] = P.bump("vd")

        for th in range(2):
            toff = th * 1024
            for q in range(4):
                for oc in range(16):
                    ocg = q * 16 + oc
                    la = wdma(sg_d[ocg], wra, 8, "wra_pe")
                    lb = wdma(su_d[ocg], wrb, 8, "wrb_pe")
                    P.add("vector", lambda e: e.tensor_copy(wca[:], wra[:]),
                          waits=[("dq", la)], inc=("vd", 1))
                    cva = P.bump("vd")
                    P.add("scalar", lambda e: e.copy(wcb_[:], wrb[:]),
                          waits=[("dq", lb)], inc=("sd", 1))
                    cvb = P.bump("sd")
                    psA = ps[oc % 2]
                    psB = ps[2 + oc % 2]
                    mmw = [("vd", cva), ("sd", cvb)]
                    if f"shsd_{th}_{q}_{oc - 2}" in P.cnt:
                        mmw.append(("sd", P.cnt[f"shsd_{th}_{q}_{oc - 2}"]))
                    if f"shvd_{th}_{q}_{oc - 2}" in P.cnt:
                        mmw.append(("vd", P.cnt[f"shvd_{th}_{q}_{oc - 2}"]))
                    if oc == 0:
                        mmw += [("vd", P.val("vd")), ("sd", P.val("sd"))]
                    for s5 in range(2):
                        for ic in range(8):
                            P.add("tensor", (lambda s5=s5, ic=ic, toff=toff, psA=psA: (lambda e: e.matmul(
                                out=psA[:, s5 * 512:(s5 + 1) * 512],
                                lhsT=wca[:, ic * 128:(ic + 1) * 128],
                                rhs=xb[:, ic * 2048 + toff + s5 * 512:ic * 2048 + toff + (s5 + 1) * 512],
                                start=(ic == 0), stop=(ic == 7))))(),
                                waits=(mmw if (s5 == 0 and ic == 0) else []))
                            mmw = []
                    for s5 in range(2):
                        for ic in range(8):
                            P.add("tensor", (lambda s5=s5, ic=ic, toff=toff, psB=psB: (lambda e: e.matmul(
                                out=psB[:, s5 * 512:(s5 + 1) * 512],
                                lhsT=wcb_[:, ic * 128:(ic + 1) * 128],
                                rhs=xb[:, ic * 2048 + toff + s5 * 512:ic * 2048 + toff + (s5 + 1) * 512],
                                start=(ic == 0), stop=(ic == 7))))(),
                                inc=(("pe", 1) if (s5 == 1 and ic == 7) else None))
                    pen = P.bump("pe")
                    P.cnt["wra_pe"] = pen
                    P.cnt["wrb_pe"] = pen
                    P.add("scalar", (lambda oc=oc, psA=psA: (lambda e: e.activation(
                        out=gt2[oc % 2][:, 0:1024], in_=psA[:, 0:1024],
                        func=AF.Silu)))(), waits=[("pe", pen)], inc=("sd", 1))
                    P.cnt[f"shsd_{th}_{q}_{oc}"] = P.bump("sd")
                    P.add("vector", (lambda oc=oc, psB=psB: (lambda e: e.scalar_tensor_tensor(
                        out=hh[:, oc * 1024:(oc + 1) * 1024], in0=psB[:, 0:1024],
                        scalar=1.0, in1=gt2[oc % 2][:, 0:1024],
                        op0=ALU.mult, op1=ALU.mult)))(),
                        waits=[("sd", P.val("sd"))], inc=("vd", 1))
                    P.cnt[f"shvd_{th}_{q}_{oc}"] = P.bump("vd")
                for oc2 in range(8):
                    lc = wdma(sd_d[oc2][:, q * 16:(q + 1) * 16, :], wrd, 16, "wrd_pe")
                    P.add("vector", lambda e: e.tensor_copy(wcd[:], wrd[:]),
                          waits=[("dq", lc)], inc=("vd", 1))
                    cvc = P.bump("vd")
                    mmw = [("vd", cvc), ("vd", P.cnt[f"shvd_{th}_{q}_15"])]
                    if f"sacc_{th}_{q}_{oc2 - 1}" in P.cnt:
                        mmw.append(("vd", P.cnt[f"sacc_{th}_{q}_{oc2 - 1}"]))
                    if oc2 == 0:
                        mmw.append(("sd", P.val("sd")))
                    for s5 in range(2):
                        for ic in range(16):
                            P.add("tensor", (lambda oc2=oc2, s5=s5, ic=ic: (lambda e: e.matmul(
                                out=ps[2 + oc2 % 2][:, s5 * 512:(s5 + 1) * 512],
                                lhsT=wcd[:, ic * 128:(ic + 1) * 128],
                                rhs=hh[:, ic * 1024 + s5 * 512:ic * 1024 + (s5 + 1) * 512],
                                start=(ic == 0), stop=(ic == 15))))(),
                                waits=(mmw if (s5 == 0 and ic == 0) else []),
                                inc=(("pe", 1) if (s5 == 1 and ic == 15) else None))
                            mmw = []
                    pen = P.bump("pe")
                    P.cnt["wrd_pe"] = pen
                    for s5 in range(2):
                        P.add("vector", (lambda oc2=oc2, s5=s5, toff=toff: (lambda e: e.tensor_tensor(
                            out=racc[:, oc2 * 2048 + toff + s5 * 512:oc2 * 2048 + toff + (s5 + 1) * 512],
                            in0=racc[:, oc2 * 2048 + toff + s5 * 512:oc2 * 2048 + toff + (s5 + 1) * 512],
                            in1=ps[2 + oc2 % 2][:, s5 * 512:(s5 + 1) * 512], op=ALU.add)))(),
                            waits=[("pe", pen)] if s5 == 0 else [],
                            inc=(("vd", 1) if s5 == 1 else None))
                    P.cnt[f"sacc_{th}_{q}_{oc2}"] = P.bump("vd")

        for half in range(2):
            off = half * 1024
            P.add("sync", (lambda half=half: (lambda e: e.dma_start(
                out=xf3.rearrange("q (c t) -> q c t", c=8),
                in_=xin[:, :, half * 1024:(half + 1) * 1024])))(),
                waits=[("vd", P.val("vd")), ("dq", P.val("dq") * 16)], inc=("dq", 16))
            xldv = P.bump("dq") * 16
            emit_ln_stats(P, racc, 8, 1024, ones_col, xsq, mrow, rrow, srow,
                          ps[0][0:1, :], ps[1][0:1, :], eps_col,
                          [("sd", P.val("sd")), ("vd", P.val("vd"))],
                          CS=2048, off=off)
            for q5 in range(2):
                emit_bcast_row(P, mrow[:, q5 * 512:(q5 + 1) * 512], ps[2][:, 0:512],
                               512, ones_row, [("vd", P.val("vd"))])
                emit_bcast_row(P, rrow[:, q5 * 512:(q5 + 1) * 512], ps[3][:, 0:512],
                               512, ones_row, [])
                bpe = P.val("pe")
                for ic in range(8):
                    od = ic * 2048 + off + q5 * 512
                    o2 = ic * 1024 + q5 * 512
                    P.add("vector", (lambda od=od: (lambda e: e.tensor_tensor(
                        out=xsq[0][:, 0:512], in0=racc[:, od:od + 512],
                        in1=ps[2][:, 0:512], op=ALU.subtract)))(),
                        waits=([("pe", bpe), ("dq", xldv)] if ic == 0 else []))
                    P.add("vector", (lambda o2=o2: (lambda e: e.tensor_tensor(
                        out=xsq[1][:, 0:512], in0=xsq[0][:, 0:512],
                        in1=ps[3][:, 0:512], op=ALU.mult)))())
                    P.add("vector", (lambda od=od, o2=o2: (lambda e: e.tensor_tensor(
                        out=racc[:, od:od + 512], in0=xsq[1][:, 0:512],
                        in1=xf3[:, o2:o2 + 512], op=ALU.add)))(),
                        inc=(("vd", 1) if ic == 7 else None))
                P.bump("vd")
            vout = P.val("vd")
            P.add("sync", (lambda half=half: (lambda e: e.dma_start(
                out=xout[:, :, half * 1024:(half + 1) * 1024],
                in_=racc.rearrange("q (c t) -> q c t", c=8)[:, :, half * 1024:(half + 1) * 1024])))(),
                waits=[("vd", vout)], inc=("dq", 16))
            P.bump("dq")

        P.add("sync", None, waits=[("dq", P.val("dq") * 16)])
        P.run_all(block, sems)
    return nc


def _prep_l1ffn_inputs(inputs):
    com = {}
    com["rg"] = np.stack([_chunk_w(np.ascontiguousarray(inputs["L1_ffn_rg"][e])) for e in range(8)]).astype(BF)
    com["ru"] = np.stack([_chunk_w(np.ascontiguousarray(inputs["L1_ffn_ru"][e])) for e in range(8)]).astype(BF)
    com["rd"] = np.stack([_chunk_w(np.ascontiguousarray(inputs["L1_ffn_rd"][e])) for e in range(8)]).astype(BF)
    com["sg"] = _chunk_w(np.ascontiguousarray(inputs["L1_ffn_sg"])).astype(BF)
    com["su"] = _chunk_w(np.ascontiguousarray(inputs["L1_ffn_su"])).astype(BF)
    com["sd"] = _chunk_w(np.ascontiguousarray(inputs["L1_ffn_sd"])).astype(BF)
    rwt = np.ascontiguousarray(inputs["L1_ffn_router_w"].T)
    com["rwt"] = np.ascontiguousarray(rwt.reshape(8, 128, 8).transpose(1, 0, 2))
    com["tril"] = np.triu(np.ones((128, 128), np.float32), 1)
    com["idn"] = np.eye(128, dtype=np.float32)
    return com


def launch_l1ffn(x3, inputs):
    nc = build_l1ffn()
    com = _prep_l1ffn_inputs(inputs)
    xfm = _feat_major(x3.reshape(T_ALL, D))
    in_maps = []
    for c in range(NCORES):
        m = dict(com)
        m["xin"] = np.ascontiguousarray(xfm[:, :, c * T_CORE:(c + 1) * T_CORE])
        tc = np.zeros((8, 1), np.float32)
        tc[:c, 0] = 1.0
        m["tcol"] = tc
        in_maps.append(m)
    res = run_bass_kernel_spmd(nc, in_maps, core_ids=list(range(NCORES)))
    out = np.concatenate([r["xout"] for r in res.results], axis=2)
    return _unfeat_major(out).reshape(B, S, D)



CSC = 64
NCH = S // CSC


def _finish(P, block, sems, xout, xres_d):
    P.add("sync", lambda e: e.dma_start(out=xout, in_=xres_d),
          waits=[("vd", P.val("vd")), ("pe", P.val("pe")),
                 ("sd", P.val("sd"))], inc=("dq", 16))
    P.bump("dq")
    P.add("sync", None, waits=[("dq", P.val("dq") * 16)])
    P.run_all(block, sems)


def build_l1attn(nch=None, tail=True, phases="full"):
    nc = bass.Bass("TRN2", target_bir_lowering=False, debug=False)
    nch = NCH if nch is None else nch
    xb = nc.dram_tensor("xb", [128, 8, S], F32, kind="ExternalInput").ap()
    xout = nc.dram_tensor("xout", [128, 8, T_CORE], F32, kind="ExternalOutput").ap()
    wq_d = nc.dram_tensor("wq", [2, 128, 8, 128], F32, kind="ExternalInput").ap()
    wk_d = nc.dram_tensor("wk", [2, 128, 8, 128], F32, kind="ExternalInput").ap()
    wv_d = nc.dram_tensor("wv", [2, 128, 8, 128], F32, kind="ExternalInput").ap()
    wg_d = nc.dram_tensor("wg", [2, 128, 8, 128], F32, kind="ExternalInput").ap()
    wo_d = nc.dram_tensor("wo", [128, 2, 8, 128], F32, kind="ExternalInput").ap()
    xres_d = nc.dram_tensor("xres", [128, 8, 2048], F32, kind="ExternalInput").ap()
    bq_d = nc.dram_tensor("bq", [128, 2], F32, kind="ExternalInput").ap()
    bk_d = nc.dram_tensor("bk", [128, 2], F32, kind="ExternalInput").ap()
    bv_d = nc.dram_tensor("bv", [128, 2], F32, kind="ExternalInput").ap()
    bg_d = nc.dram_tensor("bg", [128, 2], F32, kind="ExternalInput").ap()
    bo_d = nc.dram_tensor("bo", [128, 8], F32, kind="ExternalInput").ap()
    wsm_d = nc.dram_tensor("wsm", [128, 8, 20], F32, kind="ExternalInput").ap()
    sb1_d = nc.dram_tensor("sb1", [20, 1], F32, kind="ExternalInput").ap()
    sb2_d = nc.dram_tensor("sb2", [20, 1], F32, kind="ExternalInput").ap()
    onrm_d = nc.dram_tensor("onrm", [128, 1], F32, kind="ExternalInput").ap()
    rf_d = nc.dram_tensor("rf", [1, 1], F32, kind="ExternalInput").ap()
    idnb_d = nc.dram_tensor("idnb", [128, 128], BF16, kind="ExternalInput").ap()
    idnf_d = nc.dram_tensor("idnf", [128, 128], F32, kind="ExternalInput").ap()
    bones_d = nc.dram_tensor("bones", [2, 128], F32, kind="ExternalInput").ap()
    xnd = nc.dram_tensor("xnd", [128, 8, S], BF16, kind="Internal").ap()
    qd = nc.dram_tensor("qd", [2, 128, S], BF16, kind="Internal").ap()
    kd = nc.dram_tensor("kd", [2, 128, S], BF16, kind="Internal").ap()
    gd = nc.dram_tensor("gd", [4, 64, S], BF16, kind="Internal").ap()
    kdTb = nc.dram_tensor("kdTb", [2, 2, S, 128], BF16, kind="Internal").ap()
    vdT = nc.dram_tensor("vdT", [4, S, 64], BF16, kind="Internal").ap()
    srd2 = nc.dram_tensor("srd2", [2, 2, 5, S], F32, kind="Internal").ap()
    od1 = nc.dram_tensor("od1", [2, 2 * 64 * CSC], BF16, kind="Internal").ap()
    ypd = nc.dram_tensor("ypd", [4, 128, 8, 2048], F32, kind="Internal").ap()
    yprd = nc.dram_tensor("yprd", [128, 8, 2048], F32, kind="Internal").ap()

    with ExitStack() as ctx:
        sb = lambda n, shp, dt=F32: ctx.enter_context(nc.sbuf_tensor(n, shp, dt))
        bigA = sb("bigA", [128, 16384])
        xblk = bigA
        xnstg = sb("xnstg", [128, 8 * 2048], BF16)
        oTf = [sb(f"oTf{g}", [128, S], BF16) for g in range(2)]
        kqDc = [sb(f"kqD{g}", [128, 4 * CSC], BF16) for g in range(2)]
        fsq = [sb(f"fsq{g}", [2, 8 * CSC]) for g in range(2)]
        UW = [sb(f"UW{g}", [128, 128], BF16) for g in range(2)]
        rvSB = [sb(f"rvSB{g}", [128, 4]) for g in range(2)]
        gbo = [sb(f"gbo{g}", [2, 128], BF16) for g in range(2)]
        g1s = [sb(f"g1s{g}", [2, 64]) for g in range(2)]
        kgSB = [sb(f"kgSB{g}", [128, 128], BF16) for g in range(2)]
        upds = [sb(f"upds{g}", [128, 128]) for g in range(2)]
        nfns = [sb(f"nfns{g}", [128, 2]) for g in range(2)]
        sqj = [sb(f"sqj{g}", [128, 64], BF16) for g in range(2)]
        nrm2 = [sb(f"nrm2{g}", [2, 2]) for g in range(2)]
        inv2 = [sb(f"inv2{g}", [2, 2]) for g in range(2)]
        m1s = [sb(f"m1s{g}", [2, 2]) for g in range(2)]
        scol = [sb(f"scol{g}", [2, 8]) for g in range(2)]
        ggs = [sb(f"ggs{g}", [2, 2]) for g in range(2)]
        ccols = [sb(f"ccol{g}", [128, 4]) for g in range(2)]
        tfs = [sb(f"tfs{g}", [2, 128]) for g in range(2)]
        stg = sb("stg", [128, 128])
        stg2 = sb("stg2", [128, 128])
        stgb = sb("stgb", [128, 512], BF16)
        stgb2 = sb("stgb2", [128, 128], BF16)
        stgb3 = sb("stgb3", [128, 128], BF16)
        wcf = sb("wcf", [128, 8 * 128])
        wcb = sb("wcb", [128, 8 * 128], BF16)
        srt = sb("srt", [20, 512])
        br0 = sb("br0", [1, 512])
        br1 = sb("br1", [1, 512])
        bcl = sb("bcl", [128, 2])
        nrcol = sb("nrcol", [128, 2])
        bcols = sb("bcols", [128, 8])
        sbias = sb("sbias", [20, 1])
        sbias2 = sb("sbias2", [20, 1])
        onrm2 = sb("onrm2", [128, 1])
        rf1 = sb("rf1", [1, 1])
        rfc2 = sb("rfc2", [2, 1])
        rf1s = sb("rf1s", [128, 1])
        idnb = sb("idnb_sb", [128, 128], BF16)
        ones_col = sb("ones_col", [128, 1])
        ones_row = sb("ones_row", [1, 128])
        onesd2 = sb("onesd2", [128, 2])
        eps_col = sb("eps_col", [128, 1])
        ep6_col = sb("ep6_col", [128, 1])
        bo_sb = sb("bo_sb", [128, 8])
        mrow = bigA[0:1, 8192:9216]
        rrow = bigA[0:1, 9216:10240]
        srow = bigA[0:1, 10240:11264]
        xsq = [bigA[:, 11264:12288], bigA[:, 12288:13312]]

        pps = [ctx.enter_context(nc.psum_tensor(f"pp{i}", [128, 512], F32)) for i in range(2)]
        pgr = [ctx.enter_context(nc.psum_tensor(f"bkA{i}", [128, 512], F32)) for i in range(2)]
        bkB = [ctx.enter_context(nc.psum_tensor(f"bkB{i}", [128, 512], F32)) for i in range(2)]
        bkC = [ctx.enter_context(nc.psum_tensor(f"bkC{i}", [128, 512], F32)) for i in range(2)]
        psc = bkC[0]
        sems = {}
        for s_ in ["dq", "pe", "sd", "vd", "vch", "sch", "gch", "cc", "gq"]:
            sems[s_] = ctx.enter_context(nc.semaphore("sem_" + s_))
        block = ctx.enter_context(nc.Block())
        P = Plan()

        P.add("vector", lambda e: e.memset(ones_col[:], 1.0))
        P.add("vector", lambda e: e.memset(ones_row[:], 1.0))
        P.add("vector", lambda e: e.memset(eps_col[:], 1e-5))
        P.add("vector", lambda e: e.memset(ep6_col[:], 1e-6))
        P.add("vector", lambda e: e.memset(onesd2[:], 0.0))
        P.add("vector", lambda e: e.memset(onesd2[0:64, 0:1], 1.0))
        P.add("vector", lambda e: e.memset(onesd2[64:128, 1:2], 1.0),
              inc=("vd", 1))
        P.bump("vd")
        for a, d in [(bq_d, bcols[:, 0:2]), (bk_d, bcols[:, 2:4]), (bv_d, bcols[:, 4:6]),
                     (bg_d, bcols[:, 6:8]), (sb1_d, sbias[:]), (sb2_d, sbias2[:]),
                     (onrm_d, onrm2[:]), (rf_d, rf1[:]), (idnb_d, idnb[:]),
                     (bo_d, bo_sb[:])]:
            P.add("sync", (lambda a=a, d=d: (lambda e: e.dma_start(out=d, in_=a)))(),
                  inc=("dq", 16))
            P.bump("dq")
        P.add("sync", lambda e: e.dma_start(out=rfc2[0:1, :], in_=rf_d), inc=("dq", 16))
        P.bump("dq")
        P.add("sync", lambda e: e.dma_start(out=rfc2[1:2, :], in_=rf_d), inc=("dq", 16))
        cdq = P.bump("dq") * 16
        P.add("vector", lambda e: e.tensor_tensor(
            out=sbias[:], in0=sbias[:], in1=sbias2[:], op=ALU.add),
            waits=[("dq", cdq)], inc=("vd", 1))
        P.bump("vd")
        P.add("tensor", lambda e: e.matmul(
            out=pgr[0][:, 0:1], lhsT=ones_row[:], rhs=rf1[:],
            start=True, stop=True), waits=[("dq", cdq), ("vd", P.val("vd"))],
            inc=("pe", 1))
        P.bump("pe")
        P.add("vector", lambda e: e.tensor_scalar(
            out=rf1s[:], in0=pgr[0][:, 0:1], scalar1=1.0, scalar2=None, op0=ALU.add),
            waits=[("pe", P.val("pe"))], inc=("vd", 1))
        P.bump("vd")
        P.add("vector", lambda e: e.memset(wcb[:, 0:1024], 0.0), inc=("vd", 1))
        zvd = P.bump("vd")
        kdTbf = kdTb.rearrange("a b s c -> (a b s) c")
        for j in range(32):
            P.add("sync", (lambda j=j: (lambda e: e.dma_start(
                out=kdTbf[j * 1024:(j + 1) * 1024, :].rearrange(
                    "(n p) c -> p n c", p=128),
                in_=wcb[:, 0:1024].rearrange("p (n c) -> p n c", n=8))))(),
                waits=([("vd", zvd)] if j == 0 else []), inc=("dq", 16))
            P.bump("dq")

        if phases == "z":
            _finish(P, block, sems, xout, xres_d)
            return nc
        for tb in range(8):
            P.add("sync", (lambda tb=tb: (lambda e: e.dma_start(
                out=xblk[:, 0:8192].rearrange("q (c t) -> q c t", c=8),
                in_=xb[:, :, tb * 1024:(tb + 1) * 1024])))(),
                waits=[("vd", P.val("vd")), ("sd", P.val("sd")),
                       ("pe", P.val("pe"))],
                inc=("dq", 16))
            xldv = P.bump("dq") * 16
            for qr in range(2):
                off = qr * 512
                emit_ln_stats(P, xblk, 8, 512, ones_col, xsq, mrow, rrow, srow,
                              pps[0][0:1, 0:512], pps[0][32:33, 0:512], eps_col,
                              [("dq", xldv), ("sd", P.val("sd")), ("vd", P.val("vd"))],
                              CS=1024, off=off)
                emit_bcast_row(P, mrow[:, 0:512], pps[1][:, 0:512],
                               512, ones_row, [("vd", P.val("vd"))])
                emit_bcast_row(P, rrow[:, 0:512], psc[:, 0:512],
                               512, ones_row, [])
                bpe = P.val("pe")
                for ic in range(8):
                    o2 = ic * 1024 + off
                    P.add("vector", (lambda o2=o2: (lambda e: e.tensor_tensor(
                        out=xsq[0][:, 0:512], in0=xblk[:, o2:o2 + 512],
                        in1=pps[1][:, 0:512], op=ALU.subtract)))(),
                        waits=([("pe", bpe), ("dq", P.val("dq") * 16)] if ic == 0 else []))
                    P.add("vector", (lambda o2=o2: (lambda e: e.tensor_tensor(
                        out=xnstg[:, o2:o2 + 512], in0=xsq[0][:, 0:512],
                        in1=psc[:, 0:512], op=ALU.mult)))(),
                        inc=(("vd", 1) if ic == 7 else None))
                P.bump("vd")
            xnvd = P.val("vd")
            P.add("sync", (lambda tb=tb: (lambda e: e.dma_start(
                out=xnd[:, :, tb * 1024:(tb + 1) * 1024],
                in_=xnstg[:, 0:8192].rearrange("q (c t) -> q c t", c=8))))(),
                waits=[("vd", xnvd)], inc=("dq", 16))
            P.bump("dq")

        if phases == "a":
            _finish(P, block, sems, xout, xres_d)
            return nc
        P.add("sync", lambda e: e.dma_start(
            out=wcf[:, 0:160].rearrange("q (c o) -> q c o", c=8), in_=wsm_d),
            inc=("dq", 16))
        wsm_ld = P.bump("dq") * 16
        P.add("vector", lambda e: e.tensor_copy(wcb[:, 0:160], wcf[:, 0:160]),
              waits=[("dq", wsm_ld)], inc=("vd", 1))
        wsm_cv = P.bump("vd")
        for t5 in range(16):
            if t5 % 4 == 0:
                P.add("sync", (lambda t5=t5: (lambda e: e.dma_start(
                    out=xnstg.rearrange("q (c t) -> q c t", c=8),
                    in_=xnd[:, :, (t5 // 4) * 2048:(t5 // 4 + 1) * 2048])))(),
                    waits=[("pe", P.val("pe")), ("vd", P.val("vd"))], inc=("dq", 16))
                P.bump("dq")
            xn_ld = P.val("dq") * 16
            for ic in range(8):
                P.add("tensor", (lambda t5=t5, ic=ic: (lambda e: e.matmul(
                    out=pps[t5 % 2][0:20, 0:512], lhsT=wcb[:, ic * 20:(ic + 1) * 20],
                    rhs=xnstg[:, ic * 2048 + (t5 % 4) * 512:ic * 2048 + (t5 % 4) * 512 + 512],
                    start=(ic == 0), stop=(ic == 7))))(),
                    waits=([("dq", max(wsm_ld, xn_ld)), ("vd", wsm_cv),
                            ("sd", P.cnt.get(f"smep_{t5 - 2}", 0)),
                            ("dq", P.cnt.get(f"smdq_{t5 - 1}", 0))] if ic == 0 else []),
                    inc=(("pe", 1) if ic == 7 else None))
            pen = P.bump("pe")
            P.add("scalar", (lambda t5=t5: (lambda e: e.activation(
                out=srt[:, 0:512], in_=pps[t5 % 2][0:20, 0:512], func=AF.Sigmoid,
                bias=sbias[:])))(), waits=[("pe", pen), ("vd", P.val("vd"))], inc=("sd", 1))
            P.cnt[f"smep_{t5}"] = P.bump("sd")
            P.add("sync", (lambda t5=t5: (lambda e: e.dma_start(
                out=srd2.rearrange("h g q t -> (h g q) t")[
                    :, t5 * 512:(t5 + 1) * 512],
                in_=srt[:, 0:512])))(),
                waits=[("sd", P.val("sd"))], inc=("dq", 16))
            P.cnt[f"smdq_{t5}"] = P.bump("dq") * 16

        if phases == "sp":
            _finish(P, block, sems, xout, xres_d)
            return nc

        def emit_bigproj(wdram, bcol_off, act, fm_dst, T_dst, normalize, fold, store_T):
            for oc in range(2):
                P.add("sync", (lambda oc=oc, wdram=wdram: (lambda e: e.dma_start(
                    out=wcf.rearrange("q (c o) -> q c o", c=8), in_=wdram[oc])))(),
                    waits=[("pe", P.cnt.get("bp_pe", 0))], inc=("dq", 16))
                wld = P.bump("dq") * 16
                P.add("vector", lambda e: e.tensor_copy(wcb[:], wcf[:]),
                      waits=[("dq", wld), ("pe", P.cnt.get("bp_pe", 0))], inc=("vd", 1))
                cvd = P.bump("vd")
                for t5 in range(16):
                    if t5 % 4 == 0:
                        P.add("sync", (lambda t5=t5: (lambda e: e.dma_start(
                            out=xnstg.rearrange("q (c t) -> q c t", c=8),
                            in_=xnd[:, :, (t5 // 4) * 2048:(t5 // 4 + 1) * 2048])))(),
                            waits=[("pe", P.val("pe"))], inc=("dq", 16))
                        P.bump("dq")
                    xn_ld = P.val("dq") * 16
                    for ic in range(8):
                        P.add("tensor", (lambda t5=t5, ic=ic: (lambda e: e.matmul(
                            out=pps[t5 % 2][:, 0:512], lhsT=wcb[:, ic * 128:(ic + 1) * 128],
                            rhs=xnstg[:, ic * 2048 + (t5 % 4) * 512:ic * 2048 + (t5 % 4) * 512 + 512],
                            start=(ic == 0), stop=(ic == 7))))(),
                            waits=([("dq", xn_ld), ("vd", cvd),
                                    ("sd", P.cnt.get(f"bpsd_{t5 - 2}", 0))] if ic == 0 else []),
                            inc=(("pe", 1) if ic == 7 else None))
                    pen = P.bump("pe")
                    P.cnt["bp_pe"] = pen
                    P.add("scalar", (lambda t5=t5, act=act, oc=oc: (lambda e: e.activation(
                        out=stgb[:, 0:512], in_=pps[t5 % 2][:, 0:512], func=act,
                        bias=bcols[:, bcol_off * 2 + oc:bcol_off * 2 + oc + 1])))(),
                        waits=[("pe", pen), ("dq", P.cnt.get("bpfm_dq", 0)),
                               ("pe", P.cnt.get("bpT_pe", 0))], inc=("sd", 1))
                    P.cnt[f"bpsd_{t5}"] = P.bump("sd")
                    sde = P.val("sd")
                    if T_dst is None:
                        for h2 in range(2):
                            P.add("sync", (lambda oc=oc, h2=h2, t5=t5: (lambda e: e.dma_start(
                                out=fm_dst[oc * 2 + h2][:, t5 * 512:(t5 + 1) * 512],
                                in_=stgb[h2 * 64:(h2 + 1) * 64, 0:512])))(),
                                waits=([("sd", sde)] if h2 == 0 else []), inc=("dq", 16))
                            P.cnt["bpfm_dq"] = P.bump("dq") * 16
                        continue
                    br_ld = P.val("dq") * 16
                    for tt in range(4):
                        P.add("tensor", (lambda tt=tt, t5=t5: (lambda e: e.matmul(
                            out=pgr[0][:, 64:192], lhsT=stgb[:, tt * 128:(tt + 1) * 128],
                            rhs=idnb[:], start=True, stop=True)))(),
                            waits=[("sd", sde), ("vd", P.cnt.get("bpT_vd", 0)),
                                   ("dq", br_ld)],
                            inc=("pe", 1))
                        pT = P.bump("pe")
                        P.cnt["bp_pe"] = pT
                        P.cnt["bpT_pe"] = pT
                        if fold == "beta":
                            for h2b in range(2):
                                P.add("sync", (lambda oc=oc, t5=t5, tt=tt, h2b=h2b: (lambda e: e.dma_start(
                                    out=bcl[:, h2b:h2b + 1],
                                    in_=srd2[h2b:h2b + 1, oc, 0,
                                             t5 * 512 + tt * 128:t5 * 512 + (tt + 1) * 128
                                             ].rearrange("r (p t) -> (r p) t", t=1))))(),
                                    waits=([("vd", P.cnt.get("bpT_vd", 0))] if h2b == 0 else []),
                                    inc=("dq", 16))
                                P.bump("dq")
                            bcl_ld = P.val("dq") * 16
                        if normalize:
                            for h2 in range(2):
                                P.add("scalar", (lambda h2=h2: (lambda e: e.activation(
                                    out=sqj[0][:, 0:64], in_=pgr[0][:, 64 + h2 * 64:128 + h2 * 64],
                                    func=AF.Square, accum_out=nfns[0][:, h2:h2 + 1])))(),
                                    waits=[("pe", pT), ("vd", P.cnt.get("bpT_vd", 0))], inc=("sd", 1))
                                P.bump("sd")
                            P.add("scalar", lambda e: e.activation(
                                out=nfns[1][:, 0:2], in_=nfns[0][:, 0:2], func=AF.Sqrt,
                                bias=ep6_col[:]), inc=("sd", 1))
                            sqd = P.bump("sd")
                            P.add("vector", lambda e: e.reciprocal(
                                out=nrcol[:], in_=nfns[1][:, 0:2]),
                                waits=[("sd", sqd)], inc=("vd", 1))
                            P.bump("vd")
                            fcol = nrcol
                        else:
                            fcol = bcl if fold == "beta" else None
                        for h2 in range(2):
                            if fold == "rf1":
                                sc = rf1s[:, 0:1]
                            else:
                                sc = fcol[:, h2:h2 + 1]
                            P.add("vector", (lambda h2=h2, sc=sc: (lambda e: e.tensor_scalar(
                                out=stgb2[:, h2 * 64:(h2 + 1) * 64],
                                in0=pgr[0][:, 64 + h2 * 64:128 + h2 * 64],
                                scalar1=sc, scalar2=None, op0=ALU.mult)))(),
                                waits=[("pe", pT),
                                       ("dq", P.cnt.get("bpst_dq", 0))] +
                                      ([("sd", P.val("sd"))] if normalize else []),
                                inc=(("vd", 1) if h2 == 1 else None))
                        P.cnt["bpT_vd"] = P.bump("vd")
                        tvd = P.val("vd")
                        if store_T and fold == "beta":
                            for h2 in range(2):
                                P.add("vector", (lambda h2=h2: (lambda e: e.tensor_scalar(
                                    out=stgb3[:, h2 * 64:(h2 + 1) * 64],
                                    in0=stgb2[:, h2 * 64:(h2 + 1) * 64],
                                    scalar1=bcl[:, h2:h2 + 1], scalar2=None,
                                    op0=ALU.mult)))(),
                                    waits=([("dq", bcl_ld)] if h2 == 0 else []),
                                    inc=(("vd", 1) if h2 == 1 else None))
                            P.cnt["bpT_vd"] = P.bump("vd")
                            tvd = P.val("vd")
                        if store_T:
                            tsrc = stgb3 if fold == "beta" else stgb2
                            for h2 in range(2):
                                P.add("sync", (lambda oc=oc, h2=h2, t5=t5, tt=tt, tsrc=tsrc: (lambda e: e.dma_start(
                                    out=T_dst[oc * 2 + h2][t5 * 512 + tt * 128:t5 * 512 + (tt + 1) * 128, :],
                                    in_=tsrc[:, h2 * 64:(h2 + 1) * 64])))(),
                                    waits=([("vd", tvd)] if h2 == 0 else []), inc=("dq", 16))
                                P.cnt["bpst_dq"] = P.bump("dq") * 16
                        if fm_dst is not None:
                            P.add("tensor", (lambda: (lambda e: e.matmul(
                                out=pgr[1][:, 64:192], lhsT=stgb2[:],
                                rhs=idnb[:], start=True, stop=True)))(),
                                waits=[("vd", tvd)], inc=("pe", 1))
                            pT3 = P.bump("pe")
                            P.cnt["bp_pe"] = pT3
                            P.add("vector", lambda e: e.tensor_copy(
                                stgb2[:], pgr[1][:, 64:192]),
                                waits=[("pe", pT3)], inc=("vd", 1))
                            P.cnt["bpT_vd"] = P.bump("vd")
                            for h2 in range(2):
                                P.add("sync", (lambda oc=oc, h2=h2, t5=t5, tt=tt: (lambda e: e.dma_start(
                                    out=fm_dst[oc * 2 + h2][:, t5 * 512 + tt * 128:t5 * 512 + (tt + 1) * 128],
                                    in_=stgb2[h2 * 64:(h2 + 1) * 64, :])))(),
                                    waits=([("vd", P.val("vd"))] if h2 == 0 else []),
                                    inc=("dq", 16))
                                P.cnt["bpfm_dq"] = P.cnt["bpst_dq"] = P.bump("dq") * 16

        qdl = [qd[h // 2][(h % 2) * 64:(h % 2) * 64 + 64, :] for h in range(4)]
        kdl = [kd[h // 2][(h % 2) * 64:(h % 2) * 64 + 64, :] for h in range(4)]
        gdl = [gd[h] for h in range(4)]
        kdTl = [kdTb[h // 2][h % 2, :, (h % 2) * 64:(h % 2) * 64 + 64]
                for h in range(4)]
        vdTl = [vdT[h] for h in range(4)]
        emit_bigproj(wg_d, 3, AF.Identity, gdl, None, False, None, False)
        if phases == "p1":
            _finish(P, block, sems, xout, xres_d)
            return nc
        emit_bigproj(wq_d, 0, AF.Silu, qdl, vdTl, True, None, False)
        if phases == "p2":
            _finish(P, block, sems, xout, xres_d)
            return nc
        if phases == "p3n":
            emit_bigproj(wk_d, 1, AF.Silu, kdl, vdTl, True, None, True)
        elif phases in ("p3b", "p3b1", "p3b2"):
            emit_bigproj(wk_d, 1, AF.Silu, kdl, vdTl, True, "beta", False)
        else:
            emit_bigproj(wk_d, 1, AF.Silu, kdl,
                         vdTl if phases == "p3v" else kdTl, True, "beta", True)
        if phases in ("p3", "p3v", "p3n", "p3b", "p3b1", "p3b2"):
            _finish(P, block, sems, xout, xres_d)
            return nc
        emit_bigproj(wv_d, 2, AF.Silu, None, vdTl, False, "rf1", True)

        if phases == "ab":
            _finish(P, block, sems, xout, xres_d)
            return nc
        NI = CSC
        VB = 33 * 2
        PB = 7 * 2
        SBI = 3 * 2
        Vp = 20
        Pp = 1
        GQI = 14
        VC = Vp + NI * VB
        PC = Pp + NI * PB + 4
        SC_ = NI * SBI
        IC0, IC2, IC1, IC3 = 0, 1, 2, 3

        odall = nc.dram_tensor("odall", [2, NCH * 2 + 2, 64 * CSC], BF16,
                               kind="Internal").ap()
        sstA = [sb(f"sstA{g}", [2, 5 * CSC]) for g in range(2)]
        kst = [sb(f"kst{g}", [128, CSC], BF16) for g in range(2)]
        qst = [sb(f"qst{g}", [128, CSC], BF16) for g in range(2)]
        prod2 = sb("prod2", [128, 2 * CSC])
        gbof = [sb(f"gbof{g}", [2, 128]) for g in range(2)]
        scolf = [sb(f"scolf{g}", [2, 8]) for g in range(2)]
        tok8 = [sb(f"tok8{g}", [2, 8]) for g in range(2)]
        vstg = [sb(f"vstg{g}", [2, 64]) for g in range(2)]
        krow = [sb(f"krow{g}", [2, 128]) for g in range(2)]
        qrow = [sb(f"qrow{g}", [2, 128]) for g in range(2)]
        bones2f = sb("bones2f", [2, 128])
        P.add("sync", lambda e: e.dma_start(out=bones2f[:], in_=bones_d),
              inc=("dq", 16))
        P.bump("dq")
        dumm = sb("dumm", [1, 1])
        dumms = sb("dumms", [1, 1])
        dummd = sb("dummd", [1, 1])
        dummg = sb("dummg", [1, 1])
        idnf = sb("idnf_sb", [128, 128])
        krws2 = [bigA[0:2, 0:4096].bitcast(BF16), bigA[0:2, 4096:8192].bitcast(BF16)]
        vrws2 = [bigA[0:2, 8192:10240].bitcast(BF16), bigA[0:2, 10240:12288].bitcast(BF16)]
        orws2 = [bigA[0:2, 12288:14336].bitcast(BF16), bigA[0:2, 14336:16384].bitcast(BF16)]

        P.add("sync", lambda e: e.dma_start(out=idnf[:], in_=idnf_d), inc=("dq", 16))
        P.bump("dq")
        P.add("vector", lambda e: e.memset(kqDc[0][:], 0.0))
        P.add("vector", lambda e: e.memset(kqDc[1][:], 0.0))
        P.add("vector", lambda e: e.memset(krws2[0][:], 0.0),
              waits=[("pe", P.val("pe")), ("sd", P.val("sd"))])
        P.add("vector", lambda e: e.memset(krws2[1][:], 0.0))
        P.add("vector", lambda e: e.memset(orws2[0][:], 0.0))
        P.add("vector", lambda e: e.memset(orws2[1][:], 0.0))
        P.add("vector", lambda e: e.memset(fsq[0][:], 0.0))
        P.add("vector", lambda e: e.memset(fsq[1][:], 0.0))
        P.add("vector", lambda e: e.memset(nfns[0][:], 0.0))
        P.add("vector", lambda e: e.memset(nfns[1][:], 0.0))
        for g in range(2):
            P.add("vector", (lambda g=g: (lambda e: e.memset(UW[g][:], 0.0)))())
            P.add("vector", (lambda g=g: (lambda e: e.memset(inv2[g][:], 1.0)))())
        P.add("vector", lambda e: e.memset(dumm[:], 0.0), inc=("vd", 1))
        P.bump("vd")

        B0 = {"dq": P.val("dq") * 16, "pe": P.val("pe"), "sd": P.val("sd"),
              "vd": P.val("vd"), "gq": P.val("gq") * 16}
        import kernel as _KK; _KK._B0dump = dict(B0)

        def scan_gpsimd(e):
            e.wait_ge(sems["dq"], B0["dq"])
            with e.Fori(0, nch) as ch:
                gqb = B0["gq"] + ch * (GQI * 16)
                chVC = ch * VC
                vcb = B0["vd"] + chVC
                vprev = chVC
                pall = B0["pe"] + ch * PC
                csl = bass.ds(ch * CSC, CSC)
                tsch = bass.ts(ch, 1)
                ch2m = ch * 2 - 2
                e.wait_ge(sems["gq"], gqb)
                e.wait_ge(sems["vd"], vcb)
                e.dma_start(out=odall[0][bass.ds(ch * 2, 2), :], in_=orws2[0][:]
                            ).then_inc(sems["gq"], 16)
                e.dma_start(out=odall[1][bass.ds(ch * 2, 2), :], in_=orws2[1][:]
                            ).then_inc(sems["gq"], 16)
                e.wait_ge(sems["vd"], vprev)
                for g in range(2):
                    e.dma_start(out=kst[g][:], in_=kd[g][:, csl]
                                ).then_inc(sems["gq"], 16)
                    e.dma_start(out=qst[g][:], in_=qd[g][:, csl]
                                ).then_inc(sems["gq"], 16)
                    if g == 0:
                        e.wait_ge(sems["pe"], pall)
                    e.dma_start(
                        out=krws2[g][:].rearrange("r (t c) -> r t c", c=128),
                        in_=kdTb[g][:, csl, :]).then_inc(sems["gq"], 16)
                    e.dma_start(
                        out=vrws2[g][0:1, :].rearrange("r (t b) -> r t b", b=64),
                        in_=vdT[2 * g].rearrange("(c t) b -> c t b", t=CSC)[tsch, :, :]
                        ).then_inc(sems["gq"], 16)
                    e.dma_start(
                        out=vrws2[g][1:2, :].rearrange("r (t b) -> r t b", b=64),
                        in_=vdT[2 * g + 1].rearrange("(c t) b -> c t b", t=CSC)[tsch, :, :]
                        ).then_inc(sems["gq"], 16)
                    e.dma_start(
                        out=sstA[g][:].rearrange("p (q t) -> p q t", q=5),
                        in_=srd2[:, g, :, csl]).then_inc(sems["gq"], 16)
            e.wait_ge(sems["vd"], B0["vd"] + nch * VC)
            e.dma_start(out=odall[0][nch * 2:nch * 2 + 2, :], in_=orws2[0][:]
                        ).then_inc(sems["gq"], 16)
            e.dma_start(out=odall[1][nch * 2:nch * 2 + 2, :], in_=orws2[1][:]
                        ).then_inc(sems["gq"], 16)
            return e.dma_start(out=dummg[:], in_=rf_d)

        def scan_tensor(e):
            e.wait_ge(sems["dq"], B0["dq"])
            with e.Fori(0, nch) as ch:
                vcb = B0["vd"] + ch * VC
                scb = B0["sd"] + ch * SC_
                vpr = vcb + Vp
                e.wait_ge(sems["vd"], vcb + 18)
                e.matmul(out=pgr[0][0:2, 64:192], lhsT=onesd2[:],
                         rhs=prod2[:], start=True, stop=True
                         ).then_inc(sems["pe"], 1)
                with e.Fori(0, NI) as it:
                    vit = vpr + it * VB
                    sit = scb + it * SBI
                    it4 = bass.ts(it, 4)
                    it128 = bass.ts(it, 128)
                    streams = []
                    for g in range(2):
                        L = []

                        def W(sem, val):
                            L.append(("w", sem, val))

                        def O(fn):
                            L.append(("o", fn))

                        W("vd", vit + (g - 23))
                        W("vd", vpr)
                        O((lambda g=g: lambda: e.matmul(
                            out=pgr[g][:, 0:4], lhsT=UW[g][:],
                            rhs=kqDc[g][:, it4], start=True, stop=True))())
                        W("sd", sit + (3 * g - 4))
                        O((lambda g=g: lambda: e.matmul(
                            out=bkB[g][0:2, 4:6], lhsT=onesd2[:],
                            rhs=nfns[g][:], start=True, stop=True))())
                        w5 = vit + (13 + g)
                        W("vd", w5)
                        O((lambda g=g: lambda: e.matmul(
                            out=pgr[g][0:2, 64:192], lhsT=rvSB[g][:, 0:2],
                            rhs=idnf[:], start=True, stop=True))())
                        O((lambda g=g: lambda: e.matmul(
                            out=pgr[g][32:34, 64:192], lhsT=rvSB[g][:, 2:4],
                            rhs=idnf[:], start=True, stop=True))())
                        W("vd", w5)
                        O((lambda g=g: lambda: e.matmul(
                            out=bkB[g][:, 0:4], lhsT=bones2f[:],
                            rhs=scolf[g][:, 0:4], start=True, stop=True))())
                        W("vd", vit + (27 + g))
                        O((lambda g=g: lambda: e.matmul(
                            out=bkC[g][:, 0:128], lhsT=gbo[g][:],
                            rhs=krws2[g][:, it128], start=True, stop=True))())
                        W("vd", vit + (29 + g))
                        O((lambda g=g: lambda: e.matmul(
                            out=bkC[g][:, 128:256],
                            lhsT=kgSB[g][:], rhs=idnb[:], start=True, stop=True))())
                        streams.append(L)
                    idx = [0, 0]
                    while idx[0] < len(streams[0]) or idx[1] < len(streams[1]):
                        for g in range(2):
                            while idx[g] < len(streams[g]):
                                ent = streams[g][idx[g]]
                                idx[g] += 1
                                if ent[0] == "w":
                                    e.wait_ge(sems[ent[1]], ent[2])
                                else:
                                    ent[1]().then_inc(sems["pe"], 1)
                                    break
                for _pd in range(2):
                    e.matmul(out=pgr[0][0:1, 0:1], lhsT=rf1[:],
                             rhs=ones_row[0:1, 0:1], start=True, stop=True
                             ).then_inc(sems["pe"], 1)
                for g in range(2):
                    e.wait_ge(sems["sd"], scb + (SC_ + 3 * g - 4))
                    e.matmul(out=bkB[g][0:2, 4:6], lhsT=onesd2[:],
                             rhs=nfns[g][:], start=True, stop=True
                             ).then_inc(sems["pe"], 1)
            return e.matmul(out=pgr[0][0:1, 0:1], lhsT=rf1[:],
                            rhs=ones_row[0:1, 0:1], start=True, stop=True)

        def scan_scalar(e):
            with e.Fori(0, nch) as ch:
                peb = B0["pe"] + ch * PC + Pp
                vcb = B0["vd"] + ch * VC + Vp
                with e.Fori(0, NI) as it:
                    vit = vcb + it * VB
                    pit = peb + it * PB
                    for g in range(2):
                        pib = g * 7
                        if phases == "abl_s":
                            e.wait_ge(sems["vd"], vit + (41 + g))
                            e.copy(dumms[:], eps_col[0:1, 0:1]).then_inc(sems["sd"], 1)
                            e.wait_ge(sems["vd"], vit + (43 + g))
                            e.copy(dumms[:], eps_col[0:1, 0:1]).then_inc(sems["sd"], 1)
                            e.wait_ge(sems["pe"], pit + (13 + g))
                            e.copy(dumms[:], eps_col[0:1, 0:1]).then_inc(sems["sd"], 1)
                            continue
                        e.wait_ge(sems["vd"], vit + (41 + g))
                        e.wait_ge(sems["pe"], pit + (3 + g))
                        e.activation(out=sqj[g][:, 0:64],
                                     in_=UW[g][:, 0:64],
                                     func=AF.Square,
                                     accum_out=nfns[g][:, 0:1]
                                     ).then_inc(sems["sd"], 1)
                        e.wait_ge(sems["vd"], vit + (43 + g))
                        e.activation(out=sqj[g][:, 0:64],
                                     in_=UW[g][:, 64:128],
                                     func=AF.Square,
                                     accum_out=nfns[g][:, 1:2]
                                     ).then_inc(sems["sd"], 1)
                        e.wait_ge(sems["pe"], pit + (17 + g))
                        e.activation(out=nrm2[g][:], in_=bkB[g][0:2, 4:6],
                                     func=AF.Sqrt, scale=1.0 / 4096.0
                                     ).then_inc(sems["sd"], 1)
            return e.copy(dumms[:], eps_col[0:1, 0:1])

        def scan_vector(e):
            e.wait_ge(sems["dq"], B0["dq"])
            with e.Fori(0, nch) as ch:
                gqe = B0["gq"] + (ch * (GQI * 16) + GQI * 16)
                peb = B0["pe"] + ch * PC + Pp
                scb = B0["sd"] + ch * SC_

                def V(ins):
                    ins.then_inc(sems["vd"], 1)

                e.wait_ge(sems["gq"], gqe)
                for g in range(2):
                    V(e.tensor_copy(
                        kqDc[g][0:64, :].rearrange("p (t j) -> p t j", j=4)[:, :, 0:1],
                        kst[g][0:64, :].rearrange("p (t j) -> p t j", j=1)))
                    V(e.tensor_copy(
                        kqDc[g][64:128, :].rearrange("p (t j) -> p t j", j=4)[:, :, 1:2],
                        kst[g][64:128, :].rearrange("p (t j) -> p t j", j=1)))
                    V(e.tensor_copy(
                        kqDc[g][0:64, :].rearrange("p (t j) -> p t j", j=4)[:, :, 2:3],
                        qst[g][0:64, :].rearrange("p (t j) -> p t j", j=1)))
                    V(e.tensor_copy(
                        kqDc[g][64:128, :].rearrange("p (t j) -> p t j", j=4)[:, :, 3:4],
                        qst[g][64:128, :].rearrange("p (t j) -> p t j", j=1)))
                    for si, dj in [(0, 0), (1, 1), (2, 4), (3, 5)]:
                        V(e.tensor_copy(
                            fsq[g][:, :].rearrange("p (t j) -> p t j", j=8)[:, :, dj:dj + 1],
                            sstA[g][:, (1 + si) * CSC:(2 + si) * CSC].rearrange(
                                "p (t j) -> p t j", j=1)))
                    V(e.tensor_tensor(out=prod2[:, g * CSC:(g + 1) * CSC],
                                      in0=qst[g][:], in1=kst[g][:], op=ALU.mult))
                for g in range(2):
                    if g == 0:
                        e.wait_ge(sems["pe"], peb)
                    V(e.tensor_tensor(
                        out=fsq[g][:, :].rearrange("p (t j) -> p t j", j=8)[:, :, 2:3],
                        in0=pgr[0][0:2, 64 + g * CSC:64 + (g + 1) * CSC].rearrange(
                            "p (t j) -> p t j", j=1),
                        in1=sstA[g][:, 0:CSC].rearrange("p (t j) -> p t j", j=1),
                        op=ALU.mult))
                with e.Fori(0, NI) as it:
                    pit = peb + it * PB
                    sit = scb + it * SBI
                    it8 = bass.ts(it, 8)
                    it64 = bass.ts(it, 64)
                    streams = []
                    for g in range(2):
                        pib = g * 7
                        fsq_sl = fsq[g][:, it8]
                        vrws_sl = vrws2[g][:, it64]
                        orws_sl = orws2[g][:, it64]
                        L = []

                        def W(sem, val):
                            L.append(("w", sem, val))

                        if phases.startswith("abl_v") or phases in ("v6a", "v6b"):
                            kcut = (6 if phases in ("v6a", "v6b")
                                    else int(phases[5:]) if len(phases) > 5 else -1)
                            kctr = [0]

                            def O(fn, kcut=kcut, kctr=kctr):
                                j = kctr[0]
                                kctr[0] += 1
                                if j <= kcut:
                                    L.append(("o", fn))
                                else:
                                    L.append(("o", lambda: e.memset(dumm[:], 0.0)))
                        else:
                            def O(fn):
                                L.append(("o", fn))

                        W("pe", pit + (1 + g))
                        O((lambda g=g: lambda: e.tensor_copy(rvSB[g][:], pgr[g][:, 0:4]))())
                        O((lambda g=g, fsq_sl=fsq_sl: lambda: e.tensor_copy(tok8[g][:], fsq_sl))())
                        O((lambda g=g, vrws_sl=vrws_sl: lambda: e.tensor_copy(vstg[g][:], vrws_sl))())
                        O((lambda g=g: lambda: e.tensor_tensor(
                            out=scolf[g][:, 0:2], in0=tok8[g][:, 0:2],
                            in1=inv2[g][:], op=ALU.mult))())
                        O((lambda g=g: lambda: e.tensor_scalar(
                            out=scolf[g][:, 2:3], in0=scolf[g][:, 1:2],
                            scalar1=rfc2[:, 0:1], scalar2=None, op0=ALU.mult))())
                        O((lambda g=g: lambda: e.tensor_scalar(
                            out=scolf[g][:, 3:4], in0=scolf[g][:, 0:1],
                            scalar1=rfc2[:, 0:1], scalar2=None, op0=ALU.mult))())
                        O((lambda g=g: lambda: e.tensor_scalar(
                            out=scolf[g][:, 4:8], in0=scolf[g][:, 0:4],
                            scalar1=-1.0, scalar2=None, op0=ALU.mult))())
                        W("pe", pit + (7 + g))
                        O((lambda g=g: lambda: e.tensor_copy(
                            krow[g][:], pgr[g][0:2, 64:192]))())
                        O((lambda g=g: lambda: e.tensor_copy(
                            qrow[g][:], pgr[g][32:34, 64:192]))())
                        O((lambda g=g: lambda: e.scalar_tensor_tensor(
                            out=g1s[g][:], in0=krow[g][:, 0:64],
                            scalar=scolf[g][:, 4 + IC0:5 + IC0], in1=vstg[g][:],
                            op0=ALU.mult, op1=ALU.add))())
                        O((lambda g=g: lambda: e.scalar_tensor_tensor(
                            out=gbof[g][:, 0:64], in0=krow[g][:, 64:128],
                            scalar=scolf[g][:, 4 + IC1:5 + IC1], in1=g1s[g][:],
                            op0=ALU.mult, op1=ALU.add))())
                        O((lambda g=g: lambda: e.scalar_tensor_tensor(
                            out=g1s[g][:], in0=krow[g][:, 0:64],
                            scalar=scolf[g][:, 4 + IC3:5 + IC3], in1=vstg[g][:],
                            op0=ALU.mult, op1=ALU.add))())
                        O((lambda g=g: lambda: e.scalar_tensor_tensor(
                            out=gbof[g][:, 64:128], in0=krow[g][:, 64:128],
                            scalar=scolf[g][:, 4 + IC2:5 + IC2], in1=g1s[g][:],
                            op0=ALU.mult, op1=ALU.add))())
                        O((lambda g=g: lambda: e.tensor_copy(gbo[g][:], gbof[g][:]))())
                        W("pe", pit + (11 + g))
                        O((lambda g=g: lambda: e.tensor_copy(
                            kgSB[g][:], bkC[g][:, 0:128]))())
                        O((lambda g=g: lambda: e.tensor_copy(
                            ccols[g][:], bkB[g][:, 0:4]))())
                        O((lambda g=g: lambda: e.tensor_scalar(
                            out=upds[g][:, 0:64], in0=UW[g][:, 64:128],
                            scalar1=ccols[g][:, IC1:IC1 + 1], scalar2=None,
                            op0=ALU.mult))())
                        O((lambda g=g: lambda: e.scalar_tensor_tensor(
                            out=upds[g][:, 0:64], in0=UW[g][:, 0:64],
                            scalar=ccols[g][:, IC0:IC0 + 1], in1=upds[g][:, 0:64],
                            op0=ALU.mult, op1=ALU.add))())
                        O((lambda g=g: lambda: e.tensor_scalar(
                            out=upds[g][:, 64:128], in0=UW[g][:, 0:64],
                            scalar1=ccols[g][:, IC3:IC3 + 1], scalar2=None,
                            op0=ALU.mult))())
                        O((lambda g=g: lambda: e.scalar_tensor_tensor(
                            out=upds[g][:, 64:128], in0=UW[g][:, 64:128],
                            scalar=ccols[g][:, IC2:IC2 + 1], in1=upds[g][:, 64:128],
                            op0=ALU.mult, op1=ALU.add))())
                        W("pe", pit + (13 + g))
                        O((lambda g=g: lambda: e.tensor_tensor(
                            out=UW[g][:, 0:64], in0=upds[g][:, 0:64],
                            in1=bkC[g][:, 128:192],
                            op=ALU.add))())
                        O((lambda g=g: lambda: e.tensor_tensor(
                            out=UW[g][:, 64:128], in0=upds[g][:, 64:128],
                            in1=bkC[g][:, 192:256],
                            op=ALU.add))())
                        W("sd", sit + (g * 3 + 3))
                        O((lambda g=g: lambda: e.tensor_scalar(
                            out=m1s[g][:], in0=nrm2[g][:], scalar1=1.0,
                            scalar2=None, op0=ALU.max))())
                        O((lambda g=g: lambda: e.reciprocal(
                            out=inv2[g][:], in_=m1s[g][:]))())
                        O((lambda g=g: lambda: e.tensor_tensor(
                            out=tfs[g][:, 0:2], in0=tok8[g][:, 4:6],
                            in1=inv2[g][:], op=ALU.mult))())
                        O((lambda g=g: lambda: e.tensor_scalar(
                            out=g1s[g][:], in0=qrow[g][:, 0:64],
                            scalar1=scolf[g][:, IC0:IC0 + 1], scalar2=None,
                            op0=ALU.mult))())
                        O((lambda g=g: lambda: e.scalar_tensor_tensor(
                            out=g1s[g][:], in0=qrow[g][:, 64:128],
                            scalar=scolf[g][:, IC1:IC1 + 1], in1=g1s[g][:],
                            op0=ALU.mult, op1=ALU.add))())
                        O((lambda g=g: lambda: e.scalar_tensor_tensor(
                            out=g1s[g][:], in0=gbof[g][:, 0:64],
                            scalar=tok8[g][:, 2:3], in1=g1s[g][:],
                            op0=ALU.mult, op1=ALU.add))())
                        O((lambda g=g: lambda: e.tensor_scalar(
                            out=gbof[g][:, 0:64], in0=g1s[g][:],
                            scalar1=tfs[g][:, 0:1], scalar2=None, op0=ALU.mult))())
                        O((lambda g=g: lambda: e.tensor_scalar(
                            out=g1s[g][:], in0=qrow[g][:, 0:64],
                            scalar1=scolf[g][:, IC3:IC3 + 1], scalar2=None,
                            op0=ALU.mult))())
                        O((lambda g=g: lambda: e.scalar_tensor_tensor(
                            out=g1s[g][:], in0=qrow[g][:, 64:128],
                            scalar=scolf[g][:, IC2:IC2 + 1], in1=g1s[g][:],
                            op0=ALU.mult, op1=ALU.add))())
                        O((lambda g=g: lambda: e.scalar_tensor_tensor(
                            out=g1s[g][:], in0=gbof[g][:, 64:128],
                            scalar=tok8[g][:, 2:3], in1=g1s[g][:],
                            op0=ALU.mult, op1=ALU.add))())
                        O((lambda g=g, orws_sl=orws_sl: lambda: e.scalar_tensor_tensor(
                            out=orws_sl, in0=g1s[g][:],
                            scalar=tfs[g][:, 1:2], in1=gbof[g][:, 0:64],
                            op0=ALU.mult, op1=ALU.add))())
                        streams.append(L)
                    idx = [0, 0]
                    opn = [0, 0]
                    while idx[0] < len(streams[0]) or idx[1] < len(streams[1]):
                        for g in range(2):
                            while idx[g] < len(streams[g]):
                                ent = streams[g][idx[g]]
                                idx[g] += 1
                                if ent[0] == "w":
                                    e.wait_ge(sems[ent[1]], ent[2])
                                else:
                                    ent[1]().then_inc(sems["vd"], 1)
                                    opn[g] += 1
                                    break
            return e.memset(dumm[:], 0.0)

        P.add("gpsimd", scan_gpsimd, inc=("gq", 16))
        P.add("gpsimd", None, inc=("gq", nch * GQI * 16 + 32))
        P.cnt["gq"] = P.val("gq") + nch * GQI + 3
        P.add("tensor", scan_tensor, inc=("pe", 1))
        P.add("tensor", None, inc=("pe", nch * PC))
        P.cnt["pe"] = P.val("pe") + nch * PC + 1
        P.add("scalar", scan_scalar, inc=("sd", 1))
        P.add("scalar", None, inc=("sd", nch * SC_))
        P.cnt["sd"] = P.val("sd") + nch * SC_ + 1
        P.add("vector", scan_vector, inc=("vd", 1))
        P.add("vector", None, inc=("vd", nch * VC))
        P.cnt["vd"] = P.val("vd") + nch * VC + 1
        if tail and nch < NCH:
            P.add("vector", lambda e: e.memset(oTf[0][:], 0.0))
            P.add("vector", lambda e: e.memset(oTf[1][:], 0.0), inc=("vd", 1))
            P.bump("vd")
        if not tail:
            P.add("sync", lambda e: e.dma_start(out=xout, in_=xres_d),
                  waits=[("gq", P.val("gq") * 16), ("vd", P.val("vd")),
                         ("pe", P.val("pe")), ("sd", P.val("sd"))],
                  inc=("dq", 16))
            P.bump("dq")
            P.add("sync", None, waits=[("dq", P.val("dq") * 16)])
            P.run_all(block, sems)
            return nc
        otkB = [sb(f"otkB{i}", [64, 512], BF16) for i in range(2)]
        gstgB = [sb(f"gstgB{i}", [128, 256], BF16) for i in range(2)]
        ystgf = [sb(f"ystgf{i}", [128, 512]) for i in range(2)]
        for b4 in range(nch // 4):
            for g in range(2):
                e2 = b4 * 2 + g
                par = e2 % 2
                key = f"c2pe_{e2 - 2}"
                ei0 = b4 * 8 + g * 4
                P.add("sync", (lambda b4=b4, g=g, par=par: (lambda e: e.dma_start(
                    out=otkB[par][:, :].rearrange("t (c h b) -> t c h b", c=4, h=2),
                    in_=odall[g][8 * b4 + 2:8 * b4 + 10, :].rearrange(
                        "(c h) (t b) -> t c h b", h=2, b=64))))(),
                    waits=[("pe", P.cnt.get(key, 0))] +
                          ([("gq", P.val("gq") * 16)] if b4 == 0 and g == 0 else []),
                    inc=("dq", 16))
                P.bump("dq")
                P.add("sync", (lambda b4=b4, g=g, par=par: (lambda e: e.dma_start(
                    out=gstgB[par][0:64, :],
                    in_=gd[2 * g][:, b4 * 256:(b4 + 1) * 256])))(),
                    waits=[("sd", P.cnt.get(f"c2sl_{e2 - 2}", 0))], inc=("dq", 16))
                P.bump("dq")
                P.add("sync", (lambda b4=b4, g=g, par=par: (lambda e: e.dma_start(
                    out=gstgB[par][64:128, :],
                    in_=gd[2 * g + 1][:, b4 * 256:(b4 + 1) * 256])))(),
                    inc=("dq", 16))
                gld = P.bump("dq") * 16
                for c4 in range(4):
                    ch = 4 * b4 + c4
                    ei = ei0 + c4
                    pp = ch % 2
                    P.add("tensor", (lambda par=par, c4=c4, pp=pp: (lambda e: e.matmul(
                        out=pps[pp][:, 0:64], lhsT=otkB[par][:, c4 * 128:(c4 + 1) * 128],
                        rhs=idnb[0:64, 0:64], start=True, stop=True)))(),
                        waits=[("dq", gld), ("vd", P.cnt.get(f"c2m1_{ei - 2}", 0))],
                        inc=("pe", 1))
                    tpe = P.bump("pe")
                    P.add("scalar", (lambda pp=pp: (lambda e: e.activation(
                        out=stg[:, 0:64], in_=pps[pp][:, 0:64], func=AF.Square)))(),
                        waits=[("pe", tpe)], inc=("sd", 1))
                    P.bump("sd")
                    P.add("tensor", (lambda pp=pp: (lambda e: e.matmul(
                        out=pps[pp][0:2, 128:192], lhsT=onesd2[:], rhs=stg[:, 0:64],
                        start=True, stop=True)))(),
                        waits=[("sd", P.val("sd"))], inc=("pe", 1))
                    nrpe = P.bump("pe")
                    P.add("scalar", (lambda pp=pp: (lambda e: e.activation(
                        out=tfs[0][:, 0:64], in_=pps[pp][0:2, 128:192], func=AF.Sqrt,
                        scale=1.0 / 64.0, bias=eps_col[0:2, 0:1])))(),
                        waits=[("pe", nrpe),
                               ("vd", P.cnt.get(f"c2rc_{ei - 1}", 0))], inc=("sd", 1))
                    P.bump("sd")
                    P.add("vector", lambda e: e.reciprocal(
                        out=tfs[1][:, 0:64], in_=tfs[0][:, 0:64]),
                        waits=[("sd", P.val("sd")),
                               ("pe", P.cnt.get(f"c2bc_{ei - 1}", 0))], inc=("vd", 1))
                    P.cnt[f"c2rc_{ei}"] = P.bump("vd")
                    P.add("tensor", (lambda pp=pp: (lambda e: e.matmul(
                        out=pps[pp][:, 64:128], lhsT=bones2f[:], rhs=tfs[1][:, 0:64],
                        start=True, stop=True)))(),
                        waits=[("vd", P.val("vd"))], inc=("pe", 1))
                    P.cnt[f"c2bc_{ei}"] = P.cnt[f"c2pe_{e2}"] = P.bump("pe")
                    P.add("scalar", (lambda par=par, c4=c4: (lambda e: e.activation(
                        out=stg2[:, 0:64], in_=gstgB[par][:, c4 * 64:(c4 + 1) * 64],
                        func=AF.Silu)))(),
                        waits=[("vd", P.cnt.get(f"c2m3_{ei - 1}", 0))], inc=("sd", 1))
                    P.cnt[f"c2sl_{e2}"] = P.bump("sd")
                    sil_sd = P.val("sd")
                    P.add("vector", (lambda pp=pp: (lambda e: e.tensor_scalar(
                        out=stg[:, 64:128], in0=pps[pp][:, 0:64],
                        scalar1=onrm2[:, 0:1], scalar2=None, op0=ALU.mult)))(),
                        waits=[("pe", P.val("pe"))], inc=("vd", 1))
                    P.bump("vd")
                    P.add("vector", (lambda pp=pp: (lambda e: e.tensor_tensor(
                        out=stg[:, 64:128], in0=stg[:, 64:128],
                        in1=pps[pp][:, 64:128], op=ALU.mult)))(),
                        inc=("vd", 1))
                    P.cnt[f"c2m1_{ei}"] = P.bump("vd")
                    P.add("vector", (lambda g=g, ch=ch: (lambda e: e.tensor_tensor(
                        out=oTf[g][:, ch * 64:(ch + 1) * 64], in0=stg[:, 64:128],
                        in1=stg2[:, 0:64], op=ALU.mult)))(),
                        waits=[("sd", sil_sd)], inc=("vd", 1))
                    P.cnt[f"c2m3_{ei}"] = P.bump("vd")

        wof = bigA[:, 0:2048]
        wob = bigA[:, 8192:9216].bitcast(BF16)
        ybufv = bigA[:, 4096:8192]
        xrbuf = bigA[:, 0:4096]
        P.add("sync", lambda e: e.dma_start(
            out=wof, in_=wo_d.rearrange("p a b c -> p (a b c)")), inc=("dq", 16))
        wold = P.bump("dq") * 16
        P.add("vector", lambda e: e.tensor_copy(wob[:], wof),
              waits=[("dq", wold)], inc=("vd", 1))
        wcst = P.bump("vd")
        for q in range(4):
            for t5 in range(4):
                for oc in range(8):
                    po = oc % 2
                    j_ = (q * 4 + t5) * 8 + oc
                    for ic in range(2):
                        P.add("tensor", (lambda q=q, t5=t5, oc=oc, ic=ic, po=po: (lambda e: e.matmul(
                            out=pps[po][:, 0:512],
                            lhsT=wob[:, (ic * 8 + oc) * 128:(ic * 8 + oc + 1) * 128],
                            rhs=oTf[ic][:, q * 2048 + t5 * 512:q * 2048 + (t5 + 1) * 512],
                            start=(ic == 0), stop=(ic == 1))))(),
                            waits=([("vd", max(wcst, P.cnt.get(f"dev_{j_ - 2}", 0))),
                                    ("sd", P.val("sd"))] if ic == 0 else []),
                            inc=(("pe", 1) if ic == 1 else None))
                    mpe = P.bump("pe")
                    P.add("vector", (lambda po=po: (lambda e: e.tensor_copy(
                        ystgf[po][:], pps[po][:, 0:512])))(),
                        waits=[("pe", mpe),
                               ("dq", P.cnt.get(f"dyp_{j_ - 2}", 0))],
                        inc=("vd", 1))
                    P.cnt[f"dev_{j_}"] = P.bump("vd")
                    P.add("sync", (lambda q=q, t5=t5, oc=oc, po=po: (lambda e: e.dma_start(
                        out=ypd[q][:, oc, t5 * 512:(t5 + 1) * 512],
                        in_=ystgf[po][:])))(),
                        waits=[("vd", P.val("vd"))], inc=("dq", 16))
                    P.cnt[f"dyp_{j_}"] = P.bump("dq") * 16
        P.add("gpsimd", lambda e: e.collective_compute(
            "ReduceScatter", ALU.add,
            replica_groups=[[0, 1, 2, 3], [4, 5, 6, 7]],
            ins=[ypd.opt()], outs=[yprd.opt()]),
            waits=[("dq", P.val("dq") * 16)], inc=("cc", 1))
        for t5 in range(4):
            P.add("sync", (lambda t5=t5: (lambda e: e.dma_start(
                out=ybufv.rearrange("p (c t) -> p c t", c=8),
                in_=yprd[:, :, t5 * 512:(t5 + 1) * 512])))(),
                waits=[("cc", 1), ("vd", P.val("vd")), ("pe", P.val("pe"))],
                inc=("dq", 16))
            P.bump("dq")
            P.add("sync", (lambda t5=t5: (lambda e: e.dma_start(
                out=xrbuf.rearrange("p (c t) -> p c t", c=8),
                in_=xres_d[:, :, t5 * 512:(t5 + 1) * 512])))(),
                inc=("dq", 16))
            yld = P.bump("dq") * 16
            for oc in range(8):
                P.add("vector", (lambda oc=oc: (lambda e: e.tensor_scalar(
                    out=ybufv[:, oc * 512:(oc + 1) * 512],
                    in0=ybufv[:, oc * 512:(oc + 1) * 512],
                    scalar1=bo_sb[:, oc:oc + 1], scalar2=None, op0=ALU.add)))(),
                    waits=([("dq", yld)] if oc == 0 else []),
                    inc=(("vd", 1) if oc == 7 else None))
            P.bump("vd")
            emit_ln_stats(P, ybufv, 8, 512, ones_col, xsq, mrow, rrow, srow,
                          pps[0][0:1, 0:512], pps[0][32:33, 0:512], eps_col,
                          [("vd", P.val("vd")), ("sd", P.val("sd"))], CS=512)
            emit_bcast_row(P, mrow[:, 0:512], pps[1][:, 0:512], 512, ones_row,
                           [("vd", P.val("vd"))])
            emit_bcast_row(P, rrow[:, 0:512], psc[:, 0:512], 512, ones_row, [])
            bpe = P.val("pe")
            for oc in range(8):
                P.add("vector", (lambda oc=oc: (lambda e: e.tensor_tensor(
                    out=xsq[0][:, 0:512], in0=ybufv[:, oc * 512:(oc + 1) * 512],
                    in1=pps[1][:, 0:512], op=ALU.subtract)))(),
                    waits=([("pe", bpe)] if oc == 0 else []))
                P.add("vector", lambda e: e.tensor_tensor(
                    out=xsq[0][:, 0:512], in0=xsq[0][:, 0:512],
                    in1=psc[:, 0:512], op=ALU.mult))
                P.add("vector", (lambda oc=oc: (lambda e: e.tensor_tensor(
                    out=ybufv[:, oc * 512:(oc + 1) * 512], in0=xsq[0][:, 0:512],
                    in1=xrbuf[:, oc * 512:(oc + 1) * 512], op=ALU.add)))(),
                    inc=(("vd", 1) if oc == 7 else None))
            P.bump("vd")
            P.add("sync", (lambda t5=t5: (lambda e: e.dma_start(
                out=xout[:, :, t5 * 512:(t5 + 1) * 512],
                in_=ybufv.rearrange("p (c t) -> p c t", c=8))))(),
                waits=[("vd", P.val("vd"))], inc=("dq", 16))
            P.bump("dq")
        P.add("sync", None, waits=[("dq", P.val("dq") * 16)])
        P.run_all(block, sems)
    return nc


def _prep_l1attn_inputs(inputs, c):
    b = c // 4
    hg = c % 4
    hs = [4 * hg + i for i in range(4)]
    m = {}

    def headcols(W):
        return np.concatenate([W[:, h * 64:(h + 1) * 64] for h in hs], axis=1)

    for nm, key in [("wq", "L1_attn_Wq"), ("wk", "L1_attn_Wk"),
                    ("wv", "L1_attn_Wv"), ("wg", "L1_attn_Wg")]:
        m[nm] = _chunk_w(headcols(inputs[key]))
    for nm, key in [("bq", "L1_attn_bq"), ("bk", "L1_attn_bk"),
                    ("bv", "L1_attn_bv"), ("bg", "L1_attn_bg")]:
        bb_ = np.concatenate([inputs[key][h * 64:(h + 1) * 64] for h in hs])
        m[nm] = np.ascontiguousarray(bb_.reshape(2, 128).T)
    wo_rows = np.concatenate([inputs["L1_attn_Wo"][h * 64:(h + 1) * 64, :]
                              for h in hs], axis=0)
    m["wo"] = np.ascontiguousarray(
        wo_rows.reshape(2, 128, 8, 128).transpose(1, 0, 2, 3))
    m["bo"] = np.ascontiguousarray(inputs["L1_attn_bo"].reshape(8, 128).T)
    cols = []
    b1 = []
    b2 = []
    sml = [("L1_attn_Wb", "L1_attn_bb", None),
           ("L1_attn_Wfd", "L1_attn_bfd", "L1_attn_fast_decay_bias"),
           ("L1_attn_Wsd", "L1_attn_bsd", "L1_attn_slow_decay_bias"),
           ("L1_attn_Wfg", "L1_attn_bfg", None),
           ("L1_attn_Wsg", "L1_attn_bsg", None)]
    for j in range(2):
        for g in range(2):
            for wk_, bk_, dk_ in sml:
                h = hs[2 * g + j]
                cols.append(inputs[wk_][:, h])
                b1.append(inputs[bk_][h])
                b2.append(inputs[dk_][h] if dk_ is not None else np.float32(0.0))
    wsm = np.stack(cols, axis=1)
    m["wsm"] = np.ascontiguousarray(wsm.reshape(8, 128, 20).transpose(1, 0, 2))
    m["sb1"] = np.array(b1, np.float32).reshape(20, 1)
    m["sb2"] = np.array(b2, np.float32).reshape(20, 1)
    m["onrm"] = np.tile(np.asarray(inputs["L1_attn_onorm_w"], np.float32), 2
                        ).reshape(128, 1)
    m["rf"] = np.asarray(inputs["L1_attn_resonance_flux"],
                         np.float32).reshape(1, 1)
    m["idnb"] = np.eye(128, dtype=np.float32).astype(BF)
    m["idnf"] = np.eye(128, dtype=np.float32)
    bn = np.zeros((2, 128), np.float32)
    bn[0, 0:64] = 1.0
    bn[1, 64:128] = 1.0
    m["bones"] = bn
    return m


def launch_l1attn(x2, inputs):
    nc = build_l1attn()
    x2 = np.asarray(x2, np.float32).reshape(2, S, D)
    percore = []
    for c in range(8):
        m = _prep_l1attn_inputs(inputs, c)
        xfm = _feat_major(x2[c // 4])
        m["xb"] = xfm
        m["xres"] = np.ascontiguousarray(
            xfm[:, :, (c % 4) * 2048:(c % 4 + 1) * 2048])
        percore.append(m)
    res = run_bass_kernel_spmd(nc, percore, core_ids=list(range(8)))
    x3 = np.zeros((2, S, D), np.float32)
    for c in range(8):
        x3[c // 4, (c % 4) * 2048:(c % 4 + 1) * 2048] = _unfeat_major(
            res.results[c]["xout"])
    return x3
